# version 20
# baseline (speedup 1.0000x reference)
import numpy as np
import ml_dtypes
import concourse.bass as bass
import concourse.mybir as mybir
from concourse.bass_utils import run_bass_kernel_spmd
from contextlib import ExitStack

F32 = mybir.dt.float32
BF16 = mybir.dt.bfloat16
I32 = mybir.dt.int32
AF = mybir.ActivationFunctionType
ALU = mybir.AluOpType
AX = mybir.AxisListType
bfdt = ml_dtypes.bfloat16

B, S, D = 2, 4096, 2048
NCORES = 8
TOK = 1024
HALO = 512
TT = 8
TH = 4
TA = TT + TH
DC = 16
HQ, HKV, HD = 16, 4, 128
WIN = 640
H = 8192
HHALF = 4096
EPS_LN = 1e-5
EPS_RMS = 1.1920929e-07
THR = 0.5 + 2.0 ** -9
NEG = -1e30

ENGINES = ("sync", "vector", "scalar", "tensor", "gpsimd")


class P:

    NDMA = 4

    def __init__(self):
        self.ops = []
        self.epos = {e: [] for e in ENGINES}
        self.lastw = {}
        self.lastr = {}
        self.dmai = {e: 0 for e in ("sync", "scalar", "gpsimd")}
        self.dma_last = {}

    def op(self, eng, fn, reads=(), writes=(), dma=False):
        deps = {}
        myid = len(self.ops)

        def add(j):
            if j is not None and not (self.ops[j][0] == "tensor" and eng == "tensor"
                                      and not self.ops[j][3]):
                deps[j] = True

        for k in reads:
            add(self.lastw.get(k))
        for k in writes:
            add(self.lastw.get(k))
            for j in self.lastr.get(k, ()):
                add(j)
        if dma:
            sem = f"dq_{eng}_{self.dmai[eng] % self.NDMA}"
            self.dmai[eng] += 1
            add(self.dma_last.get(sem))
            self.dma_last[sem] = myid
        else:
            sem = None
        self.ops.append([eng, fn, deps, dma, sem])
        self.epos[eng].append(myid)
        for k in writes:
            self.lastw[k] = myid
            self.lastr.pop(k, None)
        for k in reads:
            self.lastr.setdefault(k, []).append(myid)
        return myid

    def barrier(self):
        tails = [self.epos[e][-1] for e in ENGINES if self.epos[e]]
        tails += [j for j in self.dma_last.values() if j is not None]
        for e in ENGINES:
            myid = len(self.ops)
            deps = {j: True for j in tails}
            self.ops.append([e, lambda eng: eng.nop(), deps, False, None])
            self.epos[e].append(myid)
        self.lastw.clear()
        self.lastr.clear()
        for k in list(self.dma_last):
            self.dma_last[k] = None

    def emit(self, block, sems_factory):
        n = len(self.ops)
        dom = [op[4] if op[4] else op[0] for op in self.ops]
        dpos = {}
        dcount = {}
        for j in range(n):
            k = dom[j]
            dpos[j] = dcount.get(k, 0)
            dcount[k] = dpos[j] + 1
        dops = {}
        for j in range(n):
            dops.setdefault(dom[j], []).append(j)
        clocks = [None] * n
        needs_inc = [op[3] for op in self.ops]
        waits_out = [None] * n
        implied = {e: {} for e in ENGINES}
        for j, (eng, fn, deps, dma, dsem) in enumerate(self.ops):
            imp = implied[eng]
            tgt = {}
            for d in deps:
                dk = dom[d]
                if imp.get(dk, -1) >= dpos[d]:
                    continue
                if dpos[d] > tgt.get(dk, -1):
                    tgt[dk] = dpos[d]
            w = []
            for dk, dp in sorted(tgt.items(), key=lambda x: str(x[0])):
                d = dops[dk][dp]
                needs_inc[d] = True
                w.append(d)
                dc = clocks[d]
                for k2, v2 in dc.items():
                    if v2 > imp.get(k2, -1):
                        imp[k2] = v2
                if dp > imp.get(dk, -1):
                    imp[dk] = dp
            waits_out[j] = w
            c = dict(imp)
            k = dom[j]
            if dpos[j] > c.get(k, -1):
                c[k] = dpos[j]
            clocks[j] = c
        semval = [0] * n
        cnt = {}
        for j in range(n):
            if needs_inc[j]:
                k = dom[j]
                cnt[k] = cnt.get(k, 0) + (16 if self.ops[j][3] else 1)
                semval[j] = cnt[k]
        sems = sems_factory(sorted(cnt.keys()))
        for name in ENGINES:
            eops = self.epos[name]

            def body(eng, eops=eops, name=name):
                cur = {}
                for j in eops:
                    _e, fn, deps, dma, dsem = self.ops[j]
                    for d in waits_out[j]:
                        key = dom[d]
                        v = semval[d]
                        if cur.get(key, 0) < v:
                            eng.wait_ge(sems[key], v)
                            cur[key] = v
                    instr = fn(eng)
                    if instr is not None and needs_inc[j]:
                        key = dom[j]
                        instr.then_inc(sems[key], 16 if dma else 1)
            getattr(block, name)(body)


def build(taps=()):
    nc = bass.Bass("TRN2", target_bir_lowering=False, debug=False)

    x_d = nc.dram_tensor("x", [TOK, D], F32, kind="ExternalInput").ap()
    xh_d = nc.dram_tensor("xh", [HALO, D], F32, kind="ExternalInput").ap()
    wq_d = nc.dram_tensor("wq", [D, D], F32, kind="ExternalInput").ap()
    wk_d = nc.dram_tensor("wk", [D, HKV * HD], F32, kind="ExternalInput").ap()
    wv_d = nc.dram_tensor("wv", [D, HKV * HD], F32, kind="ExternalInput").ap()
    wo_d = nc.dram_tensor("wo", [D, D], F32, kind="ExternalInput").ap()
    wgu_d = nc.dram_tensor("wgu", [2 * H, D], F32, kind="ExternalInput").ap()
    wp_d = nc.dram_tensor("wp", [D, H], F32, kind="ExternalInput").ap()
    msk_d = nc.dram_tensor("msk", [5 * 128, WIN], BF16, kind="ExternalInput").ap()
    idn_d = nc.dram_tensor("idn", [128, 128], BF16, kind="ExternalInput").ap()
    out_d = nc.dram_tensor("out", [TOK, D], F32, kind="ExternalOutput").ap()
    x2_d = nc.dram_tensor("x2buf", [TOK, D], F32)
    pp_d = nc.dram_tensor("ppart", [D, TOK], F32)

    tap_d = {}
    for tname, tshape, tdt in taps:
        tap_d[tname] = nc.dram_tensor("tap_" + tname, list(tshape), tdt, kind="ExternalOutput").ap()

    p = P()
    globals()["_P_DEBUG"] = p

    with ExitStack() as ctx:
        e = ctx.enter_context

        def sems_factory(names):
            return {sn: ctx.enter_context(nc.semaphore("sem_" + sn)) for sn in names}

        idn = e(nc.sbuf_tensor("idn_sb", [128, 128], BF16))
        msk = e(nc.sbuf_tensor("msk_sb", [128, 5 * WIN], BF16))
        st = e(nc.sbuf_tensor("stats", [128, 48], F32))
        st2 = e(nc.sbuf_tensor("stats2", [128, 48], F32))
        sxp = e(nc.sbuf_tensor("sxp", [128, 12], F32))
        disc = e(nc.sbuf_tensor("disc", [128, 2048], F32))
        ones = e(nc.sbuf_tensor("ones_bf", [128, 2], BF16))
        onesr = e(nc.sbuf_tensor("onesr", [1, 128], F32))
        idnf = e(nc.sbuf_tensor("idnf_sb", [128, 128], F32))
        negm3 = e(nc.sbuf_tensor("negm3", [1, TOK], BF16))
        rmsr = e(nc.sbuf_tensor("rmsr", [128, 8], F32))
        ssq_h = e(nc.sbuf_tensor("ssq_h", [128, 16], F32))
        qr = e(nc.sbuf_tensor("qrow", [128, 64], F32))
        qs = e(nc.sbuf_tensor("qsc", [128, 64], BF16))
        qt = e(nc.sbuf_tensor("qthr", [128, 64], F32))

        RA = e(nc.sbuf_tensor("RA", [128, 16384], F32))
        RB = e(nc.sbuf_tensor("RB", [128, 24576], BF16))
        RC = e(nc.sbuf_tensor("RC", [128, 8192], F32))
        RD = e(nc.sbuf_tensor("RD", [128, 4096], F32))
        RE = e(nc.sbuf_tensor("RE", [128, 4096], F32))
        RF = e(nc.sbuf_tensor("RF", [128, 4096], BF16))
        RG = e(nc.sbuf_tensor("RG", [128, 512], F32))
        RI = e(nc.sbuf_tensor("RI", [128, 1536], BF16))

        RAb = RA[:, :].bitcast(BF16)
        RCb = RC[:, :].bitcast(BF16)
        RDb = RD[:, :].bitcast(BF16)
        REb = RE[:, :].bitcast(BF16)
        RBf = RB[:, :].bitcast(F32)
        RFf = RF[:, :].bitcast(F32)

        p.op("sync", lambda eng: eng.dma_start(out=idn[:, :], in_=idn_d), writes=["idn"], dma=True)
        p.op("sync", lambda eng: eng.dma_start(
            out=msk[:, :], in_=msk_d.rearrange("(t p) w -> p t w", p=128)), writes=["msk"], dma=True)
        p.op("vector", lambda eng: eng.memset(ones[:, :], 1.0), writes=["ones"])
        p.op("vector", lambda eng: eng.memset(onesr[0:1, :], 1.0), writes=["onesr"])
        p.op("vector", lambda eng: eng.tensor_copy(out=idnf[:, :], in_=idn[:, :]),
             reads=["idn"], writes=["idnf"])

        def ln_stats(n, src, eps):
            for i in range(n):
                ap, rkeys = src(i)
                p.op("vector", lambda eng, ap=ap, i=i: eng.reduce_sum(
                    out=st[:, i:i + 1], in_=ap, axis=AX.X), reads=rkeys, writes=["stA"])
                p.op("scalar", lambda eng, ap=ap, i=i: eng.activation(
                    out=disc[:, 0:ap.shape[-1]] if False else disc[:, :], in_=ap, func=AF.Square,
                    accum_out=st[:, 16 + i:17 + i]),
                    reads=rkeys, writes=["disc", "stB"])
            p.op("vector", lambda eng, n=n: eng.tensor_scalar_mul(
                out=st[:, 32:32 + n], in0=st[:, 0:n], scalar1=1.0 / D),
                reads=["stA"], writes=["stC"])
            p.op("vector", lambda eng, n=n: eng.tensor_scalar_mul(
                out=st2[:, 0:n], in0=st[:, 16:16 + n], scalar1=1.0 / D),
                reads=["stB"], writes=["stD"])
            p.op("vector", lambda eng, n=n: eng.tensor_tensor(
                out=st2[:, 16:16 + n], in0=st[:, 32:32 + n], in1=st[:, 32:32 + n], op=ALU.mult),
                reads=["stC"], writes=["stE"])
            p.op("vector", lambda eng, n=n: eng.tensor_tensor(
                out=st2[:, 0:n], in0=st2[:, 0:n], in1=st2[:, 16:16 + n], op=ALU.subtract),
                reads=["stD", "stE"], writes=["stD"])
            p.op("vector", lambda eng, n=n, eps=eps: eng.tensor_scalar_add(
                out=st2[:, 0:n], in0=st2[:, 0:n], scalar1=eps),
                reads=["stD"], writes=["stD"])
            p.op("scalar", lambda eng, n=n: eng.activation(
                out=st2[:, 16:16 + n], in_=st2[:, 0:n], func=AF.Ln),
                reads=["stD"], writes=["stE"])
            p.op("scalar", lambda eng, n=n: eng.activation(
                out=st2[:, 32:32 + n], in_=st2[:, 16:16 + n], func=AF.Exp, scale=-0.5),
                reads=["stE"], writes=["stF"])

        x_sb = RA[:, 0:TT * D]
        x_hal = RC[:, :]
        lnT = RB[:, :]
        lnst = RF[:, :]
        sc_trp = ExitStack()
        trp = [sc_trp.enter_context(nc.psum_tensor(f"trp{i}", [128, 1024], BF16)) for i in range(2)]

        def xt(t):
            if t < TT:
                return x_sb[:, t * D:(t + 1) * D]
            return x_hal[:, (t - TT) * D:(t - TT + 1) * D]

        lnT3 = lnT.rearrange("p (c n) -> p c n", n=TOK + HALO)

        for t in range(TT):
            p.op("sync", lambda eng, t=t: eng.dma_start(out=xt(t), in_=x_d[t * 128:(t + 1) * 128, :]),
                 writes=[("x", t)], dma=True)
        for j in range(TH):
            p.op("sync", lambda eng, j=j: eng.dma_start(out=xt(TT + j), in_=xh_d[j * 128:(j + 1) * 128, :]),
                 writes=[("x", TT + j)], dma=True)

        ln_stats(TA, lambda t: (xt(t), [("x", t)]), EPS_LN)

        lnT_w = []
        for t in range(TA):
            sl = lnst[:, (t % 2) * D:(t % 2 + 1) * D]
            p.op("vector", lambda eng, t=t, sl=sl: eng.tensor_scalar(
                out=sl, in0=xt(t), scalar1=st[:, 32 + t:33 + t], scalar2=st2[:, 32 + t:33 + t],
                op0=ALU.subtract, op1=ALU.mult),
                reads=[("x", t), "stC", "stF"], writes=[("lnstage", t % 2)])
            tokcol = (HALO + t * 128) if t < TT else (t - TT) * 128
            for half in range(2):
                ps = trp[half]
                for j in range(8):
                    c = half * 8 + j
                    p.op("tensor", lambda eng, sl=sl, c=c, j=j, ps=ps: eng.transpose(
                        ps[:, j * 128:(j + 1) * 128], in_=sl[:, c * 128:(c + 1) * 128], identity=idn[:, :]),
                        reads=[("lnstage", t % 2), "idn", ("trpf", half)], writes=[("trp", half)])
                dst = lnT3[:, half * 8:half * 8 + 8, tokcol:tokcol + 128]
                wkey = ("lnT", t, half)
                lnT_w.append(wkey)
                p.op("scalar", lambda eng, ps=ps, dst=dst: eng.activation(out=dst, in_=ps[:, :], func=AF.Copy),
                     reads=[("trp", half)], writes=[wkey, ("trpf", half)])

        if "ln1T" in tap_d:
            p.op("sync", lambda eng: eng.dma_start(out=tap_d["ln1T"], in_=lnT), reads=lnT_w, dma=True)

        p.barrier()

        q_fm = RCb[:, 0:HQ * TOK]
        k_fm = RDb[:, 0:HKV * (TOK + HALO)]
        v_tm = REb[:, 0:TA * (HKV * 128)]
        wring = [RFf[:, i * 256:(i + 1) * 256].bitcast(BF16) for i in range(4)]
        v_fm = RI[:, 0:TOK + HALO]
        sc_trp.close()

        sc_q = ExitStack()
        qps = [sc_q.enter_context(nc.psum_tensor(f"qps{i}", [128, 1024], F32)) for i in range(4)]
        wq_r = wq_d.rearrange("(c p) f -> p c f", p=128)
        for hg in range(4):
            for c in range(DC):
                slot = c % 4
                src = wq_r[:, c, hg * 512:(hg + 1) * 512]
                p.op("gpsimd", lambda eng, src=src, slot=slot: eng.dma_start(
                    out=wring[slot], in_=src), writes=[("wring", slot)], dma=True)
                for hh in range(4):
                    for m in range(2):
                        p.op("tensor", lambda eng, slot=slot, c=c, hh=hh, m=m: eng.matmul(
                            qps[hh][:, m * 512:(m + 1) * 512],
                            lhsT=wring[slot][:, hh * 128:(hh + 1) * 128],
                            rhs=lnT3[:, c, HALO + m * 512:HALO + (m + 1) * 512],
                            start=(c == 0), stop=(c == DC - 1)),
                            reads=[("wring", slot)]
                            + ([("qpf", hh)] if c == 0 and m == 0 else []),
                            writes=[("qp", hh)])
            for hh in range(4):
                h = hg * 4 + hh
                p.op("scalar", lambda eng, hh=hh, h=h: eng.activation(
                    out=q_fm[:, h * TOK:(h + 1) * TOK], in_=qps[hh][:, :],
                    func=AF.Copy, scale=1.0 / 128),
                    reads=[("qp", hh)], writes=[("q", h), ("qpf", hh)])
        p.barrier()
        sc_q.close()

        sc_kv = ExitStack()
        kvps = [sc_kv.enter_context(nc.psum_tensor(f"kvps{i}", [128, 1536], F32)) for i in range(2)]
        trp2 = [sc_kv.enter_context(nc.psum_tensor(f"trpv{i}", [128, 1024], BF16)) for i in range(2)]
        wk_r = wk_d.rearrange("(c p) f -> p c f", p=128)
        wv_r = wv_d.rearrange("(c p) f -> p c f", p=128)
        for kvg in range(2):
            for vv, wsrc in ((0, wk_r), (1, wv_r)):
                for c in range(DC):
                    slot = c % 4
                    src = wsrc[:, c, kvg * 256:(kvg + 1) * 256]
                    p.op("gpsimd", lambda eng, src=src, slot=slot: eng.dma_start(
                        out=wring[slot][:, 0:256], in_=src), writes=[("wring", slot)], dma=True)
                    for j in range(2):
                        for m in range(3):
                            p.op("tensor", lambda eng, slot=slot, c=c, j=j, m=m: eng.matmul(
                                kvps[j][:, m * 512:(m + 1) * 512],
                                lhsT=wring[slot][:, j * 128:(j + 1) * 128],
                                rhs=lnT3[:, c, m * 512:(m + 1) * 512],
                                start=(c == 0), stop=(c == DC - 1)),
                                reads=[("wring", slot)]
                                + ([("kpf", j)] if c == 0 and m == 0 else []),
                                writes=[("kp", j)])
                for j in range(2):
                    kv = kvg * 2 + j
                    if vv == 0:
                        b = kv * (TOK + HALO)
                        p.op("scalar", lambda eng, j=j, b=b: eng.activation(
                            out=k_fm[:, b:b + 1536], in_=kvps[j][:, :], func=AF.Copy),
                            reads=[("kp", j)], writes=[("k", kv), ("kpf", j)])
                    else:
                        p.op("scalar", lambda eng, j=j: eng.activation(
                            out=v_fm[:, :], in_=kvps[j][:, :], func=AF.Copy),
                            reads=[("kp", j), ("vff",)], writes=[("vf",), ("kpf", j)])
                        for t in range(TA):
                            half = t % 2
                            ps = trp2[half]
                            p.op("tensor", lambda eng, t=t, ps=ps: eng.transpose(
                                ps[:, 0:128], in_=v_fm[:, t * 128:(t + 1) * 128], identity=idn[:, :]),
                                reads=[("vf",), "idn", ("trpf2", half)], writes=[("trp2", half)])
                            p.op("scalar", lambda eng, t=t, kv=kv, ps=ps: eng.activation(
                                out=v_tm[:, t * (HKV * 128) + kv * 128: t * (HKV * 128) + (kv + 1) * 128],
                                in_=ps[:, 0:128], func=AF.Copy),
                                reads=[("trp2", half)],
                                writes=[("v", kv, t), ("trpf2", half)] + ([("vff",)] if t == TA - 1 else []))

        if "qfm" in tap_d:
            p.op("sync", lambda eng: eng.dma_start(out=tap_d["qfm"], in_=q_fm),
                 reads=[("q", h) for h in range(HQ)], dma=True)
        if "kfm" in tap_d:
            p.op("sync", lambda eng: eng.dma_start(out=tap_d["kfm"], in_=k_fm),
                 reads=[("k", kv) for kv in range(HKV)], dma=True)
        if "vtm" in tap_d:
            p.op("sync", lambda eng: eng.dma_start(out=tap_d["vtm"], in_=v_tm),
                 reads=[("v", kv, j) for kv in range(HKV) for j in range(TA)], dma=True)

        p.barrier()
        sc_kv.close()

        att_fm = RB[:, 0:HQ * TOK]
        pf = [RFf[:, i * WIN:(i + 1) * WIN] for i in range(3)]
        pb = [RI[:, 0:WIN], RI[:, WIN:2 * WIN], RG[:, :].bitcast(BF16)[:, 0:WIN]]
        pts = [REb[:, 6144 + i * WIN: 6144 + (i + 1) * WIN] for i in range(3)]
        sc_a2 = ExitStack()
        sps = [sc_a2.enter_context(nc.psum_tensor(f"sps{i}", [128, 1024], F32)) for i in range(3)]
        ptp = sc_a2.enter_context(nc.psum_tensor("ptp", [128, WIN], BF16))
        aps = sc_a2.enter_context(nc.psum_tensor("aps", [128, 128], F32))

        mskv = msk[:, :].rearrange("p (t w) -> p t w", w=WIN)
        att_w = [("att", h, t) for h in range(HQ) for t in range(TT)]
        pairs = [(h, t) for h in range(HQ) for t in range(TT)]

        whalf = [RB[:, 16384:20480], RB[:, 20480:24576]]
        wo_r = wo_d.rearrange("(c p) f -> p c f", p=128)

        def wo_load(dsl, hf):
            src = wo_r[:, hf * 8:(hf + 1) * 8, dsl * 512:(dsl + 1) * 512]
            p.op("gpsimd", lambda eng, src=src, hf=hf: eng.dma_start(
                out=whalf[hf].rearrange("p (c f) -> p c f", f=512), in_=src),
                writes=[("wor", dsl, hf)] + ([("wor", dsl - 1, hf)] if dsl > 0 else []),
                dma=True)

        wo_load(0, 0)
        wo_load(0, 1)
        waves = [pairs[b0:b0 + 3] for b0 in range(0, len(pairs), 3)]

        def b2_scores(wv):
            for r, (h, t) in enumerate(wv):
                kv = h // 4
                kbase = kv * (TOK + HALO) + t * 128
                qsl = q_fm[:, h * TOK + t * 128: h * TOK + (t + 1) * 128]
                ps = sps[r]
                p.op("tensor", lambda eng, ps=ps, qsl=qsl, kbase=kbase: eng.matmul(
                    ps[:, 0:512], lhsT=qsl, rhs=k_fm[:, kbase:kbase + 512], start=True, stop=True),
                    reads=[("spsf", r)], writes=[("sps", r)])
                p.op("tensor", lambda eng, ps=ps, qsl=qsl, kbase=kbase: eng.matmul(
                    ps[:, 512:640], lhsT=qsl, rhs=k_fm[:, kbase + 512:kbase + 640], start=True, stop=True),
                    reads=[("spsf", r)], writes=[("sps", r)])

        b2_scores(waves[0])
        for wi, wv in enumerate(waves):
            for r, (h, t) in enumerate(wv):
                p.op("vector", lambda eng, ps=sps[r], t=t: eng.tensor_tensor(
                    out=ps[:, 0:WIN], in0=ps[:, 0:WIN], in1=mskv[:, min(t, 4), :], op=ALU.add),
                    reads=[("sps", r), "msk"], writes=[("sps", r)])
            for r, (h, t) in enumerate(wv):
                p.op("scalar", lambda eng, ps=sps[r], pfr=pf[r], r=r: eng.activation(
                    out=pfr, in_=ps[:, 0:WIN], func=AF.Exp,
                    accum_out=sxp[:, 3 + r:4 + r]),
                    reads=[("sps", r)], writes=[("pf", r), ("sume", r), ("spsf", r)])
            if wi + 1 < len(waves):
                b2_scores(waves[wi + 1])
            for r, (h, t) in enumerate(wv):
                p.op("vector", lambda eng, r=r: eng.reciprocal(
                    out=sxp[:, 6 + r:7 + r], in_=sxp[:, 3 + r:4 + r]),
                    reads=[("sume", r)], writes=[("rsum", r)])
                p.op("vector", lambda eng, pfr=pf[r], pbr=pb[r], r=r: eng.tensor_scalar_mul(
                    out=pbr, in0=pfr, scalar1=sxp[:, 6 + r:7 + r]),
                    reads=[("pf", r), ("rsum", r)], writes=[("pb", r)])
            for r, (h, t) in enumerate(wv):
                for cch in range(5):
                    p.op("tensor", lambda eng, pbr=pb[r], cch=cch: eng.transpose(
                        ptp[:, cch * 128:(cch + 1) * 128], in_=pbr[:, cch * 128:(cch + 1) * 128],
                        identity=idn[:, :]),
                        reads=[("pb", r), "idn", ("ptpf",)], writes=[("ptp",)])
                if r % 2 == 0:
                    p.op("scalar", lambda eng, ptr=pts[r]: eng.activation(
                        out=ptr, in_=ptp[:, :], func=AF.Copy),
                        reads=[("ptp",)], writes=[("pts", r), ("ptpf",)])
                else:
                    p.op("vector", lambda eng, ptr=pts[r]: eng.tensor_copy(
                        out=ptr, in_=ptp[:, :]),
                        reads=[("ptp",)], writes=[("pts", r), ("ptpf",)])
            for r, (h, t) in enumerate(wv):
                kv = h // 4
                for cch in range(5):
                    jj = t + cch
                    vsl = v_tm[:, jj * (HKV * 128) + kv * 128: jj * (HKV * 128) + (kv + 1) * 128]
                    p.op("tensor", lambda eng, ptr=pts[r], vsl=vsl, cch=cch: eng.matmul(
                        aps[:, :], lhsT=vsl, rhs=ptr[:, cch * 128:(cch + 1) * 128],
                        start=(cch == 0), stop=(cch == 4)),
                        reads=[("pts", r)] + ([("apsf",)] if cch == 0 else []),
                        writes=[("aps",)])
                p.op("scalar", lambda eng, h=h, t=t: eng.activation(
                    out=att_fm[:, h * TOK + t * 128: h * TOK + (t + 1) * 128], in_=aps[:, :], func=AF.Copy),
                    reads=[("aps",)], writes=[("att", h, t), ("apsf",)])

        if "att" in tap_d:
            p.op("sync", lambda eng: eng.dma_start(out=tap_d["att"], in_=att_fm), reads=att_w, dma=True)

        p.barrier()
        sc_a2.close()

        def ano(t):
            if t < 4:
                return RC[:, t * D:(t + 1) * D]
            if t < 6:
                return RD[:, (t - 4) * D:(t - 3) * D]
            return RE[:, (t - 6) * D:(t - 5) * D]

        sc_wo = ExitStack()
        wop = [sc_wo.enter_context(nc.psum_tensor(f"wop{i}", [128, 512], F32)) for i in range(2)]

        gi = 0
        for dsl in range(4):
            for t in range(TT):
                me = gi % 2
                ps = wop[me]
                for c in range(DC):
                    p.op("tensor", lambda eng, ps=ps, c=c, t=t: eng.matmul(
                        ps[:, :], lhsT=att_fm[:, c * TOK + t * 128: c * TOK + (t + 1) * 128],
                        rhs=whalf[c // 8][:, (c % 8) * 512:(c % 8 + 1) * 512],
                        start=(c == 0), stop=(c == DC - 1)),
                        reads=[("wor", dsl, c // 8)] + ([("wopf", me)] if c == 0 else []),
                        writes=[("wop", me)])
                p.op("vector", lambda eng, ps=ps, t=t, dsl=dsl: eng.tensor_copy(
                    out=ano(t)[:, dsl * 512:(dsl + 1) * 512], in_=ps[:, :]),
                    reads=[("wop", me)], writes=[("ao", t), ("wopf", me)])
                gi += 1
            if dsl + 1 < 4:
                wo_load(dsl + 1, 0)
                wo_load(dsl + 1, 1)

        if "attno" in tap_d:
            for t in range(TT):
                p.op("sync", lambda eng, t=t: eng.dma_start(
                    out=tap_d["attno"][t * 128:(t + 1) * 128, :], in_=ano(t)),
                    reads=[("ao", t)], dma=True)

        ln_stats(TT, lambda t: (ano(t), [("ao", t)]), EPS_LN)
        for t in range(TT):
            p.op("vector", lambda eng, t=t: eng.tensor_scalar(
                out=ano(t), in0=ano(t), scalar1=st[:, 32 + t:33 + t], scalar2=st2[:, 32 + t:33 + t],
                op0=ALU.subtract, op1=ALU.mult),
                reads=[("ao", t), "stC", "stF"], writes=[("ao", t)])
            p.op("vector", lambda eng, t=t: eng.tensor_tensor(
                out=xt(t), in0=xt(t), in1=ano(t), op=ALU.add),
                reads=[("x", t), ("ao", t)], writes=[("x", t)])
            p.op("scalar", lambda eng, t=t: eng.dma_start(out=x2_d[t * 128:(t + 1) * 128, :], in_=xt(t)),
                 reads=[("x", t)], writes=[("x2d", t)], dma=True)

        if "x2" in tap_d:
            p.op("sync", lambda eng: eng.dma_start(out=tap_d["x2"], in_=x_sb),
                 reads=[("x", t) for t in range(TT)], dma=True)

        p.barrier()
        sc_wo.close()

        x2T = RCb[:, :]
        x2st = RF[:, :]
        sq_sb = RD[:, 0:1024]
        sq_bf = RDb[:, 0:1024]
        r3row = RD[:, 1024:2048]
        sc_x2s = ExitStack()
        tr2 = [sc_x2s.enter_context(nc.psum_tensor(f"tr2_{i}", [128, 1024], BF16)) for i in range(2)]
        csp = sc_x2s.enter_context(nc.psum_tensor("csp", [128, 1024], F32))
        csq2 = sc_x2s.enter_context(nc.psum_tensor("csq2", [128, 1024], F32))

        x2T3 = x2T.rearrange("p (c n) -> p c n", n=TOK)
        for t in range(TT):
            sl = x2st[:, (t % 2) * D:(t % 2 + 1) * D]
            p.op("vector", lambda eng, t=t, sl=sl: eng.tensor_copy(out=sl, in_=xt(t)),
                 reads=[("x", t)], writes=[("x2st", t % 2)])
            for half in range(2):
                ps = tr2[half]
                for j in range(8):
                    c = half * 8 + j
                    p.op("tensor", lambda eng, sl=sl, c=c, j=j, ps=ps: eng.transpose(
                        ps[:, j * 128:(j + 1) * 128], in_=sl[:, c * 128:(c + 1) * 128], identity=idn[:, :]),
                        reads=[("x2st", t % 2), "idn", ("tr2f", half)], writes=[("tr2", half)])
                dst = x2T3[:, half * 8:half * 8 + 8, t * 128:(t + 1) * 128]
                p.op("scalar", lambda eng, ps=ps, dst=dst: eng.activation(out=dst, in_=ps[:, :], func=AF.Copy),
                     reads=[("tr2", half)], writes=[("x2T", t, half), ("tr2f", half)])

        x2t_r = [("x2T", t, hf) for t in range(TT) for hf in range(2)]
        for c in range(DC):
            for m in range(2):
                p.op("tensor", lambda eng, c=c, m=m: eng.matmul(
                    csp[0:1, m * 512:(m + 1) * 512], lhsT=ones[:, 0:1],
                    rhs=x2T3[:, c, m * 512:(m + 1) * 512],
                    start=(c == 0), stop=(c == DC - 1)),
                    reads=x2t_r + ["ones"], writes=[("csp",)])
        for c in range(DC):
            p.op("scalar", lambda eng, c=c: eng.activation(out=sq_bf, in_=x2T3[:, c, :], func=AF.Square),
                 reads=x2t_r + [("sqf",)], writes=[("sq",)])
            for m in range(2):
                p.op("tensor", lambda eng, c=c, m=m: eng.matmul(
                    csq2[0:1, m * 512:(m + 1) * 512], lhsT=ones[:, 0:1],
                    rhs=sq_bf[:, m * 512:(m + 1) * 512],
                    start=(c == 0), stop=(c == DC - 1)),
                    reads=[("sq",)], writes=[("csq2",), ("sqf",)])
        p.op("vector", lambda eng: eng.tensor_scalar_mul(
            out=negm3[0:1, :], in0=csp[0:1, 0:TOK], scalar1=-1.0 / D),
            reads=[("csp",)], writes=["negm3"])
        p.op("vector", lambda eng: eng.tensor_scalar_mul(
            out=r3row[0:1, 0:TOK], in0=csp[0:1, 0:TOK], scalar1=1.0 / D),
            reads=[("csp",)], writes=[("m3r",)])
        p.op("vector", lambda eng: eng.tensor_tensor(
            out=r3row[0:1, 0:TOK], in0=r3row[0:1, 0:TOK], in1=r3row[0:1, 0:TOK], op=ALU.mult),
            reads=[("m3r",)], writes=[("m3r",)])
        p.op("vector", lambda eng: eng.tensor_scalar(
            out=r3row[1:2, 0:TOK] if False else sq_sb[0:1, 0:TOK], in0=csq2[0:1, 0:TOK],
            scalar1=1.0 / D, scalar2=0.0, op0=ALU.mult, op1=ALU.add),
            reads=[("csq2",), ("sq",)], writes=[("e2r",)])
        p.op("vector", lambda eng: eng.tensor_tensor(
            out=r3row[0:1, 0:TOK], in0=sq_sb[0:1, 0:TOK], in1=r3row[0:1, 0:TOK], op=ALU.subtract),
            reads=[("e2r",), ("m3r",)], writes=[("v3r",)])
        p.op("vector", lambda eng: eng.tensor_scalar_add(
            out=r3row[0:1, 0:TOK], in0=r3row[0:1, 0:TOK], scalar1=EPS_LN),
            reads=[("v3r",)], writes=[("v3r",)])
        p.op("scalar", lambda eng: eng.activation(out=r3row[0:1, 0:TOK], in_=r3row[0:1, 0:TOK], func=AF.Ln),
             reads=[("v3r",)], writes=[("v3r",)])
        p.op("scalar", lambda eng: eng.activation(out=r3row[0:1, 0:TOK], in_=r3row[0:1, 0:TOK],
                                                  func=AF.Exp, scale=-0.5),
             reads=[("v3r",)], writes=[("r3row",)])
        nmr3row = RD[0:1, 2048:2048 + TOK]
        p.op("vector", lambda eng: eng.tensor_tensor(
            out=nmr3row, in0=r3row[0:1, 0:TOK], in1=negm3[0:1, :], op=ALU.mult),
            reads=[("r3row",), "negm3"], writes=[("nmr3r",)])
        r3full_bf = RDb[:, 0:TOK]
        nmr3full_bf = RDb[:, TOK:2 * TOK]
        for m in range(2):
            p.op("tensor", lambda eng, m=m: eng.matmul(
                csp[:, m * 512:(m + 1) * 512], lhsT=onesr[0:1, :],
                rhs=r3row[0:1, m * 512:(m + 1) * 512], start=True, stop=True),
                reads=[("r3row",), "onesr", ("csp",)], writes=[("csp",)])
            p.op("tensor", lambda eng, m=m: eng.matmul(
                csq2[:, m * 512:(m + 1) * 512], lhsT=onesr[0:1, :],
                rhs=nmr3row[0:1, m * 512:(m + 1) * 512], start=True, stop=True),
                reads=[("nmr3r",), "onesr", ("csq2",), ("sq",)], writes=[("csq2",)])
        p.op("vector", lambda eng: eng.tensor_copy(out=r3full_bf, in_=csp[:, 0:TOK]),
             reads=[("csp",)], writes=["r3full", ("sq",), ("e2r",)])
        p.op("vector", lambda eng: eng.tensor_copy(out=nmr3full_bf, in_=csq2[:, 0:TOK]),
             reads=[("csq2",)], writes=["nmr3full", ("e2r",)])
        for c in range(DC):
            p.op("vector", lambda eng, c=c: eng.tensor_tensor(
                out=x2T3[:, c, :], in0=x2T3[:, c, :], in1=r3full_bf, op=ALU.mult),
                reads=["r3full"], writes=[("ln3m", c)] + [("x2T", t, c // 8) for t in range(TT)])
            p.op("vector", lambda eng, c=c: eng.tensor_tensor(
                out=x2T3[:, c, :], in0=x2T3[:, c, :], in1=nmr3full_bf, op=ALU.add),
                reads=[("ln3m", c), "nmr3full"], writes=[("ln3", c)])

        p.barrier()
        sc_x2s.close()

        h_fm = RAb[:, :]
        RING = 4
        wsl = [RB[:, i * 4096:(i + 1) * 4096] for i in range(RING)]
        qsc = []
        for k_ in range(4):
            base = RB[:, 16384 + k_ * 4096: 16384 + (k_ + 1) * 4096] if k_ < 2 \
                else RDb[:, (k_ - 2) * 4096:(k_ - 1) * 4096]
            qsc.append(dict(
                wnat=base[:, 0:1024], qa=base[:, 1024:2048],
                qsg=base[:, 2048:3072], qms=base[:, 3072:4096],
                qr=qr[:, k_ * 16:k_ * 16 + 16], qs=qs[:, k_ * 16:k_ * 16 + 16],
                qt=qt[:, k_ * 16:k_ * 16 + 16]))
        sgf = [REb[:, i * 1024:(i + 1) * 1024] for i in range(2)]
        hsq = REb[:, 4096:5120]
        ssqrow = RE[0:1, 2048:3072]

        h_fm3 = h_fm.rearrange("p (c n) -> p c n", n=TOK)
        qpar = [0]

        def ano2(t):
            if t < 4:
                return RC[:, t * D:(t + 1) * D]
            if t < 6:
                return RBf[:, 8192 + (t - 4) * D: 8192 + (t - 3) * D]
            return RE[:, (t - 6) * D:(t - 5) * D]

        def emit_quant_wave(jobs, sets, sid):
            n = len(jobs)
            rb = sid % RING
            for i in range(n):
                k_, (src, jj, mk) = sets[i], jobs[i]
                p.op("gpsimd", lambda eng, a=src, w=qsc[k_]["wnat"]: eng.dma_start(out=w, in_=a),
                     writes=[("wnat", k_)], dma=True)
            for i in range(n):
                k_ = sets[i]
                p.op("vector", lambda eng, w=qsc[k_]["wnat"], q=qsc[k_]["qr"]: eng.reduce_sum(
                    out=q, in_=w.rearrange("p (g s) -> p g s", s=64),
                    axis=AX.X, apply_absolute_value=True),
                    reads=[("wnat", k_)], writes=[("qr", k_)])
            for i in range(n):
                k_ = sets[i]
                p.op("scalar", lambda eng, w=qsc[k_]["wnat"], q=qsc[k_]["qa"]: eng.activation(
                    out=q, in_=w, func=AF.Abs),
                    reads=[("wnat", k_)], writes=[("qa", k_)])
            for i in range(n):
                k_ = sets[i]
                p.op("scalar", lambda eng, w=qsc[k_]["wnat"], q=qsc[k_]["qsg"]: eng.activation(
                    out=q, in_=w, func=AF.Sign),
                    reads=[("wnat", k_)], writes=[("qsg", k_)])
            for i in range(n):
                k_ = sets[i]
                p.op("vector", lambda eng, a=qsc[k_]["qr"], b=qsc[k_]["qs"]: eng.tensor_scalar(
                    out=b, in0=a, scalar1=1.0 / 64, scalar2=1e-8, op0=ALU.mult, op1=ALU.max),
                    reads=[("qr", k_)], writes=[("qs", k_)])
                p.op("vector", lambda eng, a=qsc[k_]["qs"], b=qsc[k_]["qt"]: eng.tensor_scalar_mul(
                    out=b, in0=a, scalar1=THR),
                    reads=[("qs", k_)], writes=[("qt", k_)])
            for i in range(n):
                k_ = sets[i]
                sc = qsc[k_]
                a3 = sc["qa"].rearrange("p (g s) -> p g s", s=64)
                m3 = sc["qms"].rearrange("p (g s) -> p g s", s=64)
                tb = sc["qt"].unsqueeze(2).broadcast_to((128, 16, 64))
                p.op("vector", lambda eng, a3=a3, m3=m3, tb=tb: eng.tensor_tensor(
                    out=m3, in0=a3, in1=tb, op=ALU.is_gt),
                    reads=[("qa", k_), ("qt", k_)], writes=[("qms", k_)])
            for i in range(n):
                k_ = sets[i]
                sc = qsc[k_]
                m3 = sc["qms"].rearrange("p (g s) -> p g s", s=64)
                sb = sc["qs"].unsqueeze(2).broadcast_to((128, 16, 64))
                p.op("vector", lambda eng, m3=m3, sb=sb: eng.tensor_tensor(
                    out=m3, in0=m3, in1=sb, op=ALU.mult),
                    reads=[("qms", k_), ("qs", k_)], writes=[("qms", k_)])
            for i in range(n):
                k_ = sets[i]
                sc = qsc[k_]
                p.op("gpsimd", lambda eng, a=sc["qa"], m=sc["qms"], g=sc["qsg"]: eng.tensor_tensor(
                    out=a, in0=m, in1=g, op=ALU.mult),
                    reads=[("qms", k_), ("qsg", k_)], writes=[("qa", k_)])
            for i in range(n):
                k_, (src, jj, mk) = sets[i], jobs[i]
                tp = qpar[0] % 2
                qpar[0] += 1
                ps = trcs[tp]
                for j in range(8):
                    p.op("tensor", lambda eng, j=j, ps=ps, q=qsc[k_]["qa"]: eng.transpose(
                        ps[:, j * 128:(j + 1) * 128], in_=q[:, j * 128:(j + 1) * 128],
                        identity=idn[:, :]),
                        reads=[("qa", k_), "idn", ("trcf", tp)], writes=[("trc", tp)])
                out_ap, three_d = mk(rb)
                in_ap = ps[:, :].rearrange("p (j x) -> p j x", x=128) if three_d else ps[:, :]
                wkeys = [("wb", sid, jj)]
                if sid >= RING:
                    wkeys += [("wb", sid - RING, j2) for j2 in range(4)]
                p.op("scalar", lambda eng, out_ap=out_ap, in_ap=in_ap: eng.activation(
                    out=out_ap, in_=in_ap, func=AF.Copy),
                    reads=[("trc", tp), ("qa", k_)], writes=wkeys + [("trcf", tp)])

        qjobs = []
        for hh in range(2):
            for pair in range(16):
                for gu, base in ((0, 0), (1, H)):
                    row0 = base + hh * HHALF + pair * 256
                    for jj, (sub, colh) in enumerate(
                            [(s, c) for s in range(2) for c in range(2)]):
                        def mk_gu(rb, colh=colh, sub=sub):
                            o = wsl[rb].rearrange("p (c x) -> p c x", x=256)
                            return (o[:, colh * 8:(colh + 1) * 8,
                                      sub * 128:(sub + 1) * 128], True)
                        qjobs.append((wgu_d[row0 + sub * 128: row0 + (sub + 1) * 128,
                                            colh * 1024:(colh + 1) * 1024], jj, mk_gu))
            for dc in range(DC):
                for pcl in range(4):
                    def mk_wp(rb, pcl=pcl):
                        return (wsl[rb][:, pcl * 1024:(pcl + 1) * 1024], False)
                    qjobs.append((wp_d[dc * 128:(dc + 1) * 128,
                                       hh * HHALF + pcl * 1024: hh * HHALF + (pcl + 1) * 1024],
                                  pcl, mk_wp))
        NSLICE = len(qjobs) // 4
        qnext = [0]
        qdma_done = set()
        SETS4 = [0, 1, 2, 3]

        def emit_qdma_g(jobs, sets):
            for i, (src, jj, mk) in enumerate(jobs):
                k_ = sets[i]
                p.op("gpsimd", lambda eng, a=src, w=qsc[k_]["wnat"]: eng.dma_start(out=w, in_=a),
                     writes=[("wnat", k_)], dma=True)

        def emit_qdma(sid):
            emit_qdma_g(qjobs[sid * 4:(sid + 1) * 4], SETS4)
            qdma_done.add(sid)

        def emit_qRED_g(jobs, sets):
            for i in range(len(jobs)):
                k_ = sets[i]
                p.op("vector", lambda eng, w=qsc[k_]["wnat"], q=qsc[k_]["qr"]: eng.reduce_sum(
                    out=q, in_=w.rearrange("p (g s) -> p g s", s=64),
                    axis=AX.X, apply_absolute_value=True),
                    reads=[("wnat", k_)], writes=[("qr", k_)])

        def emit_qTS_g(jobs, sets):
            for i in range(len(jobs)):
                k_ = sets[i]
                p.op("vector", lambda eng, a=qsc[k_]["qr"], b=qsc[k_]["qs"]: eng.tensor_scalar(
                    out=b, in0=a, scalar1=1.0 / 64, scalar2=1e-8, op0=ALU.mult, op1=ALU.max),
                    reads=[("qr", k_)], writes=[("qs", k_)])
                p.op("vector", lambda eng, a=qsc[k_]["qs"], b=qsc[k_]["qt"]: eng.tensor_scalar_mul(
                    out=b, in0=a, scalar1=THR),
                    reads=[("qs", k_)], writes=[("qt", k_)])

        def emit_qA1_g(jobs, sets):
            emit_qRED_g(jobs, sets)
            emit_qTS_g(jobs, sets)

        def emit_qA2_g(jobs, sets):
            n = len(jobs)
            for i in range(n):
                k_ = sets[i]
                p.op("scalar", lambda eng, w=qsc[k_]["wnat"], q=qsc[k_]["qa"]: eng.activation(
                    out=q, in_=w, func=AF.Abs),
                    reads=[("wnat", k_)], writes=[("qa", k_)])
            for i in range(n):
                k_ = sets[i]
                p.op("scalar", lambda eng, w=qsc[k_]["wnat"], q=qsc[k_]["qsg"]: eng.activation(
                    out=q, in_=w, func=AF.Sign),
                    reads=[("wnat", k_)], writes=[("qsg", k_)])

        def emit_qA(sid):
            jobs = qjobs[sid * 4:(sid + 1) * 4]
            emit_qA1_g(jobs, SETS4)
            emit_qA2_g(jobs, SETS4)

        def emit_qB1_g(jobs, sets):
            n = len(jobs)
            for i in range(n):
                k_ = sets[i]
                sc = qsc[k_]
                a3 = sc["qa"].rearrange("p (g s) -> p g s", s=64)
                m3 = sc["qms"].rearrange("p (g s) -> p g s", s=64)
                tb = sc["qt"].unsqueeze(2).broadcast_to((128, 16, 64))
                p.op("vector", lambda eng, a3=a3, m3=m3, tb=tb: eng.tensor_tensor(
                    out=m3, in0=a3, in1=tb, op=ALU.is_gt),
                    reads=[("qa", k_), ("qt", k_)], writes=[("qms", k_)])
            for i in range(n):
                k_ = sets[i]
                sc = qsc[k_]
                m3 = sc["qms"].rearrange("p (g s) -> p g s", s=64)
                sb = sc["qs"].unsqueeze(2).broadcast_to((128, 16, 64))
                p.op("vector", lambda eng, m3=m3, sb=sb: eng.tensor_tensor(
                    out=m3, in0=m3, in1=sb, op=ALU.mult),
                    reads=[("qms", k_), ("qs", k_)], writes=[("qms", k_)])
            for i in range(n):
                k_ = sets[i]
                sc = qsc[k_]
                p.op("gpsimd", lambda eng, a=sc["qa"], m=sc["qms"], g=sc["qsg"]: eng.tensor_tensor(
                    out=a, in0=m, in1=g, op=ALU.mult),
                    reads=[("qms", k_), ("qsg", k_)], writes=[("qa", k_)])

        def emit_qB1(sid):
            emit_qB1_g(qjobs[sid * 4:(sid + 1) * 4], SETS4)

        def emit_qB2_g(jobs, sets, sid):
            rb = sid % RING
            for i in range(len(jobs)):
                k_, (src, jj, mk) = sets[i], jobs[i]
                tp = qpar[0] % 2
                qpar[0] += 1
                ps = trcs[tp]
                for j in range(8):
                    p.op("tensor", lambda eng, j=j, ps=ps, q=qsc[k_]["qa"]: eng.transpose(
                        ps[:, j * 128:(j + 1) * 128], in_=q[:, j * 128:(j + 1) * 128],
                        identity=idn[:, :]),
                        reads=[("qa", k_), "idn", ("trcf", tp)], writes=[("trc", tp)])
                out_ap, three_d = mk(rb)
                in_ap = ps[:, :].rearrange("p (j x) -> p j x", x=128) if three_d else ps[:, :]
                wkeys = [("wb", sid, jj)]
                if sid >= RING:
                    wkeys += [("wb", sid - RING, j2) for j2 in range(4)]
                p.op("scalar", lambda eng, out_ap=out_ap, in_ap=in_ap: eng.activation(
                    out=out_ap, in_=in_ap, func=AF.Copy),
                    reads=[("trc", tp), ("qa", k_)], writes=wkeys + [("trcf", tp)])

        def emit_qB2(sid):
            emit_qB2_g(qjobs[sid * 4:(sid + 1) * 4], SETS4, sid)

        RLIM = NSLICE - 12
        qa_pend = [False]

        def pump_half_a():
            if qa_pend[0] or qnext[0] >= min(NSLICE, RLIM):
                return
            sid = qnext[0]
            if sid not in qdma_done:
                emit_qdma(sid)
            emit_qA(sid)
            if sid + 1 < min(NSLICE, RLIM):
                emit_qdma(sid + 1)
            qa_pend[0] = True

        qb1_done = [False]

        def pump_half_b1():
            if qa_pend[0] and not qb1_done[0]:
                emit_qB1(qnext[0])
                qb1_done[0] = True

        def pump_half_b():
            if qa_pend[0]:
                pump_half_b1()
                emit_qB2(qnext[0])
                qnext[0] += 1
                qa_pend[0] = False
                qb1_done[0] = False

        def pump_to(upto):
            while qnext[0] < min(upto, RLIM):
                pump_half_a()
                pump_half_b()

        RSETS = [2, 3]
        NU = (NSLICE - RLIM) * 2
        rseq = [("a1", 0), ("a2", 0), ("b1", 0), ("a1", 1)]
        for u_ in range(1, NU):
            rseq += [("b2", u_ - 1), ("a2", u_), ("b1", u_)]
            if u_ + 1 < NU:
                rseq += [("a1", u_ + 1)]
        rseq += [("b2", NU - 1)]
        rsi = [0]
        rdone = {"b2": 0}

        def runit_jobs(u):
            sid = RLIM + u // 2
            o = sid * 4 + (u % 2) * 2
            return qjobs[o:o + 2], sid

        def rstep():
            if rsi[0] >= len(rseq):
                return
            stg, u = rseq[rsi[0]]
            rsi[0] += 1
            jobs, sid = runit_jobs(u)
            if stg == "a1":
                emit_qdma_g(jobs, RSETS)
            elif stg == "a2":
                emit_qA2_g(jobs, RSETS)
            elif stg == "b1":
                emit_qRED_g(jobs, RSETS)
                emit_qTS_g(jobs, RSETS)
                emit_qB1_g(jobs, RSETS)
            else:
                emit_qB2_g(jobs, RSETS, sid)
                rdone["b2"] += 1

        for hhalf in range(2):
            sc_c1 = ExitStack()
            pps = [sc_c1.enter_context(nc.psum_tensor(f"pps{hhalf}_{i}", [128, 1024], F32)) for i in range(2)]
            trcs = [sc_c1.enter_context(nc.psum_tensor(f"trc{hhalf}_{i}", [128, 1024], BF16)) for i in range(2)]
            sqp = sc_c1.enter_context(nc.psum_tensor(f"sqp{hhalf}", [128, 1024], F32))

            pump_to(hhalf * (NSLICE // 2) + RING)
            gi = 0
            for pair in range(16):
                pump_half_a()
                for gu in range(2):
                    sid = hhalf * (NSLICE // 2) + 2 * pair + gu
                    assert qnext[0] > sid, ("quant not emitted", sid)
                    rb = sid % RING
                    wv = wsl[rb].rearrange("p (c x) -> p c x", x=256)
                    mes = []
                    for hcl in range(2):
                        me = gi % 2
                        mes.append(me)
                        ps = pps[me]
                        for c in range(DC):
                            for m in range(2):
                                p.op("tensor", lambda eng, ps=ps, c=c, m=m, hcl=hcl, wv=wv: eng.matmul(
                                    ps[:, m * 512:(m + 1) * 512],
                                    lhsT=wv[:, c, hcl * 128:(hcl + 1) * 128],
                                    rhs=x2T3[:, c, m * 512:(m + 1) * 512],
                                    start=(c == 0), stop=(c == DC - 1)),
                                    reads=[("wb", sid, hcl * 2 + c // 8)]
                                    + ([("ppsf", me)] if c == 0 and m == 0 else []),
                                    writes=[("pps", me)])
                        gi += 1
                    pump_half_b1()
                    for hcl in range(2):
                        me = mes[hcl]
                        ps = pps[me]
                        if gu == 0:
                            p.op("scalar", lambda eng, ps=ps, hcl=hcl: eng.activation(
                                out=sgf[hcl], in_=ps[:, :], func=AF.Silu),
                                reads=[("pps", me)], writes=[("sg", hcl), ("ppsf", me)])
                        else:
                            ch = pair * 2 + hcl
                            p.op("vector", lambda eng, ps=ps, ch=ch, hcl=hcl: eng.tensor_tensor(
                                out=h_fm3[:, ch, :], in0=sgf[hcl], in1=ps[:, :], op=ALU.mult),
                                reads=[("sg", hcl), ("pps", me)], writes=[("hf", ch), ("ppsf", me)])
                            p.op("scalar", lambda eng, ch=ch: eng.activation(
                                out=hsq, in_=h_fm3[:, ch, :], func=AF.Square),
                                reads=[("hf", ch), ("hsqf",)], writes=[("hsq",)])
                            first = (pair == 0 and hcl == 0)
                            last = (pair == 15 and hcl == 1)
                            for m in range(2):
                                p.op("tensor", lambda eng, m=m, first=first, last=last: eng.matmul(
                                    sqp[0:1, m * 512:(m + 1) * 512], lhsT=ones[:, 0:1],
                                    rhs=hsq[:, m * 512:(m + 1) * 512],
                                    start=first, stop=last),
                                    reads=[("hsq",), "ones"], writes=[("sqp",), ("hsqf",)])
                    pump_half_b()
                    if gu == 0:
                        pump_half_a()
                pump_to(hhalf * (NSLICE // 2) + 2 * pair + 2 + RING)

            p.op("vector", lambda eng: eng.tensor_copy(out=ssqrow, in_=sqp[0:1, 0:TOK]),
                 reads=[("sqp",), ("hsq",)], writes=[("ssqr",)])
            for t in range(TT):
                p.op("tensor", lambda eng, t=t: eng.transpose(
                    pps[0][:, 0:128], in_=RE[:, 2048 + t * 128: 2048 + (t + 1) * 128],
                    identity=idnf[:, :]),
                    reads=[("ssqr",), "idnf", ("ppsf", 0)], writes=[("pps", 0)])
                p.op("vector", lambda eng, t=t, hhalf=hhalf: eng.tensor_copy(
                    out=ssq_h[:, t * 2 + hhalf: t * 2 + hhalf + 1], in_=pps[0][:, 0:1]),
                    reads=[("pps", 0)], writes=[("ssqc", hhalf, t), ("ppsf", 0)])

            if "h0" in tap_d and hhalf == 0:
                p.op("sync", lambda eng: eng.dma_start(out=tap_d["h0"], in_=h_fm),
                     reads=[("hf", ch) for ch in range(32)], dma=True)

            p.barrier()
            sc_c1.close()

            if hhalf == 1:
                ssq2 = ssq_h[:, 0:2 * TT].rearrange("p (t u) -> p t u", u=2)
                p.op("vector", lambda eng: eng.tensor_tensor(
                    out=rmsr[:, 0:TT], in0=ssq2[:, :, 0], in1=ssq2[:, :, 1], op=ALU.add),
                    reads=[], writes=["rmsA"])
                p.op("vector", lambda eng: eng.tensor_scalar(
                    out=rmsr[:, 0:TT], in0=rmsr[:, 0:TT], scalar1=1.0 / H, scalar2=EPS_RMS,
                    op0=ALU.mult, op1=ALU.add),
                    reads=["rmsA"], writes=["rmsA"])
                p.op("scalar", lambda eng: eng.activation(out=rmsr[:, 0:TT], in_=rmsr[:, 0:TT], func=AF.Ln),
                     reads=["rmsA"], writes=["rmsA"])
                p.op("scalar", lambda eng: eng.activation(out=rmsr[:, 0:TT], in_=rmsr[:, 0:TT],
                                                          func=AF.Exp, scale=-0.5),
                     reads=["rmsA"], writes=["rmsr"])

            sc_c2 = ExitStack()
            pjp = [sc_c2.enter_context(nc.psum_tensor(f"pjp{hhalf}_{i}", [128, 1024], F32)) for i in range(2)]
            trcs = [sc_c2.enter_context(nc.psum_tensor(f"trd{hhalf}_{i}", [128, 1024], BF16)) for i in range(2)]
            tro = sc_c2.enter_context(nc.psum_tensor(f"tro{hhalf}", [128, 1024], BF16))
            pst = RG[:, 0:512]
            x2r = RFf[:, :]
            ffcv = RG[:, :].bitcast(BF16)

            for dc in range(DC):
                sid = hhalf * (NSLICE // 2) + 32 + dc
                if sid < RLIM:
                    assert qnext[0] > sid, ("wp quant not emitted", sid)
                else:
                    assert rdone["b2"] >= (sid - RLIM + 1) * 2, ("tail unit", sid)
                pump_half_a()
                if hhalf == 1:
                    rstep()
                rb = sid % RING
                me = dc % 2
                ps = pjp[me]
                for m in range(2):
                    for ci in range(32):
                        p.op("tensor", lambda eng, ps=ps, ci=ci, m=m, rb=rb: eng.matmul(
                            ps[:, m * 512:(m + 1) * 512],
                            lhsT=wsl[rb][:, ci * 128:(ci + 1) * 128],
                            rhs=h_fm3[:, ci, m * 512:(m + 1) * 512],
                            start=(ci == 0), stop=(ci == 31)),
                            reads=[("wb", sid, ci // 8)]
                            + ([("pjpf", me)] if ci == 0 and m == 0 else []),
                            writes=[("pjp", me, m)])
                pump_half_b1()
                if hhalf == 1:
                    rstep()
                    rstep()
                if hhalf == 0:
                    for m in range(2):
                        p.op("vector", lambda eng, ps=ps, m=m: eng.tensor_copy(
                            out=pst, in_=ps[:, m * 512:(m + 1) * 512]),
                            reads=[("pjp", me, m), ("pstf",)], writes=[("pst",)])
                        p.op("scalar", lambda eng, dc=dc, m=m: eng.dma_start(
                            out=pp_d[dc * 128:(dc + 1) * 128, m * 512:(m + 1) * 512], in_=pst),
                            reads=[("pst",)], writes=[("ppd", dc, m), ("pstf",)], dma=True)
                    p.op("sync", lambda eng: eng.nop(),
                         reads=[("pjp", me, 0), ("pjp", me, 1)], writes=[("pjpf", me)])
                else:
                    p.op("sync", lambda eng, dc=dc: eng.dma_start(
                        out=x2r[:, 0:1024], in_=pp_d[dc * 128:(dc + 1) * 128, 0:1024]),
                        reads=[("x2rf",)], writes=[("x2r",)], dma=True)
                    p.op("vector", lambda eng, ps=ps: eng.tensor_tensor(
                        out=ffcv, in0=ps[:, :], in1=x2r[:, 0:1024], op=ALU.add),
                        reads=[("pjp", me, 0), ("pjp", me, 1), ("x2r",), ("ffcf",)],
                        writes=[("ffc",), ("pjpf", me), ("x2rf",)])
                    rstep()
                    rstep()
                    for t in range(TT):
                        p.op("tensor", lambda eng, t=t: eng.transpose(
                            tro[:, t * 128:(t + 1) * 128],
                            in_=ffcv[:, t * 128:(t + 1) * 128], identity=idn[:, :]),
                            reads=[("ffc",), "idn"] + ([("trof",)] if t == 0 else []),
                            writes=[("tro",)])
                    for t in range(TT):
                        p.op("vector", lambda eng, t=t, dc=dc: eng.tensor_scalar_mul(
                            out=ano2(t)[:, dc * 128:(dc + 1) * 128], in0=tro[:, t * 128:(t + 1) * 128],
                            scalar1=rmsr[:, t:t + 1]),
                            reads=[("tro",), "rmsr"],
                            writes=[("ffo", t)] + ([("trof",), ("ffcf",)] if t == 7 else []))
                if hhalf == 1:
                    rstep()
                    rstep()
                pump_half_b()
                pump_to(hhalf * (NSLICE // 2) + 32 + dc + 1 + RING)

            if hhalf == 1:
                if "ffo" in tap_d:
                    for t in range(TT):
                        p.op("sync", lambda eng, t=t: eng.dma_start(
                            out=tap_d["ffo"][t * 128:(t + 1) * 128, :], in_=ano2(t)),
                            reads=[("ffo", t)], dma=True)
                ln_stats(TT, lambda t: (ano2(t), [("ffo", t)]), EPS_LN)
                rdx = [RD[:, 0:D], RD[:, D:2 * D]]
                for t in range(TT):
                    xb = rdx[t % 2]
                    wk = [("x2b", t % 2)] + ([("qa", 2), ("qa", 3), ("qms", 2),
                                              ("qms", 3)] if t < 2 else [])
                    p.op("sync", lambda eng, t=t, xb=xb: eng.dma_start(
                        out=xb, in_=x2_d[t * 128:(t + 1) * 128, :]),
                        writes=wk, dma=True)
                    p.op("vector", lambda eng, t=t: eng.tensor_scalar(
                        out=ano2(t), in0=ano2(t), scalar1=st[:, 32 + t:33 + t], scalar2=st2[:, 32 + t:33 + t],
                        op0=ALU.subtract, op1=ALU.mult),
                        reads=[("ffo", t), "stC", "stF"], writes=[("ffo", t)])
                    p.op("vector", lambda eng, t=t, xb=xb: eng.tensor_tensor(
                        out=ano2(t), in0=ano2(t), in1=xb, op=ALU.add),
                        reads=[("ffo", t), ("x2b", t % 2)], writes=[("ffo", t)])
                    p.op("scalar", lambda eng, t=t: eng.dma_start(
                        out=out_d[t * 128:(t + 1) * 128, :], in_=ano2(t)),
                        reads=[("ffo", t)], writes=[("outd", t)], dma=True)

            p.barrier()
            sc_c2.close()


        p.op("gpsimd", lambda eng: eng.nop(), reads=[("outd", t) for t in range(TT)])

        with nc.Block() as block:
            p.emit(block, sems_factory)

    return nc


_NC = {}


def _masks(boundary):
    m = np.zeros((5, 128, WIN), dtype=np.float32)
    i = np.arange(128)[:, None]
    j = np.arange(WIN)[None, :]
    for t in range(5):
        valid = (j >= i + 1) & (j <= i + 512)
        if boundary and t < 4:
            valid = valid & ((128 * t + j - 512) >= 0)
        m[t] = np.where(valid, 0.0, NEG)
    return m.reshape(5 * 128, WIN).astype(bfdt)


def kernel(L0_attn_hidden_states, L0_attn_Wq, L0_attn_Wk, L0_attn_Wv, L0_attn_Wo,
           L0_ffn_w_gate_up, L0_ffn_w_proj):
    hs = np.ascontiguousarray(L0_attn_hidden_states, dtype=np.float32)
    if "prog" not in _NC:
        _NC["prog"] = build()
    nc = _NC["prog"]

    shared = {
        "wq": np.ascontiguousarray(L0_attn_Wq, dtype=np.float32),
        "wk": np.ascontiguousarray(L0_attn_Wk, dtype=np.float32),
        "wv": np.ascontiguousarray(L0_attn_Wv, dtype=np.float32),
        "wo": np.ascontiguousarray(L0_attn_Wo, dtype=np.float32),
        "wgu": np.ascontiguousarray(L0_ffn_w_gate_up, dtype=np.float32),
        "wp": np.ascontiguousarray(L0_ffn_w_proj, dtype=np.float32),
        "idn": np.eye(128, dtype=bfdt),
    }
    mb, mi = _masks(True), _masks(False)
    zh = np.zeros((HALO, D), dtype=np.float32)
    in_maps = []
    for c in range(NCORES):
        b, q = divmod(c, 4)
        im = dict(shared)
        im["x"] = np.ascontiguousarray(hs[b, q * TOK:(q + 1) * TOK, :])
        im["xh"] = zh if q == 0 else np.ascontiguousarray(hs[b, q * TOK - HALO:q * TOK, :])
        im["msk"] = mb if q == 0 else mi
        in_maps.append(im)

    res = run_bass_kernel_spmd(nc, in_maps, core_ids=list(range(NCORES)))
    out = np.empty((B, S, D), dtype=np.float32)
    for c in range(NCORES):
        b, q = divmod(c, 4)
        out[b, q * TOK:(q + 1) * TOK, :] = res.results[c]["out"]
    return out


# version 21
# speedup vs baseline: 1.0203x; 1.0203x over previous
import numpy as np
import ml_dtypes
import concourse.bass as bass
import concourse.mybir as mybir
from concourse.bass_utils import run_bass_kernel_spmd
from contextlib import ExitStack

F32 = mybir.dt.float32
BF16 = mybir.dt.bfloat16
I32 = mybir.dt.int32
AF = mybir.ActivationFunctionType
ALU = mybir.AluOpType
AX = mybir.AxisListType
bfdt = ml_dtypes.bfloat16

B, S, D = 2, 4096, 2048
NCORES = 8
TOK = 1024
HALO = 512
TT = 8
TH = 4
TA = TT + TH
DC = 16
HQ, HKV, HD = 16, 4, 128
WIN = 640
H = 8192
HHALF = 4096
EPS_LN = 1e-5
EPS_RMS = 1.1920929e-07
THR = 0.5 + 2.0 ** -9
NEG = -1e30

ENGINES = ("sync", "vector", "scalar", "tensor", "gpsimd")


class P:

    NDMA = 4

    def __init__(self):
        self.ops = []
        self.epos = {e: [] for e in ENGINES}
        self.lastw = {}
        self.lastr = {}
        self.dmai = {e: 0 for e in ("sync", "scalar", "gpsimd")}
        self.dma_last = {}

    def op(self, eng, fn, reads=(), writes=(), dma=False):
        deps = {}
        myid = len(self.ops)

        def add(j):
            if j is not None and not (self.ops[j][0] == "tensor" and eng == "tensor"
                                      and not self.ops[j][3]):
                deps[j] = True

        for k in reads:
            add(self.lastw.get(k))
        for k in writes:
            add(self.lastw.get(k))
            for j in self.lastr.get(k, ()):
                add(j)
        if dma:
            sem = f"dq_{eng}_{self.dmai[eng] % self.NDMA}"
            self.dmai[eng] += 1
            add(self.dma_last.get(sem))
            self.dma_last[sem] = myid
        else:
            sem = None
        self.ops.append([eng, fn, deps, dma, sem])
        self.epos[eng].append(myid)
        for k in writes:
            self.lastw[k] = myid
            self.lastr.pop(k, None)
        for k in reads:
            self.lastr.setdefault(k, []).append(myid)
        return myid

    def barrier(self):
        tails = [self.epos[e][-1] for e in ENGINES if self.epos[e]]
        tails += [j for j in self.dma_last.values() if j is not None]
        for e in ENGINES:
            myid = len(self.ops)
            deps = {j: True for j in tails}
            self.ops.append([e, lambda eng: eng.nop(), deps, False, None])
            self.epos[e].append(myid)
        self.lastw.clear()
        self.lastr.clear()
        for k in list(self.dma_last):
            self.dma_last[k] = None

    def emit(self, block, sems_factory):
        n = len(self.ops)
        dom = [op[4] if op[4] else op[0] for op in self.ops]
        dpos = {}
        dcount = {}
        for j in range(n):
            k = dom[j]
            dpos[j] = dcount.get(k, 0)
            dcount[k] = dpos[j] + 1
        dops = {}
        for j in range(n):
            dops.setdefault(dom[j], []).append(j)
        clocks = [None] * n
        needs_inc = [op[3] for op in self.ops]
        waits_out = [None] * n
        implied = {e: {} for e in ENGINES}
        for j, (eng, fn, deps, dma, dsem) in enumerate(self.ops):
            imp = implied[eng]
            tgt = {}
            for d in deps:
                dk = dom[d]
                if imp.get(dk, -1) >= dpos[d]:
                    continue
                if dpos[d] > tgt.get(dk, -1):
                    tgt[dk] = dpos[d]
            w = []
            for dk, dp in sorted(tgt.items(), key=lambda x: str(x[0])):
                d = dops[dk][dp]
                needs_inc[d] = True
                w.append(d)
                dc = clocks[d]
                for k2, v2 in dc.items():
                    if v2 > imp.get(k2, -1):
                        imp[k2] = v2
                if dp > imp.get(dk, -1):
                    imp[dk] = dp
            waits_out[j] = w
            c = dict(imp)
            k = dom[j]
            if dpos[j] > c.get(k, -1):
                c[k] = dpos[j]
            clocks[j] = c
        semval = [0] * n
        cnt = {}
        for j in range(n):
            if needs_inc[j]:
                k = dom[j]
                cnt[k] = cnt.get(k, 0) + (16 if self.ops[j][3] else 1)
                semval[j] = cnt[k]
        sems = sems_factory(sorted(cnt.keys()))
        for name in ENGINES:
            eops = self.epos[name]

            def body(eng, eops=eops, name=name):
                cur = {}
                for j in eops:
                    _e, fn, deps, dma, dsem = self.ops[j]
                    for d in waits_out[j]:
                        key = dom[d]
                        v = semval[d]
                        if cur.get(key, 0) < v:
                            eng.wait_ge(sems[key], v)
                            cur[key] = v
                    instr = fn(eng)
                    if instr is not None and needs_inc[j]:
                        key = dom[j]
                        instr.then_inc(sems[key], 16 if dma else 1)
            getattr(block, name)(body)


def build(taps=()):
    nc = bass.Bass("TRN2", target_bir_lowering=False, debug=False)

    x_d = nc.dram_tensor("x", [TOK, D], F32, kind="ExternalInput").ap()
    xh_d = nc.dram_tensor("xh", [HALO, D], F32, kind="ExternalInput").ap()
    wq_d = nc.dram_tensor("wq", [D, D], F32, kind="ExternalInput").ap()
    wk_d = nc.dram_tensor("wk", [D, HKV * HD], F32, kind="ExternalInput").ap()
    wv_d = nc.dram_tensor("wv", [D, HKV * HD], F32, kind="ExternalInput").ap()
    wo_d = nc.dram_tensor("wo", [D, D], F32, kind="ExternalInput").ap()
    wgu_d = nc.dram_tensor("wgu", [2 * H, D], F32, kind="ExternalInput").ap()
    wp_d = nc.dram_tensor("wp", [D, H], F32, kind="ExternalInput").ap()
    msk_d = nc.dram_tensor("msk", [5 * 128, WIN], BF16, kind="ExternalInput").ap()
    idn_d = nc.dram_tensor("idn", [128, 128], BF16, kind="ExternalInput").ap()
    out_d = nc.dram_tensor("out", [TOK, D], F32, kind="ExternalOutput").ap()
    x2_d = nc.dram_tensor("x2buf", [TOK, D], F32)
    pp_d = nc.dram_tensor("ppart", [D, TOK], F32)

    tap_d = {}
    for tname, tshape, tdt in taps:
        tap_d[tname] = nc.dram_tensor("tap_" + tname, list(tshape), tdt, kind="ExternalOutput").ap()

    p = P()
    globals()["_P_DEBUG"] = p

    with ExitStack() as ctx:
        e = ctx.enter_context

        def sems_factory(names):
            return {sn: ctx.enter_context(nc.semaphore("sem_" + sn)) for sn in names}

        idn = e(nc.sbuf_tensor("idn_sb", [128, 128], BF16))
        msk = e(nc.sbuf_tensor("msk_sb", [128, 5 * WIN], BF16))
        st = e(nc.sbuf_tensor("stats", [128, 48], F32))
        st2 = e(nc.sbuf_tensor("stats2", [128, 48], F32))
        sxp = e(nc.sbuf_tensor("sxp", [128, 12], F32))
        disc = e(nc.sbuf_tensor("disc", [128, 2048], F32))
        ones = e(nc.sbuf_tensor("ones_bf", [128, 2], BF16))
        onesr = e(nc.sbuf_tensor("onesr", [1, 128], F32))
        idnf = e(nc.sbuf_tensor("idnf_sb", [128, 128], F32))
        negm3 = e(nc.sbuf_tensor("negm3", [1, TOK], BF16))
        rmsr = e(nc.sbuf_tensor("rmsr", [128, 8], F32))
        ssq_h = e(nc.sbuf_tensor("ssq_h", [128, 16], F32))
        qr = e(nc.sbuf_tensor("qrow", [128, 64], F32))
        qs = e(nc.sbuf_tensor("qsc", [128, 64], BF16))
        qt = e(nc.sbuf_tensor("qthr", [128, 64], F32))

        RA = e(nc.sbuf_tensor("RA", [128, 16384], F32))
        RB = e(nc.sbuf_tensor("RB", [128, 24576], BF16))
        RC = e(nc.sbuf_tensor("RC", [128, 8192], F32))
        RD = e(nc.sbuf_tensor("RD", [128, 4096], F32))
        RE = e(nc.sbuf_tensor("RE", [128, 4096], F32))
        RF = e(nc.sbuf_tensor("RF", [128, 4096], BF16))
        RG = e(nc.sbuf_tensor("RG", [128, 512], F32))
        RI = e(nc.sbuf_tensor("RI", [128, 1536], BF16))

        RAb = RA[:, :].bitcast(BF16)
        RCb = RC[:, :].bitcast(BF16)
        RDb = RD[:, :].bitcast(BF16)
        REb = RE[:, :].bitcast(BF16)
        RBf = RB[:, :].bitcast(F32)
        RFf = RF[:, :].bitcast(F32)

        p.op("sync", lambda eng: eng.dma_start(out=idn[:, :], in_=idn_d), writes=["idn"], dma=True)
        p.op("sync", lambda eng: eng.dma_start(
            out=msk[:, :], in_=msk_d.rearrange("(t p) w -> p t w", p=128)), writes=["msk"], dma=True)
        wringE = [RI[:, i * 512:(i + 1) * 512] for i in range(2)]
        wqe_r = wq_d.rearrange("(c p) f -> p c f", p=128)
        for c0 in range(2):
            p.op("gpsimd", lambda eng, src=wqe_r[:, c0, 0:512], c0=c0: eng.dma_start(
                out=wringE[c0], in_=src), writes=[("wringE", c0)], dma=True)
        p.op("vector", lambda eng: eng.memset(ones[:, :], 1.0), writes=["ones"])
        p.op("vector", lambda eng: eng.memset(onesr[0:1, :], 1.0), writes=["onesr"])
        p.op("vector", lambda eng: eng.tensor_copy(out=idnf[:, :], in_=idn[:, :]),
             reads=["idn"], writes=["idnf"])

        def ln_stats(n, src, eps):
            for i in range(n):
                ap, rkeys = src(i)
                p.op("vector", lambda eng, ap=ap, i=i: eng.reduce_sum(
                    out=st[:, i:i + 1], in_=ap, axis=AX.X), reads=rkeys, writes=["stA"])
                p.op("scalar", lambda eng, ap=ap, i=i: eng.activation(
                    out=disc[:, 0:ap.shape[-1]] if False else disc[:, :], in_=ap, func=AF.Square,
                    accum_out=st[:, 16 + i:17 + i]),
                    reads=rkeys, writes=["disc", "stB"])
            p.op("vector", lambda eng, n=n: eng.tensor_scalar_mul(
                out=st[:, 32:32 + n], in0=st[:, 0:n], scalar1=1.0 / D),
                reads=["stA"], writes=["stC"])
            p.op("vector", lambda eng, n=n: eng.tensor_scalar_mul(
                out=st2[:, 0:n], in0=st[:, 16:16 + n], scalar1=1.0 / D),
                reads=["stB"], writes=["stD"])
            p.op("vector", lambda eng, n=n: eng.tensor_tensor(
                out=st2[:, 16:16 + n], in0=st[:, 32:32 + n], in1=st[:, 32:32 + n], op=ALU.mult),
                reads=["stC"], writes=["stE"])
            p.op("vector", lambda eng, n=n: eng.tensor_tensor(
                out=st2[:, 0:n], in0=st2[:, 0:n], in1=st2[:, 16:16 + n], op=ALU.subtract),
                reads=["stD", "stE"], writes=["stD"])
            p.op("vector", lambda eng, n=n, eps=eps: eng.tensor_scalar_add(
                out=st2[:, 0:n], in0=st2[:, 0:n], scalar1=eps),
                reads=["stD"], writes=["stD"])
            p.op("scalar", lambda eng, n=n: eng.activation(
                out=st2[:, 16:16 + n], in_=st2[:, 0:n], func=AF.Ln),
                reads=["stD"], writes=["stE"])
            p.op("scalar", lambda eng, n=n: eng.activation(
                out=st2[:, 32:32 + n], in_=st2[:, 16:16 + n], func=AF.Exp, scale=-0.5),
                reads=["stE"], writes=["stF"])

        x_sb = RA[:, 0:TT * D]
        x_hal = RC[:, :]
        lnT = RB[:, :]
        lnst = RF[:, :]
        sc_trp = ExitStack()
        trp = [sc_trp.enter_context(nc.psum_tensor(f"trp{i}", [128, 1024], BF16)) for i in range(2)]

        def xt(t):
            if t < TT:
                return x_sb[:, t * D:(t + 1) * D]
            return x_hal[:, (t - TT) * D:(t - TT + 1) * D]

        lnT3 = lnT.rearrange("p (c n) -> p c n", n=TOK + HALO)

        for t in range(TT):
            p.op("sync", lambda eng, t=t: eng.dma_start(out=xt(t), in_=x_d[t * 128:(t + 1) * 128, :]),
                 writes=[("x", t)], dma=True)
        for j in range(TH):
            p.op("sync", lambda eng, j=j: eng.dma_start(out=xt(TT + j), in_=xh_d[j * 128:(j + 1) * 128, :]),
                 writes=[("x", TT + j)], dma=True)

        ln_stats(TA, lambda t: (xt(t), [("x", t)]), EPS_LN)

        lnT_w = []
        for t in range(TA):
            sl = lnst[:, (t % 2) * D:(t % 2 + 1) * D]
            p.op("vector", lambda eng, t=t, sl=sl: eng.tensor_scalar(
                out=sl, in0=xt(t), scalar1=st[:, 32 + t:33 + t], scalar2=st2[:, 32 + t:33 + t],
                op0=ALU.subtract, op1=ALU.mult),
                reads=[("x", t), "stC", "stF"], writes=[("lnstage", t % 2)])
            tokcol = (HALO + t * 128) if t < TT else (t - TT) * 128
            for half in range(2):
                ps = trp[half]
                for j in range(8):
                    c = half * 8 + j
                    p.op("tensor", lambda eng, sl=sl, c=c, j=j, ps=ps: eng.transpose(
                        ps[:, j * 128:(j + 1) * 128], in_=sl[:, c * 128:(c + 1) * 128], identity=idn[:, :]),
                        reads=[("lnstage", t % 2), "idn", ("trpf", half)], writes=[("trp", half)])
                dst = lnT3[:, half * 8:half * 8 + 8, tokcol:tokcol + 128]
                wkey = ("lnT", t, half)
                lnT_w.append(wkey)
                p.op("scalar", lambda eng, ps=ps, dst=dst: eng.activation(out=dst, in_=ps[:, :], func=AF.Copy),
                     reads=[("trp", half)], writes=[wkey, ("trpf", half)])

        if "ln1T" in tap_d:
            p.op("sync", lambda eng: eng.dma_start(out=tap_d["ln1T"], in_=lnT), reads=lnT_w, dma=True)

        p.barrier()

        q_fm = RCb[:, 0:HQ * TOK]
        k_fm = RDb[:, 0:HKV * (TOK + HALO)]
        v_tm = REb[:, 0:TA * (HKV * 128)]
        wring = [RFf[:, i * 256:(i + 1) * 256].bitcast(BF16) for i in range(4)]
        v_fm = RI[:, 0:TOK + HALO]
        sc_trp.close()

        sc_q = ExitStack()
        qps = [sc_q.enter_context(nc.psum_tensor(f"qps{i}", [128, 1024], F32)) for i in range(4)]
        wq_r = wq_d.rearrange("(c p) f -> p c f", p=128)
        for hg in range(4):
            for c in range(DC):
                if hg == 0 and c < 2:
                    wap, wkey = wringE[c], ("wringE", c)
                else:
                    slot = c % 4
                    src = wq_r[:, c, hg * 512:(hg + 1) * 512]
                    p.op("gpsimd", lambda eng, src=src, slot=slot: eng.dma_start(
                        out=wring[slot], in_=src), writes=[("wring", slot)], dma=True)
                    wap, wkey = wring[slot], ("wring", slot)
                for hh in range(4):
                    for m in range(2):
                        p.op("tensor", lambda eng, wap=wap, c=c, hh=hh, m=m: eng.matmul(
                            qps[hh][:, m * 512:(m + 1) * 512],
                            lhsT=wap[:, hh * 128:(hh + 1) * 128],
                            rhs=lnT3[:, c, HALO + m * 512:HALO + (m + 1) * 512],
                            start=(c == 0), stop=(c == DC - 1)),
                            reads=[wkey]
                            + ([("qpf", hh)] if c == 0 and m == 0 else []),
                            writes=[("qp", hh)])
            for hh in range(4):
                h = hg * 4 + hh
                p.op("scalar", lambda eng, hh=hh, h=h: eng.activation(
                    out=q_fm[:, h * TOK:(h + 1) * TOK], in_=qps[hh][:, :],
                    func=AF.Copy, scale=1.0 / 128),
                    reads=[("qp", hh)], writes=[("q", h), ("qpf", hh)])
        p.barrier()
        sc_q.close()

        sc_kv = ExitStack()
        kvps = [sc_kv.enter_context(nc.psum_tensor(f"kvps{i}", [128, 1536], F32)) for i in range(2)]
        trp2 = [sc_kv.enter_context(nc.psum_tensor(f"trpv{i}", [128, 1024], BF16)) for i in range(2)]
        wk_r = wk_d.rearrange("(c p) f -> p c f", p=128)
        wv_r = wv_d.rearrange("(c p) f -> p c f", p=128)
        for kvg in range(2):
            for vv, wsrc in ((0, wk_r), (1, wv_r)):
                for c in range(DC):
                    slot = c % 4
                    src = wsrc[:, c, kvg * 256:(kvg + 1) * 256]
                    p.op("gpsimd", lambda eng, src=src, slot=slot: eng.dma_start(
                        out=wring[slot][:, 0:256], in_=src), writes=[("wring", slot)], dma=True)
                    for j in range(2):
                        for m in range(3):
                            p.op("tensor", lambda eng, slot=slot, c=c, j=j, m=m: eng.matmul(
                                kvps[j][:, m * 512:(m + 1) * 512],
                                lhsT=wring[slot][:, j * 128:(j + 1) * 128],
                                rhs=lnT3[:, c, m * 512:(m + 1) * 512],
                                start=(c == 0), stop=(c == DC - 1)),
                                reads=[("wring", slot)]
                                + ([("kpf", j)] if c == 0 and m == 0 else []),
                                writes=[("kp", j)])
                for j in range(2):
                    kv = kvg * 2 + j
                    if vv == 0:
                        b = kv * (TOK + HALO)
                        p.op("scalar", lambda eng, j=j, b=b: eng.activation(
                            out=k_fm[:, b:b + 1536], in_=kvps[j][:, :], func=AF.Copy),
                            reads=[("kp", j)], writes=[("k", kv), ("kpf", j)])
                    else:
                        p.op("scalar", lambda eng, j=j: eng.activation(
                            out=v_fm[:, :], in_=kvps[j][:, :], func=AF.Copy),
                            reads=[("kp", j), ("vff",)], writes=[("vf",), ("kpf", j)])
                        for t in range(TA):
                            half = t % 2
                            ps = trp2[half]
                            p.op("tensor", lambda eng, t=t, ps=ps: eng.transpose(
                                ps[:, 0:128], in_=v_fm[:, t * 128:(t + 1) * 128], identity=idn[:, :]),
                                reads=[("vf",), "idn", ("trpf2", half)], writes=[("trp2", half)])
                            p.op("scalar", lambda eng, t=t, kv=kv, ps=ps: eng.activation(
                                out=v_tm[:, t * (HKV * 128) + kv * 128: t * (HKV * 128) + (kv + 1) * 128],
                                in_=ps[:, 0:128], func=AF.Copy),
                                reads=[("trp2", half)],
                                writes=[("v", kv, t), ("trpf2", half)] + ([("vff",)] if t == TA - 1 else []))

        if "qfm" in tap_d:
            p.op("sync", lambda eng: eng.dma_start(out=tap_d["qfm"], in_=q_fm),
                 reads=[("q", h) for h in range(HQ)], dma=True)
        if "kfm" in tap_d:
            p.op("sync", lambda eng: eng.dma_start(out=tap_d["kfm"], in_=k_fm),
                 reads=[("k", kv) for kv in range(HKV)], dma=True)
        if "vtm" in tap_d:
            p.op("sync", lambda eng: eng.dma_start(out=tap_d["vtm"], in_=v_tm),
                 reads=[("v", kv, j) for kv in range(HKV) for j in range(TA)], dma=True)

        p.barrier()
        sc_kv.close()

        att_fm = RB[:, 0:HQ * TOK]
        pf = [RFf[:, i * WIN:(i + 1) * WIN] for i in range(2)]
        pb = [RF[:, 2560 + i * WIN: 2560 + (i + 1) * WIN] for i in range(2)]
        pts = [REb[:, 6144 + i * WIN: 6144 + (i + 1) * WIN] for i in range(2)]
        sc_a2 = ExitStack()
        sps = [sc_a2.enter_context(nc.psum_tensor(f"sps{i}", [128, 1024], F32)) for i in range(2)]
        ptp = [sc_a2.enter_context(nc.psum_tensor(f"ptp{i}", [128, WIN], BF16)) for i in range(2)]
        aps = [sc_a2.enter_context(nc.psum_tensor(f"aps{i}", [128, 128], F32)) for i in range(2)]

        mskv = msk[:, :].rearrange("p (t w) -> p t w", w=WIN)
        att_w = [("att", h, t) for h in range(HQ) for t in range(TT)]
        pairs = [(h, t) for h in range(HQ) for t in range(TT)]

        whalf = [RB[:, 16384:20480], RB[:, 20480:24576]]
        wo_r = wo_d.rearrange("(c p) f -> p c f", p=128)

        def wo_load(dsl, hf):
            src = wo_r[:, hf * 8:(hf + 1) * 8, dsl * 512:(dsl + 1) * 512]
            p.op("gpsimd", lambda eng, src=src, hf=hf: eng.dma_start(
                out=whalf[hf].rearrange("p (c f) -> p c f", f=512), in_=src),
                writes=[("wor", dsl, hf)] + ([("wor", dsl - 1, hf)] if dsl > 0 else []),
                dma=True)

        wo_load(0, 0)
        wo_load(0, 1)
        waves = [pairs[b0:b0 + 2] for b0 in range(0, len(pairs), 2)]

        def b2_scores(wv):
            for r, (h, t) in enumerate(wv):
                kv = h // 4
                kbase = kv * (TOK + HALO) + t * 128
                qsl = q_fm[:, h * TOK + t * 128: h * TOK + (t + 1) * 128]
                ps = sps[r]
                p.op("tensor", lambda eng, ps=ps, qsl=qsl, kbase=kbase: eng.matmul(
                    ps[:, 0:512], lhsT=qsl, rhs=k_fm[:, kbase:kbase + 512], start=True, stop=True),
                    reads=[("spsf", r)], writes=[("sps", r)])
                p.op("tensor", lambda eng, ps=ps, qsl=qsl, kbase=kbase: eng.matmul(
                    ps[:, 512:640], lhsT=qsl, rhs=k_fm[:, kbase + 512:kbase + 640], start=True, stop=True),
                    reads=[("spsf", r)], writes=[("sps", r)])

        b2_scores(waves[0])
        for wi, wv in enumerate(waves):
            for r, (h, t) in enumerate(wv):
                p.op("vector", lambda eng, ps=sps[r], t=t: eng.tensor_tensor(
                    out=ps[:, 0:WIN], in0=ps[:, 0:WIN], in1=mskv[:, min(t, 4), :], op=ALU.add),
                    reads=[("sps", r), "msk"], writes=[("sps", r)])
            for r, (h, t) in enumerate(wv):
                p.op("scalar", lambda eng, ps=sps[r], pfr=pf[r], r=r: eng.activation(
                    out=pfr, in_=ps[:, 0:WIN], func=AF.Exp,
                    accum_out=sxp[:, 3 + r:4 + r]),
                    reads=[("sps", r)], writes=[("pf", r), ("sume", r), ("spsf", r)])
            if wi + 1 < len(waves):
                b2_scores(waves[wi + 1])
            for r, (h, t) in enumerate(wv):
                p.op("vector", lambda eng, r=r: eng.reciprocal(
                    out=sxp[:, 6 + r:7 + r], in_=sxp[:, 3 + r:4 + r]),
                    reads=[("sume", r)], writes=[("rsum", r)])
                p.op("vector", lambda eng, pfr=pf[r], pbr=pb[r], r=r: eng.tensor_scalar_mul(
                    out=pbr, in0=pfr, scalar1=sxp[:, 6 + r:7 + r]),
                    reads=[("pf", r), ("rsum", r)], writes=[("pb", r)])
            for r, (h, t) in enumerate(wv):
                for cch in range(5):
                    p.op("tensor", lambda eng, pbr=pb[r], pp_=ptp[r], cch=cch: eng.transpose(
                        pp_[:, cch * 128:(cch + 1) * 128], in_=pbr[:, cch * 128:(cch + 1) * 128],
                        identity=idn[:, :]),
                        reads=[("pb", r), "idn", ("ptpf", r)], writes=[("ptp", r)])
                if r == 0:
                    p.op("scalar", lambda eng, ptr=pts[r], pp_=ptp[r]: eng.activation(
                        out=ptr, in_=pp_[:, :], func=AF.Copy),
                        reads=[("ptp", r)], writes=[("pts", r), ("ptpf", r)])
                else:
                    p.op("vector", lambda eng, ptr=pts[r], pp_=ptp[r]: eng.tensor_copy(
                        out=ptr, in_=pp_[:, :]),
                        reads=[("ptp", r)], writes=[("pts", r), ("ptpf", r)])
            for r, (h, t) in enumerate(wv):
                kv = h // 4
                for cch in range(5):
                    jj = t + cch
                    vsl = v_tm[:, jj * (HKV * 128) + kv * 128: jj * (HKV * 128) + (kv + 1) * 128]
                    p.op("tensor", lambda eng, ptr=pts[r], ap_=aps[r], vsl=vsl, cch=cch: eng.matmul(
                        ap_[:, :], lhsT=vsl, rhs=ptr[:, cch * 128:(cch + 1) * 128],
                        start=(cch == 0), stop=(cch == 4)),
                        reads=[("pts", r)] + ([("apsf", r)] if cch == 0 else []),
                        writes=[("aps", r)])
                p.op("scalar", lambda eng, h=h, t=t, ap_=aps[r]: eng.activation(
                    out=att_fm[:, h * TOK + t * 128: h * TOK + (t + 1) * 128], in_=ap_[:, :], func=AF.Copy),
                    reads=[("aps", r)], writes=[("att", h, t), ("apsf", r)])

        if "att" in tap_d:
            p.op("sync", lambda eng: eng.dma_start(out=tap_d["att"], in_=att_fm), reads=att_w, dma=True)

        p.barrier()
        sc_a2.close()

        def ano(t):
            if t < 4:
                return RC[:, t * D:(t + 1) * D]
            if t < 6:
                return RD[:, (t - 4) * D:(t - 3) * D]
            return RE[:, (t - 6) * D:(t - 5) * D]

        sc_wo = ExitStack()
        wop = [sc_wo.enter_context(nc.psum_tensor(f"wop{i}", [128, 512], F32)) for i in range(2)]

        gi = 0
        for dsl in range(4):
            for t in range(TT):
                me = gi % 2
                ps = wop[me]
                for c in range(DC):
                    p.op("tensor", lambda eng, ps=ps, c=c, t=t: eng.matmul(
                        ps[:, :], lhsT=att_fm[:, c * TOK + t * 128: c * TOK + (t + 1) * 128],
                        rhs=whalf[c // 8][:, (c % 8) * 512:(c % 8 + 1) * 512],
                        start=(c == 0), stop=(c == DC - 1)),
                        reads=[("wor", dsl, c // 8)] + ([("wopf", me)] if c == 0 else []),
                        writes=[("wop", me)])
                p.op("vector", lambda eng, ps=ps, t=t, dsl=dsl: eng.tensor_copy(
                    out=ano(t)[:, dsl * 512:(dsl + 1) * 512], in_=ps[:, :]),
                    reads=[("wop", me)], writes=[("ao", t), ("wopf", me)])
                gi += 1
            if dsl + 1 < 4:
                wo_load(dsl + 1, 0)
                wo_load(dsl + 1, 1)

        if "attno" in tap_d:
            for t in range(TT):
                p.op("sync", lambda eng, t=t: eng.dma_start(
                    out=tap_d["attno"][t * 128:(t + 1) * 128, :], in_=ano(t)),
                    reads=[("ao", t)], dma=True)

        ln_stats(TT, lambda t: (ano(t), [("ao", t)]), EPS_LN)
        for t in range(TT):
            p.op("vector", lambda eng, t=t: eng.tensor_scalar(
                out=ano(t), in0=ano(t), scalar1=st[:, 32 + t:33 + t], scalar2=st2[:, 32 + t:33 + t],
                op0=ALU.subtract, op1=ALU.mult),
                reads=[("ao", t), "stC", "stF"], writes=[("ao", t)])
            p.op("vector", lambda eng, t=t: eng.tensor_tensor(
                out=xt(t), in0=xt(t), in1=ano(t), op=ALU.add),
                reads=[("x", t), ("ao", t)], writes=[("x", t)])
            p.op("scalar", lambda eng, t=t: eng.dma_start(out=x2_d[t * 128:(t + 1) * 128, :], in_=xt(t)),
                 reads=[("x", t)], writes=[("x2d", t)], dma=True)

        if "x2" in tap_d:
            p.op("sync", lambda eng: eng.dma_start(out=tap_d["x2"], in_=x_sb),
                 reads=[("x", t) for t in range(TT)], dma=True)

        p.barrier()
        sc_wo.close()

        x2T = RCb[:, :]
        x2st = RF[:, :]
        sq_sb = RD[:, 0:1024]
        sq_bf = RDb[:, 0:1024]
        r3row = RD[:, 1024:2048]
        sc_x2s = ExitStack()
        tr2 = [sc_x2s.enter_context(nc.psum_tensor(f"tr2_{i}", [128, 1024], BF16)) for i in range(2)]
        csp = sc_x2s.enter_context(nc.psum_tensor("csp", [128, 1024], F32))
        csq2 = sc_x2s.enter_context(nc.psum_tensor("csq2", [128, 1024], F32))

        x2T3 = x2T.rearrange("p (c n) -> p c n", n=TOK)
        for t in range(TT):
            sl = x2st[:, (t % 2) * D:(t % 2 + 1) * D]
            p.op("vector", lambda eng, t=t, sl=sl: eng.tensor_copy(out=sl, in_=xt(t)),
                 reads=[("x", t)], writes=[("x2st", t % 2)])
            for half in range(2):
                ps = tr2[half]
                for j in range(8):
                    c = half * 8 + j
                    p.op("tensor", lambda eng, sl=sl, c=c, j=j, ps=ps: eng.transpose(
                        ps[:, j * 128:(j + 1) * 128], in_=sl[:, c * 128:(c + 1) * 128], identity=idn[:, :]),
                        reads=[("x2st", t % 2), "idn", ("tr2f", half)], writes=[("tr2", half)])
                dst = x2T3[:, half * 8:half * 8 + 8, t * 128:(t + 1) * 128]
                p.op("scalar", lambda eng, ps=ps, dst=dst: eng.activation(out=dst, in_=ps[:, :], func=AF.Copy),
                     reads=[("tr2", half)], writes=[("x2T", t, half), ("tr2f", half)])

        x2t_r = [("x2T", t, hf) for t in range(TT) for hf in range(2)]
        for c in range(DC):
            for m in range(2):
                p.op("tensor", lambda eng, c=c, m=m: eng.matmul(
                    csp[0:1, m * 512:(m + 1) * 512], lhsT=ones[:, 0:1],
                    rhs=x2T3[:, c, m * 512:(m + 1) * 512],
                    start=(c == 0), stop=(c == DC - 1)),
                    reads=x2t_r + ["ones"], writes=[("csp",)])
        for c in range(DC):
            p.op("scalar", lambda eng, c=c: eng.activation(out=sq_bf, in_=x2T3[:, c, :], func=AF.Square),
                 reads=x2t_r + [("sqf",)], writes=[("sq",)])
            for m in range(2):
                p.op("tensor", lambda eng, c=c, m=m: eng.matmul(
                    csq2[0:1, m * 512:(m + 1) * 512], lhsT=ones[:, 0:1],
                    rhs=sq_bf[:, m * 512:(m + 1) * 512],
                    start=(c == 0), stop=(c == DC - 1)),
                    reads=[("sq",)], writes=[("csq2",), ("sqf",)])
        p.op("vector", lambda eng: eng.tensor_scalar_mul(
            out=negm3[0:1, :], in0=csp[0:1, 0:TOK], scalar1=-1.0 / D),
            reads=[("csp",)], writes=["negm3"])
        p.op("vector", lambda eng: eng.tensor_scalar_mul(
            out=r3row[0:1, 0:TOK], in0=csp[0:1, 0:TOK], scalar1=1.0 / D),
            reads=[("csp",)], writes=[("m3r",)])
        p.op("vector", lambda eng: eng.tensor_tensor(
            out=r3row[0:1, 0:TOK], in0=r3row[0:1, 0:TOK], in1=r3row[0:1, 0:TOK], op=ALU.mult),
            reads=[("m3r",)], writes=[("m3r",)])
        p.op("vector", lambda eng: eng.tensor_scalar(
            out=r3row[1:2, 0:TOK] if False else sq_sb[0:1, 0:TOK], in0=csq2[0:1, 0:TOK],
            scalar1=1.0 / D, scalar2=0.0, op0=ALU.mult, op1=ALU.add),
            reads=[("csq2",), ("sq",)], writes=[("e2r",)])
        p.op("vector", lambda eng: eng.tensor_tensor(
            out=r3row[0:1, 0:TOK], in0=sq_sb[0:1, 0:TOK], in1=r3row[0:1, 0:TOK], op=ALU.subtract),
            reads=[("e2r",), ("m3r",)], writes=[("v3r",)])
        p.op("vector", lambda eng: eng.tensor_scalar_add(
            out=r3row[0:1, 0:TOK], in0=r3row[0:1, 0:TOK], scalar1=EPS_LN),
            reads=[("v3r",)], writes=[("v3r",)])
        p.op("scalar", lambda eng: eng.activation(out=r3row[0:1, 0:TOK], in_=r3row[0:1, 0:TOK], func=AF.Ln),
             reads=[("v3r",)], writes=[("v3r",)])
        p.op("scalar", lambda eng: eng.activation(out=r3row[0:1, 0:TOK], in_=r3row[0:1, 0:TOK],
                                                  func=AF.Exp, scale=-0.5),
             reads=[("v3r",)], writes=[("r3row",)])
        nmr3row = RD[0:1, 2048:2048 + TOK]
        p.op("vector", lambda eng: eng.tensor_tensor(
            out=nmr3row, in0=r3row[0:1, 0:TOK], in1=negm3[0:1, :], op=ALU.mult),
            reads=[("r3row",), "negm3"], writes=[("nmr3r",)])
        r3full_bf = RDb[:, 0:TOK]
        nmr3full_bf = RDb[:, TOK:2 * TOK]
        for m in range(2):
            p.op("tensor", lambda eng, m=m: eng.matmul(
                csp[:, m * 512:(m + 1) * 512], lhsT=onesr[0:1, :],
                rhs=r3row[0:1, m * 512:(m + 1) * 512], start=True, stop=True),
                reads=[("r3row",), "onesr", ("csp",)], writes=[("csp",)])
            p.op("tensor", lambda eng, m=m: eng.matmul(
                csq2[:, m * 512:(m + 1) * 512], lhsT=onesr[0:1, :],
                rhs=nmr3row[0:1, m * 512:(m + 1) * 512], start=True, stop=True),
                reads=[("nmr3r",), "onesr", ("csq2",), ("sq",)], writes=[("csq2",)])
        p.op("vector", lambda eng: eng.tensor_copy(out=r3full_bf, in_=csp[:, 0:TOK]),
             reads=[("csp",)], writes=["r3full", ("sq",), ("e2r",)])
        p.op("vector", lambda eng: eng.tensor_copy(out=nmr3full_bf, in_=csq2[:, 0:TOK]),
             reads=[("csq2",)], writes=["nmr3full", ("e2r",)])
        for c in range(DC):
            p.op("vector", lambda eng, c=c: eng.tensor_tensor(
                out=x2T3[:, c, :], in0=x2T3[:, c, :], in1=r3full_bf, op=ALU.mult),
                reads=["r3full"], writes=[("ln3m", c)] + [("x2T", t, c // 8) for t in range(TT)])
            p.op("vector", lambda eng, c=c: eng.tensor_tensor(
                out=x2T3[:, c, :], in0=x2T3[:, c, :], in1=nmr3full_bf, op=ALU.add),
                reads=[("ln3m", c), "nmr3full"], writes=[("ln3", c)])

        p.barrier()
        sc_x2s.close()

        h_fm = RAb[:, :]
        RING = 4
        wsl = [RB[:, i * 4096:(i + 1) * 4096] for i in range(RING)]
        qsc = []
        for k_ in range(4):
            base = RB[:, 16384 + k_ * 4096: 16384 + (k_ + 1) * 4096] if k_ < 2 \
                else RDb[:, (k_ - 2) * 4096:(k_ - 1) * 4096]
            qsc.append(dict(
                wnat=base[:, 0:1024], qa=base[:, 1024:2048],
                qsg=base[:, 2048:3072], qms=base[:, 3072:4096],
                qr=qr[:, k_ * 16:k_ * 16 + 16], qs=qs[:, k_ * 16:k_ * 16 + 16],
                qt=qt[:, k_ * 16:k_ * 16 + 16]))
        sgf = [REb[:, i * 1024:(i + 1) * 1024] for i in range(2)]
        hsq = REb[:, 4096:5120]
        ssqrow = RE[0:1, 2048:3072]

        h_fm3 = h_fm.rearrange("p (c n) -> p c n", n=TOK)
        qpar = [0]

        def ano2(t):
            if t < 4:
                return RC[:, t * D:(t + 1) * D]
            if t < 6:
                return RBf[:, 8192 + (t - 4) * D: 8192 + (t - 3) * D]
            return RE[:, (t - 6) * D:(t - 5) * D]

        def emit_quant_wave(jobs, sets, sid):
            n = len(jobs)
            rb = sid % RING
            for i in range(n):
                k_, (src, jj, mk) = sets[i], jobs[i]
                p.op("gpsimd", lambda eng, a=src, w=qsc[k_]["wnat"]: eng.dma_start(out=w, in_=a),
                     writes=[("wnat", k_)], dma=True)
            for i in range(n):
                k_ = sets[i]
                p.op("vector", lambda eng, w=qsc[k_]["wnat"], q=qsc[k_]["qr"]: eng.reduce_sum(
                    out=q, in_=w.rearrange("p (g s) -> p g s", s=64),
                    axis=AX.X, apply_absolute_value=True),
                    reads=[("wnat", k_)], writes=[("qr", k_)])
            for i in range(n):
                k_ = sets[i]
                p.op("scalar", lambda eng, w=qsc[k_]["wnat"], q=qsc[k_]["qa"]: eng.activation(
                    out=q, in_=w, func=AF.Abs),
                    reads=[("wnat", k_)], writes=[("qa", k_)])
            for i in range(n):
                k_ = sets[i]
                p.op("scalar", lambda eng, w=qsc[k_]["wnat"], q=qsc[k_]["qsg"]: eng.activation(
                    out=q, in_=w, func=AF.Sign),
                    reads=[("wnat", k_)], writes=[("qsg", k_)])
            for i in range(n):
                k_ = sets[i]
                p.op("vector", lambda eng, a=qsc[k_]["qr"], b=qsc[k_]["qs"]: eng.tensor_scalar(
                    out=b, in0=a, scalar1=1.0 / 64, scalar2=1e-8, op0=ALU.mult, op1=ALU.max),
                    reads=[("qr", k_)], writes=[("qs", k_)])
                p.op("vector", lambda eng, a=qsc[k_]["qs"], b=qsc[k_]["qt"]: eng.tensor_scalar_mul(
                    out=b, in0=a, scalar1=THR),
                    reads=[("qs", k_)], writes=[("qt", k_)])
            for i in range(n):
                k_ = sets[i]
                sc = qsc[k_]
                a3 = sc["qa"].rearrange("p (g s) -> p g s", s=64)
                m3 = sc["qms"].rearrange("p (g s) -> p g s", s=64)
                tb = sc["qt"].unsqueeze(2).broadcast_to((128, 16, 64))
                p.op("vector", lambda eng, a3=a3, m3=m3, tb=tb: eng.tensor_tensor(
                    out=m3, in0=a3, in1=tb, op=ALU.is_gt),
                    reads=[("qa", k_), ("qt", k_)], writes=[("qms", k_)])
            for i in range(n):
                k_ = sets[i]
                sc = qsc[k_]
                m3 = sc["qms"].rearrange("p (g s) -> p g s", s=64)
                sb = sc["qs"].unsqueeze(2).broadcast_to((128, 16, 64))
                p.op("vector", lambda eng, m3=m3, sb=sb: eng.tensor_tensor(
                    out=m3, in0=m3, in1=sb, op=ALU.mult),
                    reads=[("qms", k_), ("qs", k_)], writes=[("qms", k_)])
            for i in range(n):
                k_ = sets[i]
                sc = qsc[k_]
                p.op("gpsimd", lambda eng, a=sc["qa"], m=sc["qms"], g=sc["qsg"]: eng.tensor_tensor(
                    out=a, in0=m, in1=g, op=ALU.mult),
                    reads=[("qms", k_), ("qsg", k_)], writes=[("qa", k_)])
            for i in range(n):
                k_, (src, jj, mk) = sets[i], jobs[i]
                tp = qpar[0] % 2
                qpar[0] += 1
                ps = trcs[tp]
                for j in range(8):
                    p.op("tensor", lambda eng, j=j, ps=ps, q=qsc[k_]["qa"]: eng.transpose(
                        ps[:, j * 128:(j + 1) * 128], in_=q[:, j * 128:(j + 1) * 128],
                        identity=idn[:, :]),
                        reads=[("qa", k_), "idn", ("trcf", tp)], writes=[("trc", tp)])
                out_ap, three_d = mk(rb)
                in_ap = ps[:, :].rearrange("p (j x) -> p j x", x=128) if three_d else ps[:, :]
                wkeys = [("wb", sid, jj)]
                if sid >= RING:
                    wkeys += [("wb", sid - RING, j2) for j2 in range(4)]
                p.op("scalar", lambda eng, out_ap=out_ap, in_ap=in_ap: eng.activation(
                    out=out_ap, in_=in_ap, func=AF.Copy),
                    reads=[("trc", tp), ("qa", k_)], writes=wkeys + [("trcf", tp)])

        qjobs = []
        for hh in range(2):
            for pair in range(16):
                for gu, base in ((0, 0), (1, H)):
                    row0 = base + hh * HHALF + pair * 256
                    for jj, (sub, colh) in enumerate(
                            [(s, c) for s in range(2) for c in range(2)]):
                        def mk_gu(rb, colh=colh, sub=sub):
                            o = wsl[rb].rearrange("p (c x) -> p c x", x=256)
                            return (o[:, colh * 8:(colh + 1) * 8,
                                      sub * 128:(sub + 1) * 128], True)
                        qjobs.append((wgu_d[row0 + sub * 128: row0 + (sub + 1) * 128,
                                            colh * 1024:(colh + 1) * 1024], jj, mk_gu))
            for dc in range(DC):
                for pcl in range(4):
                    def mk_wp(rb, pcl=pcl):
                        return (wsl[rb][:, pcl * 1024:(pcl + 1) * 1024], False)
                    qjobs.append((wp_d[dc * 128:(dc + 1) * 128,
                                       hh * HHALF + pcl * 1024: hh * HHALF + (pcl + 1) * 1024],
                                  pcl, mk_wp))
        NSLICE = len(qjobs) // 4
        qnext = [0]
        qdma_done = set()
        SETS4 = [0, 1, 2, 3]

        def emit_qdma_g(jobs, sets):
            for i, (src, jj, mk) in enumerate(jobs):
                k_ = sets[i]
                p.op("gpsimd", lambda eng, a=src, w=qsc[k_]["wnat"]: eng.dma_start(out=w, in_=a),
                     writes=[("wnat", k_)], dma=True)

        def emit_qdma(sid):
            emit_qdma_g(qjobs[sid * 4:(sid + 1) * 4], SETS4)
            qdma_done.add(sid)

        def emit_qRED_g(jobs, sets):
            for i in range(len(jobs)):
                k_ = sets[i]
                p.op("vector", lambda eng, w=qsc[k_]["wnat"], q=qsc[k_]["qr"]: eng.reduce_sum(
                    out=q, in_=w.rearrange("p (g s) -> p g s", s=64),
                    axis=AX.X, apply_absolute_value=True),
                    reads=[("wnat", k_)], writes=[("qr", k_)])

        def emit_qTS_g(jobs, sets):
            for i in range(len(jobs)):
                k_ = sets[i]
                p.op("vector", lambda eng, a=qsc[k_]["qr"], b=qsc[k_]["qs"]: eng.tensor_scalar(
                    out=b, in0=a, scalar1=1.0 / 64, scalar2=1e-8, op0=ALU.mult, op1=ALU.max),
                    reads=[("qr", k_)], writes=[("qs", k_)])
                p.op("vector", lambda eng, a=qsc[k_]["qs"], b=qsc[k_]["qt"]: eng.tensor_scalar_mul(
                    out=b, in0=a, scalar1=THR),
                    reads=[("qs", k_)], writes=[("qt", k_)])

        def emit_qA1_g(jobs, sets):
            emit_qRED_g(jobs, sets)
            emit_qTS_g(jobs, sets)

        def emit_qA2_g(jobs, sets):
            n = len(jobs)
            for i in range(n):
                k_ = sets[i]
                p.op("scalar", lambda eng, w=qsc[k_]["wnat"], q=qsc[k_]["qa"]: eng.activation(
                    out=q, in_=w, func=AF.Abs),
                    reads=[("wnat", k_)], writes=[("qa", k_)])
            for i in range(n):
                k_ = sets[i]
                p.op("scalar", lambda eng, w=qsc[k_]["wnat"], q=qsc[k_]["qsg"]: eng.activation(
                    out=q, in_=w, func=AF.Sign),
                    reads=[("wnat", k_)], writes=[("qsg", k_)])

        def emit_qA(sid):
            jobs = qjobs[sid * 4:(sid + 1) * 4]
            emit_qA1_g(jobs, SETS4)
            emit_qA2_g(jobs, SETS4)

        def emit_qB1_g(jobs, sets):
            n = len(jobs)
            for i in range(n):
                k_ = sets[i]
                sc = qsc[k_]
                a3 = sc["qa"].rearrange("p (g s) -> p g s", s=64)
                m3 = sc["qms"].rearrange("p (g s) -> p g s", s=64)
                tb = sc["qt"].unsqueeze(2).broadcast_to((128, 16, 64))
                p.op("vector", lambda eng, a3=a3, m3=m3, tb=tb: eng.tensor_tensor(
                    out=m3, in0=a3, in1=tb, op=ALU.is_gt),
                    reads=[("qa", k_), ("qt", k_)], writes=[("qms", k_)])
            for i in range(n):
                k_ = sets[i]
                sc = qsc[k_]
                m3 = sc["qms"].rearrange("p (g s) -> p g s", s=64)
                sb = sc["qs"].unsqueeze(2).broadcast_to((128, 16, 64))
                p.op("vector", lambda eng, m3=m3, sb=sb: eng.tensor_tensor(
                    out=m3, in0=m3, in1=sb, op=ALU.mult),
                    reads=[("qms", k_), ("qs", k_)], writes=[("qms", k_)])
            for i in range(n):
                k_ = sets[i]
                sc = qsc[k_]
                p.op("gpsimd", lambda eng, a=sc["qa"], m=sc["qms"], g=sc["qsg"]: eng.tensor_tensor(
                    out=a, in0=m, in1=g, op=ALU.mult),
                    reads=[("qms", k_), ("qsg", k_)], writes=[("qa", k_)])

        def emit_qB1(sid):
            emit_qB1_g(qjobs[sid * 4:(sid + 1) * 4], SETS4)

        def emit_qB2_g(jobs, sets, sid):
            rb = sid % RING
            for i in range(len(jobs)):
                k_, (src, jj, mk) = sets[i], jobs[i]
                tp = qpar[0] % 2
                qpar[0] += 1
                ps = trcs[tp]
                for j in range(8):
                    p.op("tensor", lambda eng, j=j, ps=ps, q=qsc[k_]["qa"]: eng.transpose(
                        ps[:, j * 128:(j + 1) * 128], in_=q[:, j * 128:(j + 1) * 128],
                        identity=idn[:, :]),
                        reads=[("qa", k_), "idn", ("trcf", tp)], writes=[("trc", tp)])
                out_ap, three_d = mk(rb)
                in_ap = ps[:, :].rearrange("p (j x) -> p j x", x=128) if three_d else ps[:, :]
                wkeys = [("wb", sid, jj)]
                if sid >= RING:
                    wkeys += [("wb", sid - RING, j2) for j2 in range(4)]
                p.op("scalar", lambda eng, out_ap=out_ap, in_ap=in_ap: eng.activation(
                    out=out_ap, in_=in_ap, func=AF.Copy),
                    reads=[("trc", tp), ("qa", k_)], writes=wkeys + [("trcf", tp)])

        def emit_qB2(sid):
            emit_qB2_g(qjobs[sid * 4:(sid + 1) * 4], SETS4, sid)

        RLIM = NSLICE - 12
        qa_pend = [False]

        def pump_half_a():
            if qa_pend[0] or qnext[0] >= min(NSLICE, RLIM):
                return
            sid = qnext[0]
            if sid not in qdma_done:
                emit_qdma(sid)
            emit_qA(sid)
            if sid + 1 < min(NSLICE, RLIM):
                emit_qdma(sid + 1)
            qa_pend[0] = True

        qb1_done = [False]

        def pump_half_b1():
            if qa_pend[0] and not qb1_done[0]:
                emit_qB1(qnext[0])
                qb1_done[0] = True

        def pump_half_b():
            if qa_pend[0]:
                pump_half_b1()
                emit_qB2(qnext[0])
                qnext[0] += 1
                qa_pend[0] = False
                qb1_done[0] = False

        def pump_to(upto):
            while qnext[0] < min(upto, RLIM):
                pump_half_a()
                pump_half_b()

        RSETS = [2, 3]
        NU = (NSLICE - RLIM) * 2
        rseq = [("a1", 0), ("a2", 0), ("b1", 0), ("a1", 1)]
        for u_ in range(1, NU):
            rseq += [("b2", u_ - 1), ("a2", u_), ("b1", u_)]
            if u_ + 1 < NU:
                rseq += [("a1", u_ + 1)]
        rseq += [("b2", NU - 1)]
        rsi = [0]
        rdone = {"b2": 0}

        def runit_jobs(u):
            sid = RLIM + u // 2
            o = sid * 4 + (u % 2) * 2
            return qjobs[o:o + 2], sid

        def rstep():
            if rsi[0] >= len(rseq):
                return
            stg, u = rseq[rsi[0]]
            rsi[0] += 1
            jobs, sid = runit_jobs(u)
            if stg == "a1":
                emit_qdma_g(jobs, RSETS)
            elif stg == "a2":
                emit_qA2_g(jobs, RSETS)
            elif stg == "b1":
                emit_qRED_g(jobs, RSETS)
                emit_qTS_g(jobs, RSETS)
                emit_qB1_g(jobs, RSETS)
            else:
                emit_qB2_g(jobs, RSETS, sid)
                rdone["b2"] += 1

        for hhalf in range(2):
            sc_c1 = ExitStack()
            pps = [sc_c1.enter_context(nc.psum_tensor(f"pps{hhalf}_{i}", [128, 1024], F32)) for i in range(2)]
            trcs = [sc_c1.enter_context(nc.psum_tensor(f"trc{hhalf}_{i}", [128, 1024], BF16)) for i in range(2)]
            sqp = sc_c1.enter_context(nc.psum_tensor(f"sqp{hhalf}", [128, 1024], F32))

            pump_to(hhalf * (NSLICE // 2) + RING)
            gi = 0
            for pair in range(16):
                pump_half_a()
                for gu in range(2):
                    sid = hhalf * (NSLICE // 2) + 2 * pair + gu
                    assert qnext[0] > sid, ("quant not emitted", sid)
                    rb = sid % RING
                    wv = wsl[rb].rearrange("p (c x) -> p c x", x=256)
                    mes = []
                    for hcl in range(2):
                        me = gi % 2
                        mes.append(me)
                        ps = pps[me]
                        for c in range(DC):
                            for m in range(2):
                                p.op("tensor", lambda eng, ps=ps, c=c, m=m, hcl=hcl, wv=wv: eng.matmul(
                                    ps[:, m * 512:(m + 1) * 512],
                                    lhsT=wv[:, c, hcl * 128:(hcl + 1) * 128],
                                    rhs=x2T3[:, c, m * 512:(m + 1) * 512],
                                    start=(c == 0), stop=(c == DC - 1)),
                                    reads=[("wb", sid, hcl * 2 + c // 8)]
                                    + ([("ppsf", me)] if c == 0 and m == 0 else []),
                                    writes=[("pps", me)])
                        gi += 1
                    pump_half_b1()
                    for hcl in range(2):
                        me = mes[hcl]
                        ps = pps[me]
                        if gu == 0:
                            p.op("scalar", lambda eng, ps=ps, hcl=hcl: eng.activation(
                                out=sgf[hcl], in_=ps[:, :], func=AF.Silu),
                                reads=[("pps", me)], writes=[("sg", hcl), ("ppsf", me)])
                        else:
                            ch = pair * 2 + hcl
                            p.op("vector", lambda eng, ps=ps, ch=ch, hcl=hcl: eng.tensor_tensor(
                                out=h_fm3[:, ch, :], in0=sgf[hcl], in1=ps[:, :], op=ALU.mult),
                                reads=[("sg", hcl), ("pps", me)], writes=[("hf", ch), ("ppsf", me)])
                            p.op("scalar", lambda eng, ch=ch: eng.activation(
                                out=hsq, in_=h_fm3[:, ch, :], func=AF.Square),
                                reads=[("hf", ch), ("hsqf",)], writes=[("hsq",)])
                            first = (pair == 0 and hcl == 0)
                            last = (pair == 15 and hcl == 1)
                            for m in range(2):
                                p.op("tensor", lambda eng, m=m, first=first, last=last: eng.matmul(
                                    sqp[0:1, m * 512:(m + 1) * 512], lhsT=ones[:, 0:1],
                                    rhs=hsq[:, m * 512:(m + 1) * 512],
                                    start=first, stop=last),
                                    reads=[("hsq",), "ones"], writes=[("sqp",), ("hsqf",)])
                    pump_half_b()
                    if gu == 0:
                        pump_half_a()
                pump_to(hhalf * (NSLICE // 2) + 2 * pair + 2 + RING)

            p.op("vector", lambda eng: eng.tensor_copy(out=ssqrow, in_=sqp[0:1, 0:TOK]),
                 reads=[("sqp",), ("hsq",)], writes=[("ssqr",)])
            for t in range(TT):
                p.op("tensor", lambda eng, t=t: eng.transpose(
                    pps[0][:, 0:128], in_=RE[:, 2048 + t * 128: 2048 + (t + 1) * 128],
                    identity=idnf[:, :]),
                    reads=[("ssqr",), "idnf", ("ppsf", 0)], writes=[("pps", 0)])
                p.op("vector", lambda eng, t=t, hhalf=hhalf: eng.tensor_copy(
                    out=ssq_h[:, t * 2 + hhalf: t * 2 + hhalf + 1], in_=pps[0][:, 0:1]),
                    reads=[("pps", 0)], writes=[("ssqc", hhalf, t), ("ppsf", 0)])

            if "h0" in tap_d and hhalf == 0:
                p.op("sync", lambda eng: eng.dma_start(out=tap_d["h0"], in_=h_fm),
                     reads=[("hf", ch) for ch in range(32)], dma=True)

            p.barrier()
            sc_c1.close()

            if hhalf == 1:
                ssq2 = ssq_h[:, 0:2 * TT].rearrange("p (t u) -> p t u", u=2)
                p.op("vector", lambda eng: eng.tensor_tensor(
                    out=rmsr[:, 0:TT], in0=ssq2[:, :, 0], in1=ssq2[:, :, 1], op=ALU.add),
                    reads=[], writes=["rmsA"])
                p.op("vector", lambda eng: eng.tensor_scalar(
                    out=rmsr[:, 0:TT], in0=rmsr[:, 0:TT], scalar1=1.0 / H, scalar2=EPS_RMS,
                    op0=ALU.mult, op1=ALU.add),
                    reads=["rmsA"], writes=["rmsA"])
                p.op("scalar", lambda eng: eng.activation(out=rmsr[:, 0:TT], in_=rmsr[:, 0:TT], func=AF.Ln),
                     reads=["rmsA"], writes=["rmsA"])
                p.op("scalar", lambda eng: eng.activation(out=rmsr[:, 0:TT], in_=rmsr[:, 0:TT],
                                                          func=AF.Exp, scale=-0.5),
                     reads=["rmsA"], writes=["rmsr"])

            sc_c2 = ExitStack()
            pjp = [sc_c2.enter_context(nc.psum_tensor(f"pjp{hhalf}_{i}", [128, 1024], F32)) for i in range(2)]
            trcs = [sc_c2.enter_context(nc.psum_tensor(f"trd{hhalf}_{i}", [128, 1024], BF16)) for i in range(2)]
            tro = sc_c2.enter_context(nc.psum_tensor(f"tro{hhalf}", [128, 1024], BF16))
            pst = RG[:, 0:512]
            x2r = RFf[:, :]
            ffcv = RG[:, :].bitcast(BF16)

            for dc in range(DC):
                sid = hhalf * (NSLICE // 2) + 32 + dc
                if sid < RLIM:
                    assert qnext[0] > sid, ("wp quant not emitted", sid)
                else:
                    assert rdone["b2"] >= (sid - RLIM + 1) * 2, ("tail unit", sid)
                pump_half_a()
                if hhalf == 1:
                    rstep()
                rb = sid % RING
                me = dc % 2
                ps = pjp[me]
                for m in range(2):
                    for ci in range(32):
                        p.op("tensor", lambda eng, ps=ps, ci=ci, m=m, rb=rb: eng.matmul(
                            ps[:, m * 512:(m + 1) * 512],
                            lhsT=wsl[rb][:, ci * 128:(ci + 1) * 128],
                            rhs=h_fm3[:, ci, m * 512:(m + 1) * 512],
                            start=(ci == 0), stop=(ci == 31)),
                            reads=[("wb", sid, ci // 8)]
                            + ([("pjpf", me)] if ci == 0 and m == 0 else []),
                            writes=[("pjp", me, m)])
                pump_half_b1()
                if hhalf == 1:
                    rstep()
                    rstep()
                if hhalf == 0:
                    for m in range(2):
                        p.op("vector", lambda eng, ps=ps, m=m: eng.tensor_copy(
                            out=pst, in_=ps[:, m * 512:(m + 1) * 512]),
                            reads=[("pjp", me, m), ("pstf",)], writes=[("pst",)])
                        p.op("scalar", lambda eng, dc=dc, m=m: eng.dma_start(
                            out=pp_d[dc * 128:(dc + 1) * 128, m * 512:(m + 1) * 512], in_=pst),
                            reads=[("pst",)], writes=[("ppd", dc, m), ("pstf",)], dma=True)
                    p.op("sync", lambda eng: eng.nop(),
                         reads=[("pjp", me, 0), ("pjp", me, 1)], writes=[("pjpf", me)])
                else:
                    p.op("sync", lambda eng, dc=dc: eng.dma_start(
                        out=x2r[:, 0:1024], in_=pp_d[dc * 128:(dc + 1) * 128, 0:1024]),
                        reads=[("x2rf",)], writes=[("x2r",)], dma=True)
                    p.op("vector", lambda eng, ps=ps: eng.tensor_tensor(
                        out=ffcv, in0=ps[:, :], in1=x2r[:, 0:1024], op=ALU.add),
                        reads=[("pjp", me, 0), ("pjp", me, 1), ("x2r",), ("ffcf",)],
                        writes=[("ffc",), ("pjpf", me), ("x2rf",)])
                    rstep()
                    rstep()
                    for t in range(TT):
                        p.op("tensor", lambda eng, t=t: eng.transpose(
                            tro[:, t * 128:(t + 1) * 128],
                            in_=ffcv[:, t * 128:(t + 1) * 128], identity=idn[:, :]),
                            reads=[("ffc",), "idn"] + ([("trof",)] if t == 0 else []),
                            writes=[("tro",)])
                    for t in range(TT):
                        p.op("vector", lambda eng, t=t, dc=dc: eng.tensor_scalar_mul(
                            out=ano2(t)[:, dc * 128:(dc + 1) * 128], in0=tro[:, t * 128:(t + 1) * 128],
                            scalar1=rmsr[:, t:t + 1]),
                            reads=[("tro",), "rmsr"],
                            writes=[("ffo", t)] + ([("trof",), ("ffcf",)] if t == 7 else []))
                if hhalf == 1:
                    rstep()
                    rstep()
                pump_half_b()
                pump_to(hhalf * (NSLICE // 2) + 32 + dc + 1 + RING)

            if hhalf == 1:
                if "ffo" in tap_d:
                    for t in range(TT):
                        p.op("sync", lambda eng, t=t: eng.dma_start(
                            out=tap_d["ffo"][t * 128:(t + 1) * 128, :], in_=ano2(t)),
                            reads=[("ffo", t)], dma=True)
                ln_stats(TT, lambda t: (ano2(t), [("ffo", t)]), EPS_LN)
                rdx = [RD[:, 0:D], RD[:, D:2 * D]]
                for t in range(TT):
                    xb = rdx[t % 2]
                    wk = [("x2b", t % 2)] + ([("qa", 2), ("qa", 3), ("qms", 2),
                                              ("qms", 3)] if t < 2 else [])
                    p.op("sync", lambda eng, t=t, xb=xb: eng.dma_start(
                        out=xb, in_=x2_d[t * 128:(t + 1) * 128, :]),
                        writes=wk, dma=True)
                    p.op("vector", lambda eng, t=t: eng.tensor_scalar(
                        out=ano2(t), in0=ano2(t), scalar1=st[:, 32 + t:33 + t], scalar2=st2[:, 32 + t:33 + t],
                        op0=ALU.subtract, op1=ALU.mult),
                        reads=[("ffo", t), "stC", "stF"], writes=[("ffo", t)])
                    p.op("vector", lambda eng, t=t, xb=xb: eng.tensor_tensor(
                        out=ano2(t), in0=ano2(t), in1=xb, op=ALU.add),
                        reads=[("ffo", t), ("x2b", t % 2)], writes=[("ffo", t)])
                    p.op("scalar", lambda eng, t=t: eng.dma_start(
                        out=out_d[t * 128:(t + 1) * 128, :], in_=ano2(t)),
                        reads=[("ffo", t)], writes=[("outd", t)], dma=True)

            p.barrier()
            sc_c2.close()


        p.op("gpsimd", lambda eng: eng.nop(), reads=[("outd", t) for t in range(TT)])

        with nc.Block() as block:
            p.emit(block, sems_factory)

    return nc


_NC = {}


def _masks(boundary):
    m = np.zeros((5, 128, WIN), dtype=np.float32)
    i = np.arange(128)[:, None]
    j = np.arange(WIN)[None, :]
    for t in range(5):
        valid = (j >= i + 1) & (j <= i + 512)
        if boundary and t < 4:
            valid = valid & ((128 * t + j - 512) >= 0)
        m[t] = np.where(valid, 0.0, NEG)
    return m.reshape(5 * 128, WIN).astype(bfdt)


def kernel(L0_attn_hidden_states, L0_attn_Wq, L0_attn_Wk, L0_attn_Wv, L0_attn_Wo,
           L0_ffn_w_gate_up, L0_ffn_w_proj):
    hs = np.ascontiguousarray(L0_attn_hidden_states, dtype=np.float32)
    if "prog" not in _NC:
        _NC["prog"] = build()
    nc = _NC["prog"]

    shared = {
        "wq": np.ascontiguousarray(L0_attn_Wq, dtype=np.float32),
        "wk": np.ascontiguousarray(L0_attn_Wk, dtype=np.float32),
        "wv": np.ascontiguousarray(L0_attn_Wv, dtype=np.float32),
        "wo": np.ascontiguousarray(L0_attn_Wo, dtype=np.float32),
        "wgu": np.ascontiguousarray(L0_ffn_w_gate_up, dtype=np.float32),
        "wp": np.ascontiguousarray(L0_ffn_w_proj, dtype=np.float32),
        "idn": np.eye(128, dtype=bfdt),
    }
    mb, mi = _masks(True), _masks(False)
    zh = np.zeros((HALO, D), dtype=np.float32)
    in_maps = []
    for c in range(NCORES):
        b, q = divmod(c, 4)
        im = dict(shared)
        im["x"] = np.ascontiguousarray(hs[b, q * TOK:(q + 1) * TOK, :])
        im["xh"] = zh if q == 0 else np.ascontiguousarray(hs[b, q * TOK - HALO:q * TOK, :])
        im["msk"] = mb if q == 0 else mi
        in_maps.append(im)

    res = run_bass_kernel_spmd(nc, in_maps, core_ids=list(range(NCORES)))
    out = np.empty((B, S, D), dtype=np.float32)
    for c in range(NCORES):
        b, q = divmod(c, 4)
        out[b, q * TOK:(q + 1) * TOK, :] = res.results[c]["out"]
    return out


# version 22 (final)
# speedup vs baseline: 1.0241x; 1.0038x over previous
import numpy as np
import ml_dtypes
import concourse.bass as bass
import concourse.mybir as mybir
from concourse.bass_utils import run_bass_kernel_spmd
from contextlib import ExitStack

F32 = mybir.dt.float32
BF16 = mybir.dt.bfloat16
I32 = mybir.dt.int32
AF = mybir.ActivationFunctionType
ALU = mybir.AluOpType
AX = mybir.AxisListType
bfdt = ml_dtypes.bfloat16

B, S, D = 2, 4096, 2048
NCORES = 8
TOK = 1024
HALO = 512
TT = 8
TH = 4
TA = TT + TH
DC = 16
HQ, HKV, HD = 16, 4, 128
WIN = 640
H = 8192
HHALF = 4096
EPS_LN = 1e-5
EPS_RMS = 1.1920929e-07
THR = 0.5 + 2.0 ** -9
NEG = -1e30

ENGINES = ("sync", "vector", "scalar", "tensor", "gpsimd")


class P:

    NDMA = 4

    def __init__(self):
        self.ops = []
        self.epos = {e: [] for e in ENGINES}
        self.lastw = {}
        self.lastr = {}
        self.dmai = {e: 0 for e in ("sync", "scalar", "gpsimd")}
        self.dma_last = {}

    def op(self, eng, fn, reads=(), writes=(), dma=False):
        deps = {}
        myid = len(self.ops)

        def add(j):
            if j is not None and not (self.ops[j][0] == "tensor" and eng == "tensor"
                                      and not self.ops[j][3]):
                deps[j] = True

        for k in reads:
            add(self.lastw.get(k))
        for k in writes:
            add(self.lastw.get(k))
            for j in self.lastr.get(k, ()):
                add(j)
        if dma:
            sem = f"dq_{eng}_{self.dmai[eng] % self.NDMA}"
            self.dmai[eng] += 1
            add(self.dma_last.get(sem))
            self.dma_last[sem] = myid
        else:
            sem = None
        self.ops.append([eng, fn, deps, dma, sem])
        self.epos[eng].append(myid)
        for k in writes:
            self.lastw[k] = myid
            self.lastr.pop(k, None)
        for k in reads:
            self.lastr.setdefault(k, []).append(myid)
        return myid

    def barrier(self):
        tails = [self.epos[e][-1] for e in ENGINES if self.epos[e]]
        tails += [j for j in self.dma_last.values() if j is not None]
        for e in ENGINES:
            myid = len(self.ops)
            deps = {j: True for j in tails}
            self.ops.append([e, lambda eng: eng.nop(), deps, False, None])
            self.epos[e].append(myid)
        self.lastw.clear()
        self.lastr.clear()
        for k in list(self.dma_last):
            self.dma_last[k] = None

    def emit(self, block, sems_factory):
        n = len(self.ops)
        dom = [op[4] if op[4] else op[0] for op in self.ops]
        dpos = {}
        dcount = {}
        for j in range(n):
            k = dom[j]
            dpos[j] = dcount.get(k, 0)
            dcount[k] = dpos[j] + 1
        dops = {}
        for j in range(n):
            dops.setdefault(dom[j], []).append(j)
        clocks = [None] * n
        needs_inc = [op[3] for op in self.ops]
        waits_out = [None] * n
        implied = {e: {} for e in ENGINES}
        for j, (eng, fn, deps, dma, dsem) in enumerate(self.ops):
            imp = implied[eng]
            tgt = {}
            for d in deps:
                dk = dom[d]
                if imp.get(dk, -1) >= dpos[d]:
                    continue
                if dpos[d] > tgt.get(dk, -1):
                    tgt[dk] = dpos[d]
            w = []
            for dk, dp in sorted(tgt.items(), key=lambda x: str(x[0])):
                d = dops[dk][dp]
                needs_inc[d] = True
                w.append(d)
                dc = clocks[d]
                for k2, v2 in dc.items():
                    if v2 > imp.get(k2, -1):
                        imp[k2] = v2
                if dp > imp.get(dk, -1):
                    imp[dk] = dp
            waits_out[j] = w
            c = dict(imp)
            k = dom[j]
            if dpos[j] > c.get(k, -1):
                c[k] = dpos[j]
            clocks[j] = c
        semval = [0] * n
        cnt = {}
        for j in range(n):
            if needs_inc[j]:
                k = dom[j]
                cnt[k] = cnt.get(k, 0) + (16 if self.ops[j][3] else 1)
                semval[j] = cnt[k]
        sems = sems_factory(sorted(cnt.keys()))
        for name in ENGINES:
            eops = self.epos[name]

            def body(eng, eops=eops, name=name):
                cur = {}
                for j in eops:
                    _e, fn, deps, dma, dsem = self.ops[j]
                    for d in waits_out[j]:
                        key = dom[d]
                        v = semval[d]
                        if cur.get(key, 0) < v:
                            eng.wait_ge(sems[key], v)
                            cur[key] = v
                    instr = fn(eng)
                    if instr is not None and needs_inc[j]:
                        key = dom[j]
                        instr.then_inc(sems[key], 16 if dma else 1)
            getattr(block, name)(body)


def build(taps=()):
    nc = bass.Bass("TRN2", target_bir_lowering=False, debug=False)

    x_d = nc.dram_tensor("x", [TOK, D], F32, kind="ExternalInput").ap()
    xh_d = nc.dram_tensor("xh", [HALO, D], F32, kind="ExternalInput").ap()
    wq_d = nc.dram_tensor("wq", [D, D], F32, kind="ExternalInput").ap()
    wk_d = nc.dram_tensor("wk", [D, HKV * HD], F32, kind="ExternalInput").ap()
    wv_d = nc.dram_tensor("wv", [D, HKV * HD], F32, kind="ExternalInput").ap()
    wo_d = nc.dram_tensor("wo", [D, D], F32, kind="ExternalInput").ap()
    wgu_d = nc.dram_tensor("wgu", [2 * H, D], F32, kind="ExternalInput").ap()
    wp_d = nc.dram_tensor("wp", [D, H], F32, kind="ExternalInput").ap()
    msk_d = nc.dram_tensor("msk", [5 * 128, WIN], BF16, kind="ExternalInput").ap()
    idn_d = nc.dram_tensor("idn", [128, 128], BF16, kind="ExternalInput").ap()
    out_d = nc.dram_tensor("out", [TOK, D], F32, kind="ExternalOutput").ap()
    x2_d = nc.dram_tensor("x2buf", [TOK, D], F32)
    pp_d = nc.dram_tensor("ppart", [D, TOK], F32)

    tap_d = {}
    for tname, tshape, tdt in taps:
        tap_d[tname] = nc.dram_tensor("tap_" + tname, list(tshape), tdt, kind="ExternalOutput").ap()

    p = P()
    globals()["_P_DEBUG"] = p

    with ExitStack() as ctx:
        e = ctx.enter_context

        def sems_factory(names):
            return {sn: ctx.enter_context(nc.semaphore("sem_" + sn)) for sn in names}

        idn = e(nc.sbuf_tensor("idn_sb", [128, 128], BF16))
        msk = e(nc.sbuf_tensor("msk_sb", [128, 5 * WIN], BF16))
        st = e(nc.sbuf_tensor("stats", [128, 48], F32))
        st2 = e(nc.sbuf_tensor("stats2", [128, 48], F32))
        sxp = e(nc.sbuf_tensor("sxp", [128, 12], F32))
        disc = e(nc.sbuf_tensor("disc", [128, 2048], F32))
        ones = e(nc.sbuf_tensor("ones_bf", [128, 2], BF16))
        onesr = e(nc.sbuf_tensor("onesr", [1, 128], F32))
        idnf = e(nc.sbuf_tensor("idnf_sb", [128, 128], F32))
        negm3 = e(nc.sbuf_tensor("negm3", [1, TOK], BF16))
        rmsr = e(nc.sbuf_tensor("rmsr", [128, 8], F32))
        ssq_h = e(nc.sbuf_tensor("ssq_h", [128, 16], F32))
        qr = e(nc.sbuf_tensor("qrow", [128, 64], F32))
        qs = e(nc.sbuf_tensor("qsc", [128, 64], BF16))
        qt = e(nc.sbuf_tensor("qthr", [128, 64], F32))

        RA = e(nc.sbuf_tensor("RA", [128, 16384], F32))
        RB = e(nc.sbuf_tensor("RB", [128, 24576], BF16))
        RC = e(nc.sbuf_tensor("RC", [128, 8192], F32))
        RD = e(nc.sbuf_tensor("RD", [128, 4096], F32))
        RE = e(nc.sbuf_tensor("RE", [128, 4096], F32))
        RF = e(nc.sbuf_tensor("RF", [128, 4096], BF16))
        RG = e(nc.sbuf_tensor("RG", [128, 512], F32))
        RI = e(nc.sbuf_tensor("RI", [128, 1536], BF16))

        RAb = RA[:, :].bitcast(BF16)
        RCb = RC[:, :].bitcast(BF16)
        RDb = RD[:, :].bitcast(BF16)
        REb = RE[:, :].bitcast(BF16)
        RBf = RB[:, :].bitcast(F32)
        RFf = RF[:, :].bitcast(F32)

        p.op("sync", lambda eng: eng.dma_start(out=idn[:, :], in_=idn_d), writes=["idn"], dma=True)
        p.op("sync", lambda eng: eng.dma_start(
            out=msk[:, :], in_=msk_d.rearrange("(t p) w -> p t w", p=128)), writes=["msk"], dma=True)
        wringE = [RI[:, i * 512:(i + 1) * 512] for i in range(2)]
        wqe_r = wq_d.rearrange("(c p) f -> p c f", p=128)
        for c0 in range(2):
            p.op("gpsimd", lambda eng, src=wqe_r[:, c0, 0:512], c0=c0: eng.dma_start(
                out=wringE[c0], in_=src), writes=[("wringE", c0)], dma=True)
        p.op("vector", lambda eng: eng.memset(ones[:, :], 1.0), writes=["ones"])
        p.op("vector", lambda eng: eng.memset(onesr[0:1, :], 1.0), writes=["onesr"])
        p.op("vector", lambda eng: eng.tensor_copy(out=idnf[:, :], in_=idn[:, :]),
             reads=["idn"], writes=["idnf"])

        def ln_stats(n, src, eps):
            for i in range(n):
                ap, rkeys = src(i)
                p.op("vector", lambda eng, ap=ap, i=i: eng.reduce_sum(
                    out=st[:, i:i + 1], in_=ap, axis=AX.X), reads=rkeys, writes=["stA"])
                p.op("scalar", lambda eng, ap=ap, i=i: eng.activation(
                    out=disc[:, 0:ap.shape[-1]] if False else disc[:, :], in_=ap, func=AF.Square,
                    accum_out=st[:, 16 + i:17 + i]),
                    reads=rkeys, writes=["disc", "stB"])
            p.op("vector", lambda eng, n=n: eng.tensor_scalar_mul(
                out=st[:, 32:32 + n], in0=st[:, 0:n], scalar1=1.0 / D),
                reads=["stA"], writes=["stC"])
            p.op("vector", lambda eng, n=n: eng.tensor_scalar_mul(
                out=st2[:, 0:n], in0=st[:, 16:16 + n], scalar1=1.0 / D),
                reads=["stB"], writes=["stD"])
            p.op("vector", lambda eng, n=n: eng.tensor_tensor(
                out=st2[:, 16:16 + n], in0=st[:, 32:32 + n], in1=st[:, 32:32 + n], op=ALU.mult),
                reads=["stC"], writes=["stE"])
            p.op("vector", lambda eng, n=n: eng.tensor_tensor(
                out=st2[:, 0:n], in0=st2[:, 0:n], in1=st2[:, 16:16 + n], op=ALU.subtract),
                reads=["stD", "stE"], writes=["stD"])
            p.op("vector", lambda eng, n=n, eps=eps: eng.tensor_scalar_add(
                out=st2[:, 0:n], in0=st2[:, 0:n], scalar1=eps),
                reads=["stD"], writes=["stD"])
            p.op("scalar", lambda eng, n=n: eng.activation(
                out=st2[:, 16:16 + n], in_=st2[:, 0:n], func=AF.Ln),
                reads=["stD"], writes=["stE"])
            p.op("scalar", lambda eng, n=n: eng.activation(
                out=st2[:, 32:32 + n], in_=st2[:, 16:16 + n], func=AF.Exp, scale=-0.5),
                reads=["stE"], writes=["stF"])

        x_sb = RA[:, 0:TT * D]
        x_hal = RC[:, :]
        lnT = RB[:, :]
        lnst = RF[:, :]
        sc_trp = ExitStack()
        trp = [sc_trp.enter_context(nc.psum_tensor(f"trp{i}", [128, 1024], BF16)) for i in range(2)]

        def xt(t):
            if t < TT:
                return x_sb[:, t * D:(t + 1) * D]
            return x_hal[:, (t - TT) * D:(t - TT + 1) * D]

        lnT3 = lnT.rearrange("p (c n) -> p c n", n=TOK + HALO)

        for t in range(TT):
            p.op("sync", lambda eng, t=t: eng.dma_start(out=xt(t), in_=x_d[t * 128:(t + 1) * 128, :]),
                 writes=[("x", t)], dma=True)
        for j in range(TH):
            p.op("sync", lambda eng, j=j: eng.dma_start(out=xt(TT + j), in_=xh_d[j * 128:(j + 1) * 128, :]),
                 writes=[("x", TT + j)], dma=True)

        ln_stats(TA, lambda t: (xt(t), [("x", t)]), EPS_LN)

        lnT_w = []
        for t in range(TA):
            sl = lnst[:, (t % 2) * D:(t % 2 + 1) * D]
            p.op("vector", lambda eng, t=t, sl=sl: eng.tensor_scalar(
                out=sl, in0=xt(t), scalar1=st[:, 32 + t:33 + t], scalar2=st2[:, 32 + t:33 + t],
                op0=ALU.subtract, op1=ALU.mult),
                reads=[("x", t), "stC", "stF"], writes=[("lnstage", t % 2)])
            tokcol = (HALO + t * 128) if t < TT else (t - TT) * 128
            for half in range(2):
                ps = trp[half]
                for j in range(8):
                    c = half * 8 + j
                    p.op("tensor", lambda eng, sl=sl, c=c, j=j, ps=ps: eng.transpose(
                        ps[:, j * 128:(j + 1) * 128], in_=sl[:, c * 128:(c + 1) * 128], identity=idn[:, :]),
                        reads=[("lnstage", t % 2), "idn", ("trpf", half)], writes=[("trp", half)])
                dst = lnT3[:, half * 8:half * 8 + 8, tokcol:tokcol + 128]
                wkey = ("lnT", t, half)
                lnT_w.append(wkey)
                p.op("scalar", lambda eng, ps=ps, dst=dst: eng.activation(out=dst, in_=ps[:, :], func=AF.Copy),
                     reads=[("trp", half)], writes=[wkey, ("trpf", half)])

        if "ln1T" in tap_d:
            p.op("sync", lambda eng: eng.dma_start(out=tap_d["ln1T"], in_=lnT), reads=lnT_w, dma=True)

        p.barrier()

        q_fm = RCb[:, 0:HQ * TOK]
        k_fm = RDb[:, 0:HKV * (TOK + HALO)]
        v_tm = REb[:, 0:TA * (HKV * 128)]
        wring = [RFf[:, i * 256:(i + 1) * 256].bitcast(BF16) for i in range(4)]
        v_fm = RI[:, 0:TOK + HALO]
        sc_trp.close()

        sc_q = ExitStack()
        qps = [sc_q.enter_context(nc.psum_tensor(f"qps{i}", [128, 1024], F32)) for i in range(4)]
        wq_r = wq_d.rearrange("(c p) f -> p c f", p=128)
        for hg in range(4):
            for c in range(DC):
                if hg == 0 and c < 2:
                    wap, wkey = wringE[c], ("wringE", c)
                else:
                    slot = c % 4
                    src = wq_r[:, c, hg * 512:(hg + 1) * 512]
                    p.op("gpsimd", lambda eng, src=src, slot=slot: eng.dma_start(
                        out=wring[slot], in_=src), writes=[("wring", slot)], dma=True)
                    wap, wkey = wring[slot], ("wring", slot)
                for hh in range(4):
                    for m in range(2):
                        p.op("tensor", lambda eng, wap=wap, c=c, hh=hh, m=m: eng.matmul(
                            qps[hh][:, m * 512:(m + 1) * 512],
                            lhsT=wap[:, hh * 128:(hh + 1) * 128],
                            rhs=lnT3[:, c, HALO + m * 512:HALO + (m + 1) * 512],
                            start=(c == 0), stop=(c == DC - 1)),
                            reads=[wkey]
                            + ([("qpf", hh)] if c == 0 and m == 0 else []),
                            writes=[("qp", hh)])
            for hh in range(4):
                h = hg * 4 + hh
                p.op("scalar", lambda eng, hh=hh, h=h: eng.activation(
                    out=q_fm[:, h * TOK:(h + 1) * TOK], in_=qps[hh][:, :],
                    func=AF.Copy, scale=1.0 / 128),
                    reads=[("qp", hh)], writes=[("q", h), ("qpf", hh)])
        sc_q.close()

        qpf_all = [("qpf", hh) for hh in range(4)]
        sc_kv = ExitStack()
        kvps = [sc_kv.enter_context(nc.psum_tensor(f"kvps{i}", [128, 1536], F32)) for i in range(2)]
        trp2 = [sc_kv.enter_context(nc.psum_tensor(f"trpv{i}", [128, 1024], BF16)) for i in range(2)]
        wk_r = wk_d.rearrange("(c p) f -> p c f", p=128)
        wv_r = wv_d.rearrange("(c p) f -> p c f", p=128)
        for kvg in range(2):
            for vv, wsrc in ((0, wk_r), (1, wv_r)):
                for c in range(DC):
                    slot = c % 4
                    src = wsrc[:, c, kvg * 256:(kvg + 1) * 256]
                    p.op("gpsimd", lambda eng, src=src, slot=slot: eng.dma_start(
                        out=wring[slot][:, 0:256], in_=src), writes=[("wring", slot)], dma=True)
                    for j in range(2):
                        for m in range(3):
                            p.op("tensor", lambda eng, slot=slot, c=c, j=j, m=m: eng.matmul(
                                kvps[j][:, m * 512:(m + 1) * 512],
                                lhsT=wring[slot][:, j * 128:(j + 1) * 128],
                                rhs=lnT3[:, c, m * 512:(m + 1) * 512],
                                start=(c == 0), stop=(c == DC - 1)),
                                reads=[("wring", slot)]
                                + (qpf_all + [("kpf", j)] if c == 0 and m == 0 else []),
                                writes=[("kp", j)])
                for j in range(2):
                    kv = kvg * 2 + j
                    if vv == 0:
                        b = kv * (TOK + HALO)
                        p.op("scalar", lambda eng, j=j, b=b: eng.activation(
                            out=k_fm[:, b:b + 1536], in_=kvps[j][:, :], func=AF.Copy),
                            reads=[("kp", j)], writes=[("k", kv), ("kpf", j)])
                    else:
                        p.op("scalar", lambda eng, j=j: eng.activation(
                            out=v_fm[:, :], in_=kvps[j][:, :], func=AF.Copy),
                            reads=[("kp", j), ("vff",)], writes=[("vf",), ("kpf", j)])
                        for t in range(TA):
                            half = t % 2
                            ps = trp2[half]
                            p.op("tensor", lambda eng, t=t, ps=ps: eng.transpose(
                                ps[:, 0:128], in_=v_fm[:, t * 128:(t + 1) * 128], identity=idn[:, :]),
                                reads=[("vf",), "idn", ("trpf2", half)], writes=[("trp2", half)])
                            p.op("scalar", lambda eng, t=t, kv=kv, ps=ps: eng.activation(
                                out=v_tm[:, t * (HKV * 128) + kv * 128: t * (HKV * 128) + (kv + 1) * 128],
                                in_=ps[:, 0:128], func=AF.Copy),
                                reads=[("trp2", half)],
                                writes=[("v", kv, t), ("trpf2", half)] + ([("vff",)] if t == TA - 1 else []))

        if "qfm" in tap_d:
            p.op("sync", lambda eng: eng.dma_start(out=tap_d["qfm"], in_=q_fm),
                 reads=[("q", h) for h in range(HQ)], dma=True)
        if "kfm" in tap_d:
            p.op("sync", lambda eng: eng.dma_start(out=tap_d["kfm"], in_=k_fm),
                 reads=[("k", kv) for kv in range(HKV)], dma=True)
        if "vtm" in tap_d:
            p.op("sync", lambda eng: eng.dma_start(out=tap_d["vtm"], in_=v_tm),
                 reads=[("v", kv, j) for kv in range(HKV) for j in range(TA)], dma=True)

        p.barrier()
        sc_kv.close()

        att_fm = RB[:, 0:HQ * TOK]
        pf = [RFf[:, i * WIN:(i + 1) * WIN] for i in range(2)]
        pb = [RF[:, 2560 + i * WIN: 2560 + (i + 1) * WIN] for i in range(2)]
        pts = [REb[:, 6144 + i * WIN: 6144 + (i + 1) * WIN] for i in range(2)]
        sc_a2 = ExitStack()
        sps = [sc_a2.enter_context(nc.psum_tensor(f"sps{i}", [128, 1024], F32)) for i in range(2)]
        ptp = [sc_a2.enter_context(nc.psum_tensor(f"ptp{i}", [128, WIN], BF16)) for i in range(2)]
        aps = [sc_a2.enter_context(nc.psum_tensor(f"aps{i}", [128, 128], F32)) for i in range(2)]

        mskv = msk[:, :].rearrange("p (t w) -> p t w", w=WIN)
        att_w = [("att", h, t) for h in range(HQ) for t in range(TT)]
        pairs = [(h, t) for h in range(HQ) for t in range(TT)]

        whalf = [RB[:, 16384:20480], RB[:, 20480:24576]]
        wo_r = wo_d.rearrange("(c p) f -> p c f", p=128)

        def wo_load(dsl, hf):
            src = wo_r[:, hf * 8:(hf + 1) * 8, dsl * 512:(dsl + 1) * 512]
            p.op("gpsimd", lambda eng, src=src, hf=hf: eng.dma_start(
                out=whalf[hf].rearrange("p (c f) -> p c f", f=512), in_=src),
                writes=[("wor", dsl, hf)] + ([("wor", dsl - 1, hf)] if dsl > 0 else []),
                dma=True)

        wo_load(0, 0)
        wo_load(0, 1)
        waves = [pairs[b0:b0 + 2] for b0 in range(0, len(pairs), 2)]

        def b2_scores(wv):
            for r, (h, t) in enumerate(wv):
                kv = h // 4
                kbase = kv * (TOK + HALO) + t * 128
                qsl = q_fm[:, h * TOK + t * 128: h * TOK + (t + 1) * 128]
                ps = sps[r]
                p.op("tensor", lambda eng, ps=ps, qsl=qsl, kbase=kbase: eng.matmul(
                    ps[:, 0:512], lhsT=qsl, rhs=k_fm[:, kbase:kbase + 512], start=True, stop=True),
                    reads=[("spsf", r)], writes=[("sps", r)])
                p.op("tensor", lambda eng, ps=ps, qsl=qsl, kbase=kbase: eng.matmul(
                    ps[:, 512:640], lhsT=qsl, rhs=k_fm[:, kbase + 512:kbase + 640], start=True, stop=True),
                    reads=[("spsf", r)], writes=[("sps", r)])

        b2_scores(waves[0])
        for wi, wv in enumerate(waves):
            for r, (h, t) in enumerate(wv):
                p.op("vector", lambda eng, ps=sps[r], t=t: eng.tensor_tensor(
                    out=ps[:, 0:WIN], in0=ps[:, 0:WIN], in1=mskv[:, min(t, 4), :], op=ALU.add),
                    reads=[("sps", r), "msk"], writes=[("sps", r)])
            for r, (h, t) in enumerate(wv):
                p.op("scalar", lambda eng, ps=sps[r], pfr=pf[r], r=r: eng.activation(
                    out=pfr, in_=ps[:, 0:WIN], func=AF.Exp,
                    accum_out=sxp[:, 3 + r:4 + r]),
                    reads=[("sps", r)], writes=[("pf", r), ("sume", r), ("spsf", r)])
            if wi + 1 < len(waves):
                b2_scores(waves[wi + 1])
            for r, (h, t) in enumerate(wv):
                p.op("vector", lambda eng, r=r: eng.reciprocal(
                    out=sxp[:, 6 + r:7 + r], in_=sxp[:, 3 + r:4 + r]),
                    reads=[("sume", r)], writes=[("rsum", r)])
                p.op("vector", lambda eng, pfr=pf[r], pbr=pb[r], r=r: eng.tensor_scalar_mul(
                    out=pbr, in0=pfr, scalar1=sxp[:, 6 + r:7 + r]),
                    reads=[("pf", r), ("rsum", r)], writes=[("pb", r)])
            for r, (h, t) in enumerate(wv):
                for cch in range(5):
                    p.op("tensor", lambda eng, pbr=pb[r], pp_=ptp[r], cch=cch: eng.transpose(
                        pp_[:, cch * 128:(cch + 1) * 128], in_=pbr[:, cch * 128:(cch + 1) * 128],
                        identity=idn[:, :]),
                        reads=[("pb", r), "idn", ("ptpf", r)], writes=[("ptp", r)])
                if r == 0:
                    p.op("scalar", lambda eng, ptr=pts[r], pp_=ptp[r]: eng.activation(
                        out=ptr, in_=pp_[:, :], func=AF.Copy),
                        reads=[("ptp", r)], writes=[("pts", r), ("ptpf", r)])
                else:
                    p.op("vector", lambda eng, ptr=pts[r], pp_=ptp[r]: eng.tensor_copy(
                        out=ptr, in_=pp_[:, :]),
                        reads=[("ptp", r)], writes=[("pts", r), ("ptpf", r)])
            for r, (h, t) in enumerate(wv):
                kv = h // 4
                for cch in range(5):
                    jj = t + cch
                    vsl = v_tm[:, jj * (HKV * 128) + kv * 128: jj * (HKV * 128) + (kv + 1) * 128]
                    p.op("tensor", lambda eng, ptr=pts[r], ap_=aps[r], vsl=vsl, cch=cch: eng.matmul(
                        ap_[:, :], lhsT=vsl, rhs=ptr[:, cch * 128:(cch + 1) * 128],
                        start=(cch == 0), stop=(cch == 4)),
                        reads=[("pts", r)] + ([("apsf", r)] if cch == 0 else []),
                        writes=[("aps", r)])
                p.op("scalar", lambda eng, h=h, t=t, ap_=aps[r]: eng.activation(
                    out=att_fm[:, h * TOK + t * 128: h * TOK + (t + 1) * 128], in_=ap_[:, :], func=AF.Copy),
                    reads=[("aps", r)], writes=[("att", h, t), ("apsf", r)])

        if "att" in tap_d:
            p.op("sync", lambda eng: eng.dma_start(out=tap_d["att"], in_=att_fm), reads=att_w, dma=True)

        p.barrier()
        sc_a2.close()

        def ano(t):
            if t < 4:
                return RC[:, t * D:(t + 1) * D]
            if t < 6:
                return RD[:, (t - 4) * D:(t - 3) * D]
            return RE[:, (t - 6) * D:(t - 5) * D]

        sc_wo = ExitStack()
        wop = [sc_wo.enter_context(nc.psum_tensor(f"wop{i}", [128, 512], F32)) for i in range(2)]

        gi = 0
        for dsl in range(4):
            for t in range(TT):
                me = gi % 2
                ps = wop[me]
                for c in range(DC):
                    p.op("tensor", lambda eng, ps=ps, c=c, t=t: eng.matmul(
                        ps[:, :], lhsT=att_fm[:, c * TOK + t * 128: c * TOK + (t + 1) * 128],
                        rhs=whalf[c // 8][:, (c % 8) * 512:(c % 8 + 1) * 512],
                        start=(c == 0), stop=(c == DC - 1)),
                        reads=[("wor", dsl, c // 8)] + ([("wopf", me)] if c == 0 else []),
                        writes=[("wop", me)])
                p.op("vector", lambda eng, ps=ps, t=t, dsl=dsl: eng.tensor_copy(
                    out=ano(t)[:, dsl * 512:(dsl + 1) * 512], in_=ps[:, :]),
                    reads=[("wop", me)], writes=[("ao", t), ("wopf", me)])
                gi += 1
            if dsl + 1 < 4:
                wo_load(dsl + 1, 0)
                wo_load(dsl + 1, 1)

        if "attno" in tap_d:
            for t in range(TT):
                p.op("sync", lambda eng, t=t: eng.dma_start(
                    out=tap_d["attno"][t * 128:(t + 1) * 128, :], in_=ano(t)),
                    reads=[("ao", t)], dma=True)

        ln_stats(TT, lambda t: (ano(t), [("ao", t)]), EPS_LN)
        for t in range(TT):
            p.op("vector", lambda eng, t=t: eng.tensor_scalar(
                out=ano(t), in0=ano(t), scalar1=st[:, 32 + t:33 + t], scalar2=st2[:, 32 + t:33 + t],
                op0=ALU.subtract, op1=ALU.mult),
                reads=[("ao", t), "stC", "stF"], writes=[("ao", t)])
            p.op("vector", lambda eng, t=t: eng.tensor_tensor(
                out=xt(t), in0=xt(t), in1=ano(t), op=ALU.add),
                reads=[("x", t), ("ao", t)], writes=[("x", t)])
            p.op("scalar", lambda eng, t=t: eng.dma_start(out=x2_d[t * 128:(t + 1) * 128, :], in_=xt(t)),
                 reads=[("x", t)], writes=[("x2d", t)], dma=True)

        if "x2" in tap_d:
            p.op("sync", lambda eng: eng.dma_start(out=tap_d["x2"], in_=x_sb),
                 reads=[("x", t) for t in range(TT)], dma=True)

        p.barrier()
        sc_wo.close()

        x2T = RCb[:, :]
        x2st = RF[:, :]
        sq_sb = RD[:, 0:1024]
        sq_bf = RDb[:, 0:1024]
        r3row = RD[:, 1024:2048]
        sc_x2s = ExitStack()
        tr2 = [sc_x2s.enter_context(nc.psum_tensor(f"tr2_{i}", [128, 1024], BF16)) for i in range(2)]
        csp = sc_x2s.enter_context(nc.psum_tensor("csp", [128, 1024], F32))
        csq2 = sc_x2s.enter_context(nc.psum_tensor("csq2", [128, 1024], F32))

        x2T3 = x2T.rearrange("p (c n) -> p c n", n=TOK)
        for t in range(TT):
            sl = x2st[:, (t % 2) * D:(t % 2 + 1) * D]
            p.op("vector", lambda eng, t=t, sl=sl: eng.tensor_copy(out=sl, in_=xt(t)),
                 reads=[("x", t)], writes=[("x2st", t % 2)])
            for half in range(2):
                ps = tr2[half]
                for j in range(8):
                    c = half * 8 + j
                    p.op("tensor", lambda eng, sl=sl, c=c, j=j, ps=ps: eng.transpose(
                        ps[:, j * 128:(j + 1) * 128], in_=sl[:, c * 128:(c + 1) * 128], identity=idn[:, :]),
                        reads=[("x2st", t % 2), "idn", ("tr2f", half)], writes=[("tr2", half)])
                dst = x2T3[:, half * 8:half * 8 + 8, t * 128:(t + 1) * 128]
                p.op("scalar", lambda eng, ps=ps, dst=dst: eng.activation(out=dst, in_=ps[:, :], func=AF.Copy),
                     reads=[("tr2", half)], writes=[("x2T", t, half), ("tr2f", half)])

        x2t_r = [("x2T", t, hf) for t in range(TT) for hf in range(2)]
        for c in range(DC):
            for m in range(2):
                p.op("tensor", lambda eng, c=c, m=m: eng.matmul(
                    csp[0:1, m * 512:(m + 1) * 512], lhsT=ones[:, 0:1],
                    rhs=x2T3[:, c, m * 512:(m + 1) * 512],
                    start=(c == 0), stop=(c == DC - 1)),
                    reads=x2t_r + ["ones"], writes=[("csp",)])
        for c in range(DC):
            p.op("scalar", lambda eng, c=c: eng.activation(out=sq_bf, in_=x2T3[:, c, :], func=AF.Square),
                 reads=x2t_r + [("sqf",)], writes=[("sq",)])
            for m in range(2):
                p.op("tensor", lambda eng, c=c, m=m: eng.matmul(
                    csq2[0:1, m * 512:(m + 1) * 512], lhsT=ones[:, 0:1],
                    rhs=sq_bf[:, m * 512:(m + 1) * 512],
                    start=(c == 0), stop=(c == DC - 1)),
                    reads=[("sq",)], writes=[("csq2",), ("sqf",)])
        p.op("vector", lambda eng: eng.tensor_scalar_mul(
            out=negm3[0:1, :], in0=csp[0:1, 0:TOK], scalar1=-1.0 / D),
            reads=[("csp",)], writes=["negm3"])
        p.op("vector", lambda eng: eng.tensor_scalar_mul(
            out=r3row[0:1, 0:TOK], in0=csp[0:1, 0:TOK], scalar1=1.0 / D),
            reads=[("csp",)], writes=[("m3r",)])
        p.op("vector", lambda eng: eng.tensor_tensor(
            out=r3row[0:1, 0:TOK], in0=r3row[0:1, 0:TOK], in1=r3row[0:1, 0:TOK], op=ALU.mult),
            reads=[("m3r",)], writes=[("m3r",)])
        p.op("vector", lambda eng: eng.tensor_scalar(
            out=r3row[1:2, 0:TOK] if False else sq_sb[0:1, 0:TOK], in0=csq2[0:1, 0:TOK],
            scalar1=1.0 / D, scalar2=0.0, op0=ALU.mult, op1=ALU.add),
            reads=[("csq2",), ("sq",)], writes=[("e2r",)])
        p.op("vector", lambda eng: eng.tensor_tensor(
            out=r3row[0:1, 0:TOK], in0=sq_sb[0:1, 0:TOK], in1=r3row[0:1, 0:TOK], op=ALU.subtract),
            reads=[("e2r",), ("m3r",)], writes=[("v3r",)])
        p.op("vector", lambda eng: eng.tensor_scalar_add(
            out=r3row[0:1, 0:TOK], in0=r3row[0:1, 0:TOK], scalar1=EPS_LN),
            reads=[("v3r",)], writes=[("v3r",)])
        p.op("scalar", lambda eng: eng.activation(out=r3row[0:1, 0:TOK], in_=r3row[0:1, 0:TOK], func=AF.Ln),
             reads=[("v3r",)], writes=[("v3r",)])
        p.op("scalar", lambda eng: eng.activation(out=r3row[0:1, 0:TOK], in_=r3row[0:1, 0:TOK],
                                                  func=AF.Exp, scale=-0.5),
             reads=[("v3r",)], writes=[("r3row",)])
        nmr3row = RD[0:1, 2048:2048 + TOK]
        p.op("vector", lambda eng: eng.tensor_tensor(
            out=nmr3row, in0=r3row[0:1, 0:TOK], in1=negm3[0:1, :], op=ALU.mult),
            reads=[("r3row",), "negm3"], writes=[("nmr3r",)])
        r3full_bf = RDb[:, 0:TOK]
        nmr3full_bf = RDb[:, TOK:2 * TOK]
        for m in range(2):
            p.op("tensor", lambda eng, m=m: eng.matmul(
                csp[:, m * 512:(m + 1) * 512], lhsT=onesr[0:1, :],
                rhs=r3row[0:1, m * 512:(m + 1) * 512], start=True, stop=True),
                reads=[("r3row",), "onesr", ("csp",)], writes=[("csp",)])
            p.op("tensor", lambda eng, m=m: eng.matmul(
                csq2[:, m * 512:(m + 1) * 512], lhsT=onesr[0:1, :],
                rhs=nmr3row[0:1, m * 512:(m + 1) * 512], start=True, stop=True),
                reads=[("nmr3r",), "onesr", ("csq2",), ("sq",)], writes=[("csq2",)])
        p.op("vector", lambda eng: eng.tensor_copy(out=r3full_bf, in_=csp[:, 0:TOK]),
             reads=[("csp",)], writes=["r3full", ("sq",), ("e2r",)])
        p.op("vector", lambda eng: eng.tensor_copy(out=nmr3full_bf, in_=csq2[:, 0:TOK]),
             reads=[("csq2",)], writes=["nmr3full", ("e2r",)])
        for c in range(DC):
            p.op("vector", lambda eng, c=c: eng.tensor_tensor(
                out=x2T3[:, c, :], in0=x2T3[:, c, :], in1=r3full_bf, op=ALU.mult),
                reads=["r3full"], writes=[("ln3m", c)] + [("x2T", t, c // 8) for t in range(TT)])
            p.op("vector", lambda eng, c=c: eng.tensor_tensor(
                out=x2T3[:, c, :], in0=x2T3[:, c, :], in1=nmr3full_bf, op=ALU.add),
                reads=[("ln3m", c), "nmr3full"], writes=[("ln3", c)])

        p.barrier()
        sc_x2s.close()

        h_fm = RAb[:, :]
        RING = 4
        wsl = [RB[:, i * 4096:(i + 1) * 4096] for i in range(RING)]
        qsc = []
        for k_ in range(4):
            base = RB[:, 16384 + k_ * 4096: 16384 + (k_ + 1) * 4096] if k_ < 2 \
                else RDb[:, (k_ - 2) * 4096:(k_ - 1) * 4096]
            qsc.append(dict(
                wnat=base[:, 0:1024], qa=base[:, 1024:2048],
                qsg=base[:, 2048:3072], qms=base[:, 3072:4096],
                qr=qr[:, k_ * 16:k_ * 16 + 16], qs=qs[:, k_ * 16:k_ * 16 + 16],
                qt=qt[:, k_ * 16:k_ * 16 + 16]))
        sgf = [REb[:, i * 1024:(i + 1) * 1024] for i in range(2)]
        hsq = REb[:, 4096:5120]
        ssqrow = RE[0:1, 2048:3072]

        h_fm3 = h_fm.rearrange("p (c n) -> p c n", n=TOK)
        qpar = [0]

        def ano2(t):
            if t < 4:
                return RC[:, t * D:(t + 1) * D]
            if t < 6:
                return RBf[:, 8192 + (t - 4) * D: 8192 + (t - 3) * D]
            return RE[:, (t - 6) * D:(t - 5) * D]

        def emit_quant_wave(jobs, sets, sid):
            n = len(jobs)
            rb = sid % RING
            for i in range(n):
                k_, (src, jj, mk) = sets[i], jobs[i]
                p.op("gpsimd", lambda eng, a=src, w=qsc[k_]["wnat"]: eng.dma_start(out=w, in_=a),
                     writes=[("wnat", k_)], dma=True)
            for i in range(n):
                k_ = sets[i]
                p.op("vector", lambda eng, w=qsc[k_]["wnat"], q=qsc[k_]["qr"]: eng.reduce_sum(
                    out=q, in_=w.rearrange("p (g s) -> p g s", s=64),
                    axis=AX.X, apply_absolute_value=True),
                    reads=[("wnat", k_)], writes=[("qr", k_)])
            for i in range(n):
                k_ = sets[i]
                p.op("scalar", lambda eng, w=qsc[k_]["wnat"], q=qsc[k_]["qa"]: eng.activation(
                    out=q, in_=w, func=AF.Abs),
                    reads=[("wnat", k_)], writes=[("qa", k_)])
            for i in range(n):
                k_ = sets[i]
                p.op("scalar", lambda eng, w=qsc[k_]["wnat"], q=qsc[k_]["qsg"]: eng.activation(
                    out=q, in_=w, func=AF.Sign),
                    reads=[("wnat", k_)], writes=[("qsg", k_)])
            for i in range(n):
                k_ = sets[i]
                p.op("vector", lambda eng, a=qsc[k_]["qr"], b=qsc[k_]["qs"]: eng.tensor_scalar(
                    out=b, in0=a, scalar1=1.0 / 64, scalar2=1e-8, op0=ALU.mult, op1=ALU.max),
                    reads=[("qr", k_)], writes=[("qs", k_)])
                p.op("vector", lambda eng, a=qsc[k_]["qs"], b=qsc[k_]["qt"]: eng.tensor_scalar_mul(
                    out=b, in0=a, scalar1=THR),
                    reads=[("qs", k_)], writes=[("qt", k_)])
            for i in range(n):
                k_ = sets[i]
                sc = qsc[k_]
                a3 = sc["qa"].rearrange("p (g s) -> p g s", s=64)
                m3 = sc["qms"].rearrange("p (g s) -> p g s", s=64)
                tb = sc["qt"].unsqueeze(2).broadcast_to((128, 16, 64))
                p.op("vector", lambda eng, a3=a3, m3=m3, tb=tb: eng.tensor_tensor(
                    out=m3, in0=a3, in1=tb, op=ALU.is_gt),
                    reads=[("qa", k_), ("qt", k_)], writes=[("qms", k_)])
            for i in range(n):
                k_ = sets[i]
                sc = qsc[k_]
                m3 = sc["qms"].rearrange("p (g s) -> p g s", s=64)
                sb = sc["qs"].unsqueeze(2).broadcast_to((128, 16, 64))
                p.op("vector", lambda eng, m3=m3, sb=sb: eng.tensor_tensor(
                    out=m3, in0=m3, in1=sb, op=ALU.mult),
                    reads=[("qms", k_), ("qs", k_)], writes=[("qms", k_)])
            for i in range(n):
                k_ = sets[i]
                sc = qsc[k_]
                p.op("gpsimd", lambda eng, a=sc["qa"], m=sc["qms"], g=sc["qsg"]: eng.tensor_tensor(
                    out=a, in0=m, in1=g, op=ALU.mult),
                    reads=[("qms", k_), ("qsg", k_)], writes=[("qa", k_)])
            for i in range(n):
                k_, (src, jj, mk) = sets[i], jobs[i]
                tp = qpar[0] % 2
                qpar[0] += 1
                ps = trcs[tp]
                for j in range(8):
                    p.op("tensor", lambda eng, j=j, ps=ps, q=qsc[k_]["qa"]: eng.transpose(
                        ps[:, j * 128:(j + 1) * 128], in_=q[:, j * 128:(j + 1) * 128],
                        identity=idn[:, :]),
                        reads=[("qa", k_), "idn", ("trcf", tp)], writes=[("trc", tp)])
                out_ap, three_d = mk(rb)
                in_ap = ps[:, :].rearrange("p (j x) -> p j x", x=128) if three_d else ps[:, :]
                wkeys = [("wb", sid, jj)]
                if sid >= RING:
                    wkeys += [("wb", sid - RING, j2) for j2 in range(4)]
                p.op("scalar", lambda eng, out_ap=out_ap, in_ap=in_ap: eng.activation(
                    out=out_ap, in_=in_ap, func=AF.Copy),
                    reads=[("trc", tp), ("qa", k_)], writes=wkeys + [("trcf", tp)])

        qjobs = []
        for hh in range(2):
            for pair in range(16):
                for gu, base in ((0, 0), (1, H)):
                    row0 = base + hh * HHALF + pair * 256
                    for jj, (sub, colh) in enumerate(
                            [(s, c) for s in range(2) for c in range(2)]):
                        def mk_gu(rb, colh=colh, sub=sub):
                            o = wsl[rb].rearrange("p (c x) -> p c x", x=256)
                            return (o[:, colh * 8:(colh + 1) * 8,
                                      sub * 128:(sub + 1) * 128], True)
                        qjobs.append((wgu_d[row0 + sub * 128: row0 + (sub + 1) * 128,
                                            colh * 1024:(colh + 1) * 1024], jj, mk_gu))
            for dc in range(DC):
                for pcl in range(4):
                    def mk_wp(rb, pcl=pcl):
                        return (wsl[rb][:, pcl * 1024:(pcl + 1) * 1024], False)
                    qjobs.append((wp_d[dc * 128:(dc + 1) * 128,
                                       hh * HHALF + pcl * 1024: hh * HHALF + (pcl + 1) * 1024],
                                  pcl, mk_wp))
        NSLICE = len(qjobs) // 4
        qnext = [0]
        qdma_done = set()
        SETS4 = [0, 1, 2, 3]

        def emit_qdma_g(jobs, sets):
            for i, (src, jj, mk) in enumerate(jobs):
                k_ = sets[i]
                p.op("gpsimd", lambda eng, a=src, w=qsc[k_]["wnat"]: eng.dma_start(out=w, in_=a),
                     writes=[("wnat", k_)], dma=True)

        def emit_qdma(sid):
            emit_qdma_g(qjobs[sid * 4:(sid + 1) * 4], SETS4)
            qdma_done.add(sid)

        def emit_qRED_g(jobs, sets):
            for i in range(len(jobs)):
                k_ = sets[i]
                p.op("vector", lambda eng, w=qsc[k_]["wnat"], q=qsc[k_]["qr"]: eng.reduce_sum(
                    out=q, in_=w.rearrange("p (g s) -> p g s", s=64),
                    axis=AX.X, apply_absolute_value=True),
                    reads=[("wnat", k_)], writes=[("qr", k_)])

        def emit_qTS_g(jobs, sets):
            for i in range(len(jobs)):
                k_ = sets[i]
                p.op("vector", lambda eng, a=qsc[k_]["qr"], b=qsc[k_]["qs"]: eng.tensor_scalar(
                    out=b, in0=a, scalar1=1.0 / 64, scalar2=1e-8, op0=ALU.mult, op1=ALU.max),
                    reads=[("qr", k_)], writes=[("qs", k_)])
                p.op("vector", lambda eng, a=qsc[k_]["qs"], b=qsc[k_]["qt"]: eng.tensor_scalar_mul(
                    out=b, in0=a, scalar1=THR),
                    reads=[("qs", k_)], writes=[("qt", k_)])

        def emit_qA1_g(jobs, sets):
            emit_qRED_g(jobs, sets)
            emit_qTS_g(jobs, sets)

        def emit_qA2_g(jobs, sets):
            n = len(jobs)
            for i in range(n):
                k_ = sets[i]
                p.op("scalar", lambda eng, w=qsc[k_]["wnat"], q=qsc[k_]["qa"]: eng.activation(
                    out=q, in_=w, func=AF.Abs),
                    reads=[("wnat", k_)], writes=[("qa", k_)])
            for i in range(n):
                k_ = sets[i]
                p.op("scalar", lambda eng, w=qsc[k_]["wnat"], q=qsc[k_]["qsg"]: eng.activation(
                    out=q, in_=w, func=AF.Sign),
                    reads=[("wnat", k_)], writes=[("qsg", k_)])

        def emit_qA(sid):
            jobs = qjobs[sid * 4:(sid + 1) * 4]
            emit_qA1_g(jobs, SETS4)
            emit_qA2_g(jobs, SETS4)

        def emit_qB1_g(jobs, sets):
            n = len(jobs)
            for i in range(n):
                k_ = sets[i]
                sc = qsc[k_]
                a3 = sc["qa"].rearrange("p (g s) -> p g s", s=64)
                m3 = sc["qms"].rearrange("p (g s) -> p g s", s=64)
                tb = sc["qt"].unsqueeze(2).broadcast_to((128, 16, 64))
                p.op("vector", lambda eng, a3=a3, m3=m3, tb=tb: eng.tensor_tensor(
                    out=m3, in0=a3, in1=tb, op=ALU.is_gt),
                    reads=[("qa", k_), ("qt", k_)], writes=[("qms", k_)])
            for i in range(n):
                k_ = sets[i]
                sc = qsc[k_]
                m3 = sc["qms"].rearrange("p (g s) -> p g s", s=64)
                sb = sc["qs"].unsqueeze(2).broadcast_to((128, 16, 64))
                p.op("vector", lambda eng, m3=m3, sb=sb: eng.tensor_tensor(
                    out=m3, in0=m3, in1=sb, op=ALU.mult),
                    reads=[("qms", k_), ("qs", k_)], writes=[("qms", k_)])
            for i in range(n):
                k_ = sets[i]
                sc = qsc[k_]
                p.op("gpsimd", lambda eng, a=sc["qa"], m=sc["qms"], g=sc["qsg"]: eng.tensor_tensor(
                    out=a, in0=m, in1=g, op=ALU.mult),
                    reads=[("qms", k_), ("qsg", k_)], writes=[("qa", k_)])

        def emit_qB1(sid):
            emit_qB1_g(qjobs[sid * 4:(sid + 1) * 4], SETS4)

        def emit_qB2_g(jobs, sets, sid):
            rb = sid % RING
            for i in range(len(jobs)):
                k_, (src, jj, mk) = sets[i], jobs[i]
                tp = qpar[0] % 2
                qpar[0] += 1
                ps = trcs[tp]
                for j in range(8):
                    p.op("tensor", lambda eng, j=j, ps=ps, q=qsc[k_]["qa"]: eng.transpose(
                        ps[:, j * 128:(j + 1) * 128], in_=q[:, j * 128:(j + 1) * 128],
                        identity=idn[:, :]),
                        reads=[("qa", k_), "idn", ("trcf", tp)], writes=[("trc", tp)])
                out_ap, three_d = mk(rb)
                in_ap = ps[:, :].rearrange("p (j x) -> p j x", x=128) if three_d else ps[:, :]
                wkeys = [("wb", sid, jj)]
                if sid >= RING:
                    wkeys += [("wb", sid - RING, j2) for j2 in range(4)]
                p.op("scalar", lambda eng, out_ap=out_ap, in_ap=in_ap: eng.activation(
                    out=out_ap, in_=in_ap, func=AF.Copy),
                    reads=[("trc", tp), ("qa", k_)], writes=wkeys + [("trcf", tp)])

        def emit_qB2(sid):
            emit_qB2_g(qjobs[sid * 4:(sid + 1) * 4], SETS4, sid)

        RLIM = NSLICE - 12
        qa_pend = [False]

        def pump_half_a():
            if qa_pend[0] or qnext[0] >= min(NSLICE, RLIM):
                return
            sid = qnext[0]
            if sid not in qdma_done:
                emit_qdma(sid)
            emit_qA(sid)
            if sid + 1 < min(NSLICE, RLIM):
                emit_qdma(sid + 1)
            qa_pend[0] = True

        qb1_done = [False]

        def pump_half_b1():
            if qa_pend[0] and not qb1_done[0]:
                emit_qB1(qnext[0])
                qb1_done[0] = True

        def pump_half_b():
            if qa_pend[0]:
                pump_half_b1()
                emit_qB2(qnext[0])
                qnext[0] += 1
                qa_pend[0] = False
                qb1_done[0] = False

        def pump_to(upto):
            while qnext[0] < min(upto, RLIM):
                pump_half_a()
                pump_half_b()

        RSETS = [2, 3]
        NU = (NSLICE - RLIM) * 2
        rseq = [("a1", 0), ("a2", 0), ("b1", 0), ("a1", 1)]
        for u_ in range(1, NU):
            rseq += [("b2", u_ - 1), ("a2", u_), ("b1", u_)]
            if u_ + 1 < NU:
                rseq += [("a1", u_ + 1)]
        rseq += [("b2", NU - 1)]
        rsi = [0]
        rdone = {"b2": 0}

        def runit_jobs(u):
            sid = RLIM + u // 2
            o = sid * 4 + (u % 2) * 2
            return qjobs[o:o + 2], sid

        def rstep():
            if rsi[0] >= len(rseq):
                return
            stg, u = rseq[rsi[0]]
            rsi[0] += 1
            jobs, sid = runit_jobs(u)
            if stg == "a1":
                emit_qdma_g(jobs, RSETS)
            elif stg == "a2":
                emit_qA2_g(jobs, RSETS)
            elif stg == "b1":
                emit_qRED_g(jobs, RSETS)
                emit_qTS_g(jobs, RSETS)
                emit_qB1_g(jobs, RSETS)
            else:
                emit_qB2_g(jobs, RSETS, sid)
                rdone["b2"] += 1

        for hhalf in range(2):
            sc_c1 = ExitStack()
            pps = [sc_c1.enter_context(nc.psum_tensor(f"pps{hhalf}_{i}", [128, 1024], F32)) for i in range(2)]
            trcs = [sc_c1.enter_context(nc.psum_tensor(f"trc{hhalf}_{i}", [128, 1024], BF16)) for i in range(2)]
            sqp = sc_c1.enter_context(nc.psum_tensor(f"sqp{hhalf}", [128, 1024], F32))

            pump_to(hhalf * (NSLICE // 2) + RING)
            gi = 0
            for pair in range(16):
                pump_half_a()
                for gu in range(2):
                    sid = hhalf * (NSLICE // 2) + 2 * pair + gu
                    assert qnext[0] > sid, ("quant not emitted", sid)
                    rb = sid % RING
                    wv = wsl[rb].rearrange("p (c x) -> p c x", x=256)
                    mes = []
                    for hcl in range(2):
                        me = gi % 2
                        mes.append(me)
                        ps = pps[me]
                        for c in range(DC):
                            for m in range(2):
                                p.op("tensor", lambda eng, ps=ps, c=c, m=m, hcl=hcl, wv=wv: eng.matmul(
                                    ps[:, m * 512:(m + 1) * 512],
                                    lhsT=wv[:, c, hcl * 128:(hcl + 1) * 128],
                                    rhs=x2T3[:, c, m * 512:(m + 1) * 512],
                                    start=(c == 0), stop=(c == DC - 1)),
                                    reads=[("wb", sid, hcl * 2 + c // 8)]
                                    + ([("ppsf", me)] if c == 0 and m == 0 else []),
                                    writes=[("pps", me)])
                        gi += 1
                    pump_half_b1()
                    for hcl in range(2):
                        me = mes[hcl]
                        ps = pps[me]
                        if gu == 0:
                            p.op("scalar", lambda eng, ps=ps, hcl=hcl: eng.activation(
                                out=sgf[hcl], in_=ps[:, :], func=AF.Silu),
                                reads=[("pps", me)], writes=[("sg", hcl), ("ppsf", me)])
                        else:
                            ch = pair * 2 + hcl
                            p.op("vector", lambda eng, ps=ps, ch=ch, hcl=hcl: eng.tensor_tensor(
                                out=h_fm3[:, ch, :], in0=sgf[hcl], in1=ps[:, :], op=ALU.mult),
                                reads=[("sg", hcl), ("pps", me)], writes=[("hf", ch), ("ppsf", me)])
                            p.op("scalar", lambda eng, ch=ch: eng.activation(
                                out=hsq, in_=h_fm3[:, ch, :], func=AF.Square),
                                reads=[("hf", ch), ("hsqf",)], writes=[("hsq",)])
                            first = (pair == 0 and hcl == 0)
                            last = (pair == 15 and hcl == 1)
                            for m in range(2):
                                p.op("tensor", lambda eng, m=m, first=first, last=last: eng.matmul(
                                    sqp[0:1, m * 512:(m + 1) * 512], lhsT=ones[:, 0:1],
                                    rhs=hsq[:, m * 512:(m + 1) * 512],
                                    start=first, stop=last),
                                    reads=[("hsq",), "ones"], writes=[("sqp",), ("hsqf",)])
                    pump_half_b()
                    if gu == 0:
                        pump_half_a()
                pump_to(hhalf * (NSLICE // 2) + 2 * pair + 2 + RING)

            p.op("vector", lambda eng: eng.tensor_copy(out=ssqrow, in_=sqp[0:1, 0:TOK]),
                 reads=[("sqp",), ("hsq",)], writes=[("ssqr",)])
            for t in range(TT):
                p.op("tensor", lambda eng, t=t: eng.transpose(
                    pps[0][:, 0:128], in_=RE[:, 2048 + t * 128: 2048 + (t + 1) * 128],
                    identity=idnf[:, :]),
                    reads=[("ssqr",), "idnf", ("ppsf", 0)], writes=[("pps", 0)])
                p.op("vector", lambda eng, t=t, hhalf=hhalf: eng.tensor_copy(
                    out=ssq_h[:, t * 2 + hhalf: t * 2 + hhalf + 1], in_=pps[0][:, 0:1]),
                    reads=[("pps", 0)], writes=[("ssqc", hhalf, t), ("ppsf", 0)])

            if "h0" in tap_d and hhalf == 0:
                p.op("sync", lambda eng: eng.dma_start(out=tap_d["h0"], in_=h_fm),
                     reads=[("hf", ch) for ch in range(32)], dma=True)

            p.barrier()
            sc_c1.close()

            if hhalf == 1:
                ssq2 = ssq_h[:, 0:2 * TT].rearrange("p (t u) -> p t u", u=2)
                p.op("vector", lambda eng: eng.tensor_tensor(
                    out=rmsr[:, 0:TT], in0=ssq2[:, :, 0], in1=ssq2[:, :, 1], op=ALU.add),
                    reads=[], writes=["rmsA"])
                p.op("vector", lambda eng: eng.tensor_scalar(
                    out=rmsr[:, 0:TT], in0=rmsr[:, 0:TT], scalar1=1.0 / H, scalar2=EPS_RMS,
                    op0=ALU.mult, op1=ALU.add),
                    reads=["rmsA"], writes=["rmsA"])
                p.op("scalar", lambda eng: eng.activation(out=rmsr[:, 0:TT], in_=rmsr[:, 0:TT], func=AF.Ln),
                     reads=["rmsA"], writes=["rmsA"])
                p.op("scalar", lambda eng: eng.activation(out=rmsr[:, 0:TT], in_=rmsr[:, 0:TT],
                                                          func=AF.Exp, scale=-0.5),
                     reads=["rmsA"], writes=["rmsr"])

            sc_c2 = ExitStack()
            pjp = [sc_c2.enter_context(nc.psum_tensor(f"pjp{hhalf}_{i}", [128, 1024], F32)) for i in range(2)]
            trcs = [sc_c2.enter_context(nc.psum_tensor(f"trd{hhalf}_{i}", [128, 1024], BF16)) for i in range(2)]
            tro = sc_c2.enter_context(nc.psum_tensor(f"tro{hhalf}", [128, 1024], BF16))
            pst = RG[:, 0:512]
            x2r = RFf[:, :]
            ffcv = RG[:, :].bitcast(BF16)

            for dc in range(DC):
                sid = hhalf * (NSLICE // 2) + 32 + dc
                if sid < RLIM:
                    assert qnext[0] > sid, ("wp quant not emitted", sid)
                else:
                    assert rdone["b2"] >= (sid - RLIM + 1) * 2, ("tail unit", sid)
                pump_half_a()
                if hhalf == 1:
                    rstep()
                rb = sid % RING
                me = dc % 2
                ps = pjp[me]
                for m in range(2):
                    for ci in range(32):
                        p.op("tensor", lambda eng, ps=ps, ci=ci, m=m, rb=rb: eng.matmul(
                            ps[:, m * 512:(m + 1) * 512],
                            lhsT=wsl[rb][:, ci * 128:(ci + 1) * 128],
                            rhs=h_fm3[:, ci, m * 512:(m + 1) * 512],
                            start=(ci == 0), stop=(ci == 31)),
                            reads=[("wb", sid, ci // 8)]
                            + ([("pjpf", me)] if ci == 0 and m == 0 else []),
                            writes=[("pjp", me, m)])
                pump_half_b1()
                if hhalf == 1:
                    rstep()
                    rstep()
                if hhalf == 0:
                    for m in range(2):
                        p.op("vector", lambda eng, ps=ps, m=m: eng.tensor_copy(
                            out=pst, in_=ps[:, m * 512:(m + 1) * 512]),
                            reads=[("pjp", me, m), ("pstf",)], writes=[("pst",)])
                        p.op("scalar", lambda eng, dc=dc, m=m: eng.dma_start(
                            out=pp_d[dc * 128:(dc + 1) * 128, m * 512:(m + 1) * 512], in_=pst),
                            reads=[("pst",)], writes=[("ppd", dc, m), ("pstf",)], dma=True)
                    p.op("sync", lambda eng: eng.nop(),
                         reads=[("pjp", me, 0), ("pjp", me, 1)], writes=[("pjpf", me)])
                else:
                    p.op("sync", lambda eng, dc=dc: eng.dma_start(
                        out=x2r[:, 0:1024], in_=pp_d[dc * 128:(dc + 1) * 128, 0:1024]),
                        reads=[("x2rf",)], writes=[("x2r",)], dma=True)
                    p.op("vector", lambda eng, ps=ps: eng.tensor_tensor(
                        out=ffcv, in0=ps[:, :], in1=x2r[:, 0:1024], op=ALU.add),
                        reads=[("pjp", me, 0), ("pjp", me, 1), ("x2r",), ("ffcf",)],
                        writes=[("ffc",), ("pjpf", me), ("x2rf",)])
                    rstep()
                    rstep()
                    for t in range(TT):
                        p.op("tensor", lambda eng, t=t: eng.transpose(
                            tro[:, t * 128:(t + 1) * 128],
                            in_=ffcv[:, t * 128:(t + 1) * 128], identity=idn[:, :]),
                            reads=[("ffc",), "idn"] + ([("trof",)] if t == 0 else []),
                            writes=[("tro",)])
                    for t in range(TT):
                        p.op("vector", lambda eng, t=t, dc=dc: eng.tensor_scalar_mul(
                            out=ano2(t)[:, dc * 128:(dc + 1) * 128], in0=tro[:, t * 128:(t + 1) * 128],
                            scalar1=rmsr[:, t:t + 1]),
                            reads=[("tro",), "rmsr"],
                            writes=[("ffo", t)] + ([("trof",), ("ffcf",)] if t == 7 else []))
                if hhalf == 1:
                    rstep()
                    rstep()
                pump_half_b()
                pump_to(hhalf * (NSLICE // 2) + 32 + dc + 1 + RING)

            if hhalf == 1:
                if "ffo" in tap_d:
                    for t in range(TT):
                        p.op("sync", lambda eng, t=t: eng.dma_start(
                            out=tap_d["ffo"][t * 128:(t + 1) * 128, :], in_=ano2(t)),
                            reads=[("ffo", t)], dma=True)
                ln_stats(TT, lambda t: (ano2(t), [("ffo", t)]), EPS_LN)
                rdx = [RD[:, 0:D], RD[:, D:2 * D]]
                for t in range(TT):
                    xb = rdx[t % 2]
                    wk = [("x2b", t % 2)] + ([("qa", 2), ("qa", 3), ("qms", 2),
                                              ("qms", 3)] if t < 2 else [])
                    p.op("sync", lambda eng, t=t, xb=xb: eng.dma_start(
                        out=xb, in_=x2_d[t * 128:(t + 1) * 128, :]),
                        writes=wk, dma=True)
                    p.op("vector", lambda eng, t=t: eng.tensor_scalar(
                        out=ano2(t), in0=ano2(t), scalar1=st[:, 32 + t:33 + t], scalar2=st2[:, 32 + t:33 + t],
                        op0=ALU.subtract, op1=ALU.mult),
                        reads=[("ffo", t), "stC", "stF"], writes=[("ffo", t)])
                    p.op("vector", lambda eng, t=t, xb=xb: eng.tensor_tensor(
                        out=ano2(t), in0=ano2(t), in1=xb, op=ALU.add),
                        reads=[("ffo", t), ("x2b", t % 2)], writes=[("ffo", t)])
                    p.op("scalar", lambda eng, t=t: eng.dma_start(
                        out=out_d[t * 128:(t + 1) * 128, :], in_=ano2(t)),
                        reads=[("ffo", t)], writes=[("outd", t)], dma=True)

            p.barrier()
            sc_c2.close()


        p.op("gpsimd", lambda eng: eng.nop(), reads=[("outd", t) for t in range(TT)])

        with nc.Block() as block:
            p.emit(block, sems_factory)

    return nc


_NC = {}


def _masks(boundary):
    m = np.zeros((5, 128, WIN), dtype=np.float32)
    i = np.arange(128)[:, None]
    j = np.arange(WIN)[None, :]
    for t in range(5):
        valid = (j >= i + 1) & (j <= i + 512)
        if boundary and t < 4:
            valid = valid & ((128 * t + j - 512) >= 0)
        m[t] = np.where(valid, 0.0, NEG)
    return m.reshape(5 * 128, WIN).astype(bfdt)


def kernel(L0_attn_hidden_states, L0_attn_Wq, L0_attn_Wk, L0_attn_Wv, L0_attn_Wo,
           L0_ffn_w_gate_up, L0_ffn_w_proj):
    hs = np.ascontiguousarray(L0_attn_hidden_states, dtype=np.float32)
    if "prog" not in _NC:
        _NC["prog"] = build()
    nc = _NC["prog"]

    shared = {
        "wq": np.ascontiguousarray(L0_attn_Wq, dtype=np.float32),
        "wk": np.ascontiguousarray(L0_attn_Wk, dtype=np.float32),
        "wv": np.ascontiguousarray(L0_attn_Wv, dtype=np.float32),
        "wo": np.ascontiguousarray(L0_attn_Wo, dtype=np.float32),
        "wgu": np.ascontiguousarray(L0_ffn_w_gate_up, dtype=np.float32),
        "wp": np.ascontiguousarray(L0_ffn_w_proj, dtype=np.float32),
        "idn": np.eye(128, dtype=bfdt),
    }
    mb, mi = _masks(True), _masks(False)
    zh = np.zeros((HALO, D), dtype=np.float32)
    in_maps = []
    for c in range(NCORES):
        b, q = divmod(c, 4)
        im = dict(shared)
        im["x"] = np.ascontiguousarray(hs[b, q * TOK:(q + 1) * TOK, :])
        im["xh"] = zh if q == 0 else np.ascontiguousarray(hs[b, q * TOK - HALO:q * TOK, :])
        im["msk"] = mb if q == 0 else mi
        in_maps.append(im)

    res = run_bass_kernel_spmd(nc, in_maps, core_ids=list(range(NCORES)))
    out = np.empty((B, S, D), dtype=np.float32)
    for c in range(NCORES):
        b, q = divmod(c, 4)
        out[b, q * TOK:(q + 1) * TOK, :] = res.results[c]["out"]
    return out


# version 23
# speedup vs baseline: 1.0374x; 1.0130x over previous
import numpy as np
import ml_dtypes
import concourse.bass as bass
import concourse.mybir as mybir
from concourse.bass_utils import run_bass_kernel_spmd
from contextlib import ExitStack

F32 = mybir.dt.float32
BF16 = mybir.dt.bfloat16
I32 = mybir.dt.int32
AF = mybir.ActivationFunctionType
ALU = mybir.AluOpType
AX = mybir.AxisListType
bfdt = ml_dtypes.bfloat16

B, S, D = 2, 4096, 2048
NCORES = 8
TOK = 1024
HALO = 512
TT = 8
TH = 4
TA = TT + TH
DC = 16
HQ, HKV, HD = 16, 4, 128
WIN = 640
H = 8192
HHALF = 4096
EPS_LN = 1e-5
EPS_RMS = 1.1920929e-07
THR = 0.5 + 2.0 ** -9
NEG = -1e30

ENGINES = ("sync", "vector", "scalar", "tensor", "gpsimd")


class P:

    NDMA = 4

    def __init__(self):
        self.ops = []
        self.epos = {e: [] for e in ENGINES}
        self.lastw = {}
        self.lastr = {}
        self.dmai = {e: 0 for e in ("sync", "scalar", "gpsimd")}
        self.dma_last = {}

    def op(self, eng, fn, reads=(), writes=(), dma=False):
        deps = {}
        myid = len(self.ops)

        def add(j):
            if j is not None and not (self.ops[j][0] == "tensor" and eng == "tensor"
                                      and not self.ops[j][3]):
                deps[j] = True

        for k in reads:
            add(self.lastw.get(k))
        for k in writes:
            add(self.lastw.get(k))
            for j in self.lastr.get(k, ()):
                add(j)
        if dma:
            sem = f"dq_{eng}_{self.dmai[eng] % self.NDMA}"
            self.dmai[eng] += 1
            add(self.dma_last.get(sem))
            self.dma_last[sem] = myid
        else:
            sem = None
        self.ops.append([eng, fn, deps, dma, sem])
        self.epos[eng].append(myid)
        for k in writes:
            self.lastw[k] = myid
            self.lastr.pop(k, None)
        for k in reads:
            self.lastr.setdefault(k, []).append(myid)
        return myid

    def barrier(self):
        tails = [self.epos[e][-1] for e in ENGINES if self.epos[e]]
        tails += [j for j in self.dma_last.values() if j is not None]
        for e in ENGINES:
            myid = len(self.ops)
            deps = {j: True for j in tails}
            self.ops.append([e, lambda eng: eng.nop(), deps, False, None])
            self.epos[e].append(myid)
        self.lastw.clear()
        self.lastr.clear()
        for k in list(self.dma_last):
            self.dma_last[k] = None

    def emit(self, block, sems_factory):
        n = len(self.ops)
        dom = [op[4] if op[4] else op[0] for op in self.ops]
        dpos = {}
        dcount = {}
        for j in range(n):
            k = dom[j]
            dpos[j] = dcount.get(k, 0)
            dcount[k] = dpos[j] + 1
        dops = {}
        for j in range(n):
            dops.setdefault(dom[j], []).append(j)
        clocks = [None] * n
        needs_inc = [op[3] for op in self.ops]
        waits_out = [None] * n
        implied = {e: {} for e in ENGINES}
        for j, (eng, fn, deps, dma, dsem) in enumerate(self.ops):
            imp = implied[eng]
            tgt = {}
            for d in deps:
                dk = dom[d]
                if imp.get(dk, -1) >= dpos[d]:
                    continue
                if dpos[d] > tgt.get(dk, -1):
                    tgt[dk] = dpos[d]
            w = []
            for dk, dp in sorted(tgt.items(), key=lambda x: str(x[0])):
                d = dops[dk][dp]
                needs_inc[d] = True
                w.append(d)
                dc = clocks[d]
                for k2, v2 in dc.items():
                    if v2 > imp.get(k2, -1):
                        imp[k2] = v2
                if dp > imp.get(dk, -1):
                    imp[dk] = dp
            waits_out[j] = w
            c = dict(imp)
            k = dom[j]
            if dpos[j] > c.get(k, -1):
                c[k] = dpos[j]
            clocks[j] = c
        semval = [0] * n
        cnt = {}
        for j in range(n):
            if needs_inc[j]:
                k = dom[j]
                cnt[k] = cnt.get(k, 0) + (16 if self.ops[j][3] else 1)
                semval[j] = cnt[k]
        sems = sems_factory(sorted(cnt.keys()))
        for name in ENGINES:
            eops = self.epos[name]

            def body(eng, eops=eops, name=name):
                cur = {}
                for j in eops:
                    _e, fn, deps, dma, dsem = self.ops[j]
                    for d in waits_out[j]:
                        key = dom[d]
                        v = semval[d]
                        if cur.get(key, 0) < v:
                            eng.wait_ge(sems[key], v)
                            cur[key] = v
                    instr = fn(eng)
                    if instr is not None and needs_inc[j]:
                        key = dom[j]
                        instr.then_inc(sems[key], 16 if dma else 1)
            getattr(block, name)(body)


def build(taps=()):
    nc = bass.Bass("TRN2", target_bir_lowering=False, debug=False)

    x_d = nc.dram_tensor("x", [TOK, D], F32, kind="ExternalInput").ap()
    xh_d = nc.dram_tensor("xh", [HALO, D], F32, kind="ExternalInput").ap()
    wq_d = nc.dram_tensor("wq", [D, D], F32, kind="ExternalInput").ap()
    wk_d = nc.dram_tensor("wk", [D, HKV * HD], F32, kind="ExternalInput").ap()
    wv_d = nc.dram_tensor("wv", [D, HKV * HD], F32, kind="ExternalInput").ap()
    wo_d = nc.dram_tensor("wo", [D, D], F32, kind="ExternalInput").ap()
    wgu_d = nc.dram_tensor("wgu", [2 * H, D], F32, kind="ExternalInput").ap()
    wp_d = nc.dram_tensor("wp", [D, H], F32, kind="ExternalInput").ap()
    msk_d = nc.dram_tensor("msk", [5 * 128, WIN], BF16, kind="ExternalInput").ap()
    idn_d = nc.dram_tensor("idn", [128, 128], BF16, kind="ExternalInput").ap()
    out_d = nc.dram_tensor("out", [TOK, D], F32, kind="ExternalOutput").ap()
    x2_d = nc.dram_tensor("x2buf", [TOK, D], F32)
    pp_d = nc.dram_tensor("ppart", [D, TOK], F32)

    tap_d = {}
    for tname, tshape, tdt in taps:
        tap_d[tname] = nc.dram_tensor("tap_" + tname, list(tshape), tdt, kind="ExternalOutput").ap()

    p = P()
    globals()["_P_DEBUG"] = p

    with ExitStack() as ctx:
        e = ctx.enter_context

        def sems_factory(names):
            return {sn: ctx.enter_context(nc.semaphore("sem_" + sn)) for sn in names}

        idn = e(nc.sbuf_tensor("idn_sb", [128, 128], BF16))
        msk = e(nc.sbuf_tensor("msk_sb", [128, 5 * WIN], BF16))
        st = e(nc.sbuf_tensor("stats", [128, 48], F32))
        st2 = e(nc.sbuf_tensor("stats2", [128, 48], F32))
        sxp = e(nc.sbuf_tensor("sxp", [128, 12], F32))
        disc = e(nc.sbuf_tensor("disc", [128, 2048], F32))
        ones = e(nc.sbuf_tensor("ones_bf", [128, 2], BF16))
        onesr = e(nc.sbuf_tensor("onesr", [1, 128], F32))
        idnf = e(nc.sbuf_tensor("idnf_sb", [128, 128], F32))
        negm3 = e(nc.sbuf_tensor("negm3", [1, TOK], BF16))
        rmsr = e(nc.sbuf_tensor("rmsr", [128, 8], F32))
        ssq_h = e(nc.sbuf_tensor("ssq_h", [128, 16], F32))
        qr = e(nc.sbuf_tensor("qrow", [128, 64], F32))
        qs = e(nc.sbuf_tensor("qsc", [128, 64], BF16))
        qt = e(nc.sbuf_tensor("qthr", [128, 64], F32))

        RA = e(nc.sbuf_tensor("RA", [128, 16384], F32))
        RB = e(nc.sbuf_tensor("RB", [128, 24576], BF16))
        RC = e(nc.sbuf_tensor("RC", [128, 8192], F32))
        RD = e(nc.sbuf_tensor("RD", [128, 4096], F32))
        RE = e(nc.sbuf_tensor("RE", [128, 4096], F32))
        RF = e(nc.sbuf_tensor("RF", [128, 4096], BF16))
        RG = e(nc.sbuf_tensor("RG", [128, 512], F32))
        RI = e(nc.sbuf_tensor("RI", [128, 1536], BF16))

        RAb = RA[:, :].bitcast(BF16)
        RCb = RC[:, :].bitcast(BF16)
        RDb = RD[:, :].bitcast(BF16)
        REb = RE[:, :].bitcast(BF16)
        RBf = RB[:, :].bitcast(F32)
        RFf = RF[:, :].bitcast(F32)

        p.op("sync", lambda eng: eng.dma_start(out=idn[:, :], in_=idn_d), writes=["idn"], dma=True)
        p.op("sync", lambda eng: eng.dma_start(
            out=msk[:, :], in_=msk_d.rearrange("(t p) w -> p t w", p=128)), writes=["msk"], dma=True)
        wringE = [RI[:, i * 512:(i + 1) * 512] for i in range(2)]
        wqe_r = wq_d.rearrange("(c p) f -> p c f", p=128)
        for c0 in range(2):
            p.op("gpsimd", lambda eng, src=wqe_r[:, c0, 0:512], c0=c0: eng.dma_start(
                out=wringE[c0], in_=src), writes=[("wringE", c0)], dma=True)
        p.op("vector", lambda eng: eng.memset(ones[:, :], 1.0), writes=["ones"])
        p.op("vector", lambda eng: eng.memset(onesr[0:1, :], 1.0), writes=["onesr"])
        p.op("vector", lambda eng: eng.tensor_copy(out=idnf[:, :], in_=idn[:, :]),
             reads=["idn"], writes=["idnf"])

        def ln_stats(n, src, eps):
            for i in range(n):
                ap, rkeys = src(i)
                p.op("vector", lambda eng, ap=ap, i=i: eng.reduce_sum(
                    out=st[:, i:i + 1], in_=ap, axis=AX.X), reads=rkeys, writes=["stA"])
                p.op("scalar", lambda eng, ap=ap, i=i: eng.activation(
                    out=disc[:, 0:ap.shape[-1]] if False else disc[:, :], in_=ap, func=AF.Square,
                    accum_out=st[:, 16 + i:17 + i]),
                    reads=rkeys, writes=["disc", "stB"])
            p.op("vector", lambda eng, n=n: eng.tensor_scalar_mul(
                out=st[:, 32:32 + n], in0=st[:, 0:n], scalar1=1.0 / D),
                reads=["stA"], writes=["stC"])
            p.op("vector", lambda eng, n=n: eng.tensor_scalar_mul(
                out=st2[:, 0:n], in0=st[:, 16:16 + n], scalar1=1.0 / D),
                reads=["stB"], writes=["stD"])
            p.op("vector", lambda eng, n=n: eng.tensor_tensor(
                out=st2[:, 16:16 + n], in0=st[:, 32:32 + n], in1=st[:, 32:32 + n], op=ALU.mult),
                reads=["stC"], writes=["stE"])
            p.op("vector", lambda eng, n=n: eng.tensor_tensor(
                out=st2[:, 0:n], in0=st2[:, 0:n], in1=st2[:, 16:16 + n], op=ALU.subtract),
                reads=["stD", "stE"], writes=["stD"])
            p.op("vector", lambda eng, n=n, eps=eps: eng.tensor_scalar_add(
                out=st2[:, 0:n], in0=st2[:, 0:n], scalar1=eps),
                reads=["stD"], writes=["stD"])
            p.op("scalar", lambda eng, n=n: eng.activation(
                out=st2[:, 16:16 + n], in_=st2[:, 0:n], func=AF.Ln),
                reads=["stD"], writes=["stE"])
            p.op("scalar", lambda eng, n=n: eng.activation(
                out=st2[:, 32:32 + n], in_=st2[:, 16:16 + n], func=AF.Exp, scale=-0.5),
                reads=["stE"], writes=["stF"])

        x_sb = RA[:, 0:TT * D]
        x_hal = RC[:, :]
        lnT = RB[:, :]
        lnst = RF[:, :]
        sc_trp = ExitStack()
        trp = [sc_trp.enter_context(nc.psum_tensor(f"trp{i}", [128, 1024], BF16)) for i in range(2)]

        def xt(t):
            if t < TT:
                return x_sb[:, t * D:(t + 1) * D]
            return x_hal[:, (t - TT) * D:(t - TT + 1) * D]

        lnT3 = lnT.rearrange("p (c n) -> p c n", n=TOK + HALO)

        for t in range(TT):
            p.op("sync", lambda eng, t=t: eng.dma_start(out=xt(t), in_=x_d[t * 128:(t + 1) * 128, :]),
                 writes=[("x", t)], dma=True)
        for j in range(TH):
            p.op("sync", lambda eng, j=j: eng.dma_start(out=xt(TT + j), in_=xh_d[j * 128:(j + 1) * 128, :]),
                 writes=[("x", TT + j)], dma=True)

        ln_stats(TA, lambda t: (xt(t), [("x", t)]), EPS_LN)

        lnT_w = []
        for t in range(TA):
            sl = lnst[:, (t % 2) * D:(t % 2 + 1) * D]
            p.op("vector", lambda eng, t=t, sl=sl: eng.tensor_scalar(
                out=sl, in0=xt(t), scalar1=st[:, 32 + t:33 + t], scalar2=st2[:, 32 + t:33 + t],
                op0=ALU.subtract, op1=ALU.mult),
                reads=[("x", t), "stC", "stF"], writes=[("lnstage", t % 2)])
            tokcol = (HALO + t * 128) if t < TT else (t - TT) * 128
            for half in range(2):
                ps = trp[half]
                for j in range(8):
                    c = half * 8 + j
                    p.op("tensor", lambda eng, sl=sl, c=c, j=j, ps=ps: eng.transpose(
                        ps[:, j * 128:(j + 1) * 128], in_=sl[:, c * 128:(c + 1) * 128], identity=idn[:, :]),
                        reads=[("lnstage", t % 2), "idn", ("trpf", half)], writes=[("trp", half)])
                dst = lnT3[:, half * 8:half * 8 + 8, tokcol:tokcol + 128]
                wkey = ("lnT", t, half)
                lnT_w.append(wkey)
                p.op("scalar", lambda eng, ps=ps, dst=dst: eng.activation(out=dst, in_=ps[:, :], func=AF.Copy),
                     reads=[("trp", half)], writes=[wkey, ("trpf", half)])

        if "ln1T" in tap_d:
            p.op("sync", lambda eng: eng.dma_start(out=tap_d["ln1T"], in_=lnT), reads=lnT_w, dma=True)

        p.barrier()

        q_fm = RCb[:, 0:HQ * TOK]
        k_fm = RDb[:, 0:HKV * (TOK + HALO)]
        v_tm = REb[:, 0:TA * (HKV * 128)]
        wring = [RFf[:, i * 256:(i + 1) * 256].bitcast(BF16) for i in range(4)]
        v_fm = RI[:, 0:TOK + HALO]
        sc_trp.close()

        sc_q = ExitStack()
        qps = [sc_q.enter_context(nc.psum_tensor(f"qps{i}", [128, 1024], F32)) for i in range(4)]
        wq_r = wq_d.rearrange("(c p) f -> p c f", p=128)
        for hg in range(4):
            for c in range(DC):
                if hg == 0 and c < 2:
                    wap, wkey = wringE[c], ("wringE", c)
                else:
                    slot = c % 4
                    src = wq_r[:, c, hg * 512:(hg + 1) * 512]
                    p.op("gpsimd", lambda eng, src=src, slot=slot: eng.dma_start(
                        out=wring[slot], in_=src), writes=[("wring", slot)], dma=True)
                    wap, wkey = wring[slot], ("wring", slot)
                for hh in range(4):
                    for m in range(2):
                        p.op("tensor", lambda eng, wap=wap, c=c, hh=hh, m=m: eng.matmul(
                            qps[hh][:, m * 512:(m + 1) * 512],
                            lhsT=wap[:, hh * 128:(hh + 1) * 128],
                            rhs=lnT3[:, c, HALO + m * 512:HALO + (m + 1) * 512],
                            start=(c == 0), stop=(c == DC - 1)),
                            reads=[wkey]
                            + ([("qpf", hh)] if c == 0 and m == 0 else []),
                            writes=[("qp", hh)])
            for hh in range(4):
                h = hg * 4 + hh
                p.op("scalar", lambda eng, hh=hh, h=h: eng.activation(
                    out=q_fm[:, h * TOK:(h + 1) * TOK], in_=qps[hh][:, :],
                    func=AF.Copy, scale=1.0 / 128),
                    reads=[("qp", hh)], writes=[("q", h), ("qpf", hh)])
        sc_q.close()

        qpf_all = [("qpf", hh) for hh in range(4)]
        sc_kv = ExitStack()
        kvps = [sc_kv.enter_context(nc.psum_tensor(f"kvps{i}", [128, 1536], F32)) for i in range(2)]
        trp2 = [sc_kv.enter_context(nc.psum_tensor(f"trpv{i}", [128, 1024], BF16)) for i in range(2)]
        wk_r = wk_d.rearrange("(c p) f -> p c f", p=128)
        wv_r = wv_d.rearrange("(c p) f -> p c f", p=128)
        for kvg in range(2):
            for vv, wsrc in ((0, wk_r), (1, wv_r)):
                for c in range(DC):
                    slot = c % 4
                    src = wsrc[:, c, kvg * 256:(kvg + 1) * 256]
                    p.op("gpsimd", lambda eng, src=src, slot=slot: eng.dma_start(
                        out=wring[slot][:, 0:256], in_=src), writes=[("wring", slot)], dma=True)
                    for j in range(2):
                        for m in range(3):
                            p.op("tensor", lambda eng, slot=slot, c=c, j=j, m=m: eng.matmul(
                                kvps[j][:, m * 512:(m + 1) * 512],
                                lhsT=wring[slot][:, j * 128:(j + 1) * 128],
                                rhs=lnT3[:, c, m * 512:(m + 1) * 512],
                                start=(c == 0), stop=(c == DC - 1)),
                                reads=[("wring", slot)]
                                + (qpf_all + [("kpf", j)] if c == 0 and m == 0 else []),
                                writes=[("kp", j)])
                for j in range(2):
                    kv = kvg * 2 + j
                    if vv == 0:
                        b = kv * (TOK + HALO)
                        p.op("scalar", lambda eng, j=j, b=b: eng.activation(
                            out=k_fm[:, b:b + 1536], in_=kvps[j][:, :], func=AF.Copy),
                            reads=[("kp", j)], writes=[("k", kv), ("kpf", j)])
                    else:
                        p.op("scalar", lambda eng, j=j: eng.activation(
                            out=v_fm[:, :], in_=kvps[j][:, :], func=AF.Copy),
                            reads=[("kp", j), ("vff",)], writes=[("vf",), ("kpf", j)])
                        for t in range(TA):
                            half = t % 2
                            ps = trp2[half]
                            p.op("tensor", lambda eng, t=t, ps=ps: eng.transpose(
                                ps[:, 0:128], in_=v_fm[:, t * 128:(t + 1) * 128], identity=idn[:, :]),
                                reads=[("vf",), "idn", ("trpf2", half)], writes=[("trp2", half)])
                            p.op("scalar", lambda eng, t=t, kv=kv, ps=ps: eng.activation(
                                out=v_tm[:, t * (HKV * 128) + kv * 128: t * (HKV * 128) + (kv + 1) * 128],
                                in_=ps[:, 0:128], func=AF.Copy),
                                reads=[("trp2", half)],
                                writes=[("v", kv, t), ("trpf2", half)] + ([("vff",)] if t == TA - 1 else []))

        if "qfm" in tap_d:
            p.op("sync", lambda eng: eng.dma_start(out=tap_d["qfm"], in_=q_fm),
                 reads=[("q", h) for h in range(HQ)], dma=True)
        if "kfm" in tap_d:
            p.op("sync", lambda eng: eng.dma_start(out=tap_d["kfm"], in_=k_fm),
                 reads=[("k", kv) for kv in range(HKV)], dma=True)
        if "vtm" in tap_d:
            p.op("sync", lambda eng: eng.dma_start(out=tap_d["vtm"], in_=v_tm),
                 reads=[("v", kv, j) for kv in range(HKV) for j in range(TA)], dma=True)

        p.barrier()
        sc_kv.close()

        att_fm = RB[:, 0:HQ * TOK]
        pf = [RFf[:, i * WIN:(i + 1) * WIN] for i in range(2)]
        pb = [RF[:, 2560 + i * WIN: 2560 + (i + 1) * WIN] for i in range(2)]
        pts = [REb[:, 6144 + i * WIN: 6144 + (i + 1) * WIN] for i in range(2)]
        sc_a2 = ExitStack()
        sps = [sc_a2.enter_context(nc.psum_tensor(f"sps{i}", [128, 1024], F32)) for i in range(2)]
        ptp = [sc_a2.enter_context(nc.psum_tensor(f"ptp{i}", [128, WIN], BF16)) for i in range(2)]
        aps = [sc_a2.enter_context(nc.psum_tensor(f"aps{i}", [128, 128], F32)) for i in range(2)]

        mskv = msk[:, :].rearrange("p (t w) -> p t w", w=WIN)
        att_w = [("att", h, t) for h in range(HQ) for t in range(TT)]
        pairs = [(h, t) for h in range(HQ) for t in range(TT)]

        whalf = [RB[:, 16384:20480], RB[:, 20480:24576]]
        wo_r = wo_d.rearrange("(c p) f -> p c f", p=128)

        def wo_load(dsl, hf):
            src = wo_r[:, hf * 8:(hf + 1) * 8, dsl * 512:(dsl + 1) * 512]
            p.op("gpsimd", lambda eng, src=src, hf=hf: eng.dma_start(
                out=whalf[hf].rearrange("p (c f) -> p c f", f=512), in_=src),
                writes=[("wor", dsl, hf)] + ([("wor", dsl - 1, hf)] if dsl > 0 else []),
                dma=True)

        wo_load(0, 0)
        wo_load(0, 1)
        waves = [pairs[b0:b0 + 2] for b0 in range(0, len(pairs), 2)]

        def b2_scores(wv):
            for r, (h, t) in enumerate(wv):
                kv = h // 4
                kbase = kv * (TOK + HALO) + t * 128
                qsl = q_fm[:, h * TOK + t * 128: h * TOK + (t + 1) * 128]
                ps = sps[r]
                p.op("tensor", lambda eng, ps=ps, qsl=qsl, kbase=kbase: eng.matmul(
                    ps[:, 0:512], lhsT=qsl, rhs=k_fm[:, kbase:kbase + 512], start=True, stop=True),
                    reads=[("spsf", r)], writes=[("sps", r)])
                p.op("tensor", lambda eng, ps=ps, qsl=qsl, kbase=kbase: eng.matmul(
                    ps[:, 512:640], lhsT=qsl, rhs=k_fm[:, kbase + 512:kbase + 640], start=True, stop=True),
                    reads=[("spsf", r)], writes=[("sps", r)])

        b2_scores(waves[0])
        for wi, wv in enumerate(waves):
            for r, (h, t) in enumerate(wv):
                p.op("vector", lambda eng, ps=sps[r], t=t: eng.tensor_tensor(
                    out=ps[:, 0:WIN], in0=ps[:, 0:WIN], in1=mskv[:, min(t, 4), :], op=ALU.add),
                    reads=[("sps", r), "msk"], writes=[("sps", r)])
            for r, (h, t) in enumerate(wv):
                p.op("scalar", lambda eng, ps=sps[r], pfr=pf[r], r=r: eng.activation(
                    out=pfr, in_=ps[:, 0:WIN], func=AF.Exp,
                    accum_out=sxp[:, 3 + r:4 + r]),
                    reads=[("sps", r)], writes=[("pf", r), ("sume", r), ("spsf", r)])
            if wi + 1 < len(waves):
                b2_scores(waves[wi + 1])
            for r, (h, t) in enumerate(wv):
                p.op("vector", lambda eng, r=r: eng.reciprocal(
                    out=sxp[:, 6 + r:7 + r], in_=sxp[:, 3 + r:4 + r]),
                    reads=[("sume", r)], writes=[("rsum", r)])
                p.op("vector", lambda eng, pfr=pf[r], pbr=pb[r], r=r: eng.tensor_scalar_mul(
                    out=pbr, in0=pfr, scalar1=sxp[:, 6 + r:7 + r]),
                    reads=[("pf", r), ("rsum", r)], writes=[("pb", r)])
            for r, (h, t) in enumerate(wv):
                for cch in range(5):
                    p.op("tensor", lambda eng, pbr=pb[r], pp_=ptp[r], cch=cch: eng.transpose(
                        pp_[:, cch * 128:(cch + 1) * 128], in_=pbr[:, cch * 128:(cch + 1) * 128],
                        identity=idn[:, :]),
                        reads=[("pb", r), "idn", ("ptpf", r)], writes=[("ptp", r)])
                if r == 0:
                    p.op("scalar", lambda eng, ptr=pts[r], pp_=ptp[r]: eng.activation(
                        out=ptr, in_=pp_[:, :], func=AF.Copy),
                        reads=[("ptp", r)], writes=[("pts", r), ("ptpf", r)])
                else:
                    p.op("vector", lambda eng, ptr=pts[r], pp_=ptp[r]: eng.tensor_copy(
                        out=ptr, in_=pp_[:, :]),
                        reads=[("ptp", r)], writes=[("pts", r), ("ptpf", r)])
            for r, (h, t) in enumerate(wv):
                kv = h // 4
                for cch in range(5):
                    jj = t + cch
                    vsl = v_tm[:, jj * (HKV * 128) + kv * 128: jj * (HKV * 128) + (kv + 1) * 128]
                    p.op("tensor", lambda eng, ptr=pts[r], ap_=aps[r], vsl=vsl, cch=cch: eng.matmul(
                        ap_[:, :], lhsT=vsl, rhs=ptr[:, cch * 128:(cch + 1) * 128],
                        start=(cch == 0), stop=(cch == 4)),
                        reads=[("pts", r)] + ([("apsf", r)] if cch == 0 else []),
                        writes=[("aps", r)])
                p.op("scalar", lambda eng, h=h, t=t, ap_=aps[r]: eng.activation(
                    out=att_fm[:, h * TOK + t * 128: h * TOK + (t + 1) * 128], in_=ap_[:, :], func=AF.Copy),
                    reads=[("aps", r)], writes=[("att", h, t), ("apsf", r)])

        if "att" in tap_d:
            p.op("sync", lambda eng: eng.dma_start(out=tap_d["att"], in_=att_fm), reads=att_w, dma=True)

        p.barrier()
        sc_a2.close()

        def ano(t):
            if t < 4:
                return RC[:, t * D:(t + 1) * D]
            if t < 6:
                return RD[:, (t - 4) * D:(t - 3) * D]
            return RE[:, (t - 6) * D:(t - 5) * D]

        sc_wo = ExitStack()
        wop = [sc_wo.enter_context(nc.psum_tensor(f"wop{i}", [128, 512], F32)) for i in range(2)]

        gi = 0
        for dsl in range(4):
            for t in range(TT):
                me = gi % 2
                ps = wop[me]
                for c in range(DC):
                    p.op("tensor", lambda eng, ps=ps, c=c, t=t: eng.matmul(
                        ps[:, :], lhsT=att_fm[:, c * TOK + t * 128: c * TOK + (t + 1) * 128],
                        rhs=whalf[c // 8][:, (c % 8) * 512:(c % 8 + 1) * 512],
                        start=(c == 0), stop=(c == DC - 1)),
                        reads=[("wor", dsl, c // 8)] + ([("wopf", me)] if c == 0 else []),
                        writes=[("wop", me)])
                p.op("vector", lambda eng, ps=ps, t=t, dsl=dsl: eng.tensor_copy(
                    out=ano(t)[:, dsl * 512:(dsl + 1) * 512], in_=ps[:, :]),
                    reads=[("wop", me)], writes=[("ao", t), ("wopf", me)])
                gi += 1
            if dsl + 1 < 4:
                wo_load(dsl + 1, 0)
                wo_load(dsl + 1, 1)

        if "attno" in tap_d:
            for t in range(TT):
                p.op("sync", lambda eng, t=t: eng.dma_start(
                    out=tap_d["attno"][t * 128:(t + 1) * 128, :], in_=ano(t)),
                    reads=[("ao", t)], dma=True)

        ln_stats(TT, lambda t: (ano(t), [("ao", t)]), EPS_LN)
        for t in range(TT):
            p.op("vector", lambda eng, t=t: eng.tensor_scalar(
                out=ano(t), in0=ano(t), scalar1=st[:, 32 + t:33 + t], scalar2=st2[:, 32 + t:33 + t],
                op0=ALU.subtract, op1=ALU.mult),
                reads=[("ao", t), "stC", "stF"], writes=[("ao", t)])
            p.op("vector", lambda eng, t=t: eng.tensor_tensor(
                out=xt(t), in0=xt(t), in1=ano(t), op=ALU.add),
                reads=[("x", t), ("ao", t)], writes=[("x", t)])
            p.op("scalar", lambda eng, t=t: eng.dma_start(out=x2_d[t * 128:(t + 1) * 128, :], in_=xt(t)),
                 reads=[("x", t)], writes=[("x2d", t)], dma=True)

        if "x2" in tap_d:
            p.op("sync", lambda eng: eng.dma_start(out=tap_d["x2"], in_=x_sb),
                 reads=[("x", t) for t in range(TT)], dma=True)

        p.barrier()
        sc_wo.close()

        x2T = RCb[:, :]
        x2st = RF[:, :]
        sq_sb = RD[:, 0:1024]
        sq_bf = RDb[:, 0:1024]
        r3row = RD[:, 1024:2048]
        sc_x2s = ExitStack()
        tr2 = [sc_x2s.enter_context(nc.psum_tensor(f"tr2_{i}", [128, 1024], BF16)) for i in range(2)]
        csp = sc_x2s.enter_context(nc.psum_tensor("csp", [128, 1024], F32))
        csq2 = sc_x2s.enter_context(nc.psum_tensor("csq2", [128, 1024], F32))

        x2T3 = x2T.rearrange("p (c n) -> p c n", n=TOK)
        for t in range(TT):
            sl = x2st[:, (t % 2) * D:(t % 2 + 1) * D]
            p.op("vector", lambda eng, t=t, sl=sl: eng.tensor_copy(out=sl, in_=xt(t)),
                 reads=[("x", t)], writes=[("x2st", t % 2)])
            for half in range(2):
                ps = tr2[half]
                for j in range(8):
                    c = half * 8 + j
                    p.op("tensor", lambda eng, sl=sl, c=c, j=j, ps=ps: eng.transpose(
                        ps[:, j * 128:(j + 1) * 128], in_=sl[:, c * 128:(c + 1) * 128], identity=idn[:, :]),
                        reads=[("x2st", t % 2), "idn", ("tr2f", half)], writes=[("tr2", half)])
                dst = x2T3[:, half * 8:half * 8 + 8, t * 128:(t + 1) * 128]
                p.op("scalar", lambda eng, ps=ps, dst=dst: eng.activation(out=dst, in_=ps[:, :], func=AF.Copy),
                     reads=[("tr2", half)], writes=[("x2T", t, half), ("tr2f", half)])

        x2t_r = [("x2T", t, hf) for t in range(TT) for hf in range(2)]
        for c in range(DC):
            for m in range(2):
                p.op("tensor", lambda eng, c=c, m=m: eng.matmul(
                    csp[0:1, m * 512:(m + 1) * 512], lhsT=ones[:, 0:1],
                    rhs=x2T3[:, c, m * 512:(m + 1) * 512],
                    start=(c == 0), stop=(c == DC - 1)),
                    reads=x2t_r + ["ones"], writes=[("csp",)])
        for c in range(DC):
            p.op("scalar", lambda eng, c=c: eng.activation(out=sq_bf, in_=x2T3[:, c, :], func=AF.Square),
                 reads=x2t_r + [("sqf",)], writes=[("sq",)])
            for m in range(2):
                p.op("tensor", lambda eng, c=c, m=m: eng.matmul(
                    csq2[0:1, m * 512:(m + 1) * 512], lhsT=ones[:, 0:1],
                    rhs=sq_bf[:, m * 512:(m + 1) * 512],
                    start=(c == 0), stop=(c == DC - 1)),
                    reads=[("sq",)], writes=[("csq2",), ("sqf",)])
        p.op("vector", lambda eng: eng.tensor_scalar_mul(
            out=negm3[0:1, :], in0=csp[0:1, 0:TOK], scalar1=-1.0 / D),
            reads=[("csp",)], writes=["negm3"])
        p.op("vector", lambda eng: eng.tensor_scalar_mul(
            out=r3row[0:1, 0:TOK], in0=csp[0:1, 0:TOK], scalar1=1.0 / D),
            reads=[("csp",)], writes=[("m3r",)])
        p.op("vector", lambda eng: eng.tensor_tensor(
            out=r3row[0:1, 0:TOK], in0=r3row[0:1, 0:TOK], in1=r3row[0:1, 0:TOK], op=ALU.mult),
            reads=[("m3r",)], writes=[("m3r",)])
        p.op("vector", lambda eng: eng.tensor_scalar(
            out=r3row[1:2, 0:TOK] if False else sq_sb[0:1, 0:TOK], in0=csq2[0:1, 0:TOK],
            scalar1=1.0 / D, scalar2=0.0, op0=ALU.mult, op1=ALU.add),
            reads=[("csq2",), ("sq",)], writes=[("e2r",)])
        p.op("vector", lambda eng: eng.tensor_tensor(
            out=r3row[0:1, 0:TOK], in0=sq_sb[0:1, 0:TOK], in1=r3row[0:1, 0:TOK], op=ALU.subtract),
            reads=[("e2r",), ("m3r",)], writes=[("v3r",)])
        p.op("vector", lambda eng: eng.tensor_scalar_add(
            out=r3row[0:1, 0:TOK], in0=r3row[0:1, 0:TOK], scalar1=EPS_LN),
            reads=[("v3r",)], writes=[("v3r",)])
        p.op("scalar", lambda eng: eng.activation(out=r3row[0:1, 0:TOK], in_=r3row[0:1, 0:TOK], func=AF.Ln),
             reads=[("v3r",)], writes=[("v3r",)])
        p.op("scalar", lambda eng: eng.activation(out=r3row[0:1, 0:TOK], in_=r3row[0:1, 0:TOK],
                                                  func=AF.Exp, scale=-0.5),
             reads=[("v3r",)], writes=[("r3row",)])
        nmr3row = RD[0:1, 2048:2048 + TOK]
        p.op("vector", lambda eng: eng.tensor_tensor(
            out=nmr3row, in0=r3row[0:1, 0:TOK], in1=negm3[0:1, :], op=ALU.mult),
            reads=[("r3row",), "negm3"], writes=[("nmr3r",)])
        r3full_bf = RDb[:, 0:TOK]
        nmr3full_bf = RDb[:, TOK:2 * TOK]
        for m in range(2):
            p.op("tensor", lambda eng, m=m: eng.matmul(
                csp[:, m * 512:(m + 1) * 512], lhsT=onesr[0:1, :],
                rhs=r3row[0:1, m * 512:(m + 1) * 512], start=True, stop=True),
                reads=[("r3row",), "onesr", ("csp",)], writes=[("csp",)])
            p.op("tensor", lambda eng, m=m: eng.matmul(
                csq2[:, m * 512:(m + 1) * 512], lhsT=onesr[0:1, :],
                rhs=nmr3row[0:1, m * 512:(m + 1) * 512], start=True, stop=True),
                reads=[("nmr3r",), "onesr", ("csq2",), ("sq",)], writes=[("csq2",)])
        p.op("vector", lambda eng: eng.tensor_copy(out=r3full_bf, in_=csp[:, 0:TOK]),
             reads=[("csp",)], writes=["r3full", ("sq",), ("e2r",)])
        p.op("vector", lambda eng: eng.tensor_copy(out=nmr3full_bf, in_=csq2[:, 0:TOK]),
             reads=[("csq2",)], writes=["nmr3full", ("e2r",)])
        for c in range(DC):
            p.op("vector", lambda eng, c=c: eng.tensor_tensor(
                out=x2T3[:, c, :], in0=x2T3[:, c, :], in1=r3full_bf, op=ALU.mult),
                reads=["r3full"], writes=[("ln3m", c)] + [("x2T", t, c // 8) for t in range(TT)])
            p.op("vector", lambda eng, c=c: eng.tensor_tensor(
                out=x2T3[:, c, :], in0=x2T3[:, c, :], in1=nmr3full_bf, op=ALU.add),
                reads=[("ln3m", c), "nmr3full"], writes=[("ln3", c)])

        p.barrier()
        sc_x2s.close()

        h_fm = RAb[:, :]
        RING = 4
        wsl = [RB[:, i * 4096:(i + 1) * 4096] for i in range(RING)]
        qsc = []
        for k_ in range(4):
            base = RB[:, 16384 + k_ * 4096: 16384 + (k_ + 1) * 4096] if k_ < 2 \
                else RDb[:, (k_ - 2) * 4096:(k_ - 1) * 4096]
            qsc.append(dict(
                wnat=base[:, 0:1024], qa=base[:, 1024:2048],
                qsg=base[:, 2048:3072], qms=base[:, 3072:4096],
                qr=qr[:, k_ * 16:k_ * 16 + 16], qs=qs[:, k_ * 16:k_ * 16 + 16],
                qt=qt[:, k_ * 16:k_ * 16 + 16]))
        sgf = [REb[:, i * 1024:(i + 1) * 1024] for i in range(2)]
        hsq = REb[:, 4096:5120]
        ssqrow = RE[0:1, 2048:3072]

        h_fm3 = h_fm.rearrange("p (c n) -> p c n", n=TOK)
        qpar = [0]

        def ano2(t):
            if t < 4:
                return RC[:, t * D:(t + 1) * D]
            if t < 6:
                return RBf[:, 8192 + (t - 4) * D: 8192 + (t - 3) * D]
            return RE[:, (t - 6) * D:(t - 5) * D]

        def emit_quant_wave(jobs, sets, sid):
            n = len(jobs)
            rb = sid % RING
            for i in range(n):
                k_, (src, jj, mk) = sets[i], jobs[i]
                p.op("gpsimd", lambda eng, a=src, w=qsc[k_]["wnat"]: eng.dma_start(out=w, in_=a),
                     writes=[("wnat", k_)], dma=True)
            for i in range(n):
                k_ = sets[i]
                p.op("vector", lambda eng, w=qsc[k_]["wnat"], q=qsc[k_]["qr"]: eng.reduce_sum(
                    out=q, in_=w.rearrange("p (g s) -> p g s", s=64),
                    axis=AX.X, apply_absolute_value=True),
                    reads=[("wnat", k_)], writes=[("qr", k_)])
            for i in range(n):
                k_ = sets[i]
                p.op("scalar", lambda eng, w=qsc[k_]["wnat"], q=qsc[k_]["qa"]: eng.activation(
                    out=q, in_=w, func=AF.Abs),
                    reads=[("wnat", k_)], writes=[("qa", k_)])
            for i in range(n):
                k_ = sets[i]
                p.op("scalar", lambda eng, w=qsc[k_]["wnat"], q=qsc[k_]["qsg"]: eng.activation(
                    out=q, in_=w, func=AF.Sign),
                    reads=[("wnat", k_)], writes=[("qsg", k_)])
            for i in range(n):
                k_ = sets[i]
                p.op("vector", lambda eng, a=qsc[k_]["qr"], b=qsc[k_]["qs"]: eng.tensor_scalar(
                    out=b, in0=a, scalar1=1.0 / 64, scalar2=1e-8, op0=ALU.mult, op1=ALU.max),
                    reads=[("qr", k_)], writes=[("qs", k_)])
                p.op("vector", lambda eng, a=qsc[k_]["qs"], b=qsc[k_]["qt"]: eng.tensor_scalar_mul(
                    out=b, in0=a, scalar1=THR),
                    reads=[("qs", k_)], writes=[("qt", k_)])
            for i in range(n):
                k_ = sets[i]
                sc = qsc[k_]
                a3 = sc["qa"].rearrange("p (g s) -> p g s", s=64)
                m3 = sc["qms"].rearrange("p (g s) -> p g s", s=64)
                tb = sc["qt"].unsqueeze(2).broadcast_to((128, 16, 64))
                p.op("vector", lambda eng, a3=a3, m3=m3, tb=tb: eng.tensor_tensor(
                    out=m3, in0=a3, in1=tb, op=ALU.is_gt),
                    reads=[("qa", k_), ("qt", k_)], writes=[("qms", k_)])
            for i in range(n):
                k_ = sets[i]
                sc = qsc[k_]
                m3 = sc["qms"].rearrange("p (g s) -> p g s", s=64)
                sb = sc["qs"].unsqueeze(2).broadcast_to((128, 16, 64))
                p.op("vector", lambda eng, m3=m3, sb=sb: eng.tensor_tensor(
                    out=m3, in0=m3, in1=sb, op=ALU.mult),
                    reads=[("qms", k_), ("qs", k_)], writes=[("qms", k_)])
            for i in range(n):
                k_ = sets[i]
                sc = qsc[k_]
                p.op("gpsimd", lambda eng, a=sc["qa"], m=sc["qms"], g=sc["qsg"]: eng.tensor_tensor(
                    out=a, in0=m, in1=g, op=ALU.mult),
                    reads=[("qms", k_), ("qsg", k_)], writes=[("qa", k_)])
            for i in range(n):
                k_, (src, jj, mk) = sets[i], jobs[i]
                tp = qpar[0] % 2
                qpar[0] += 1
                ps = trcs[tp]
                for j in range(8):
                    p.op("tensor", lambda eng, j=j, ps=ps, q=qsc[k_]["qa"]: eng.transpose(
                        ps[:, j * 128:(j + 1) * 128], in_=q[:, j * 128:(j + 1) * 128],
                        identity=idn[:, :]),
                        reads=[("qa", k_), "idn", ("trcf", tp)], writes=[("trc", tp)])
                out_ap, three_d = mk(rb)
                in_ap = ps[:, :].rearrange("p (j x) -> p j x", x=128) if three_d else ps[:, :]
                wkeys = [("wb", sid, jj)]
                if sid >= RING:
                    wkeys += [("wb", sid - RING, j2) for j2 in range(4)]
                p.op("scalar", lambda eng, out_ap=out_ap, in_ap=in_ap: eng.activation(
                    out=out_ap, in_=in_ap, func=AF.Copy),
                    reads=[("trc", tp), ("qa", k_)], writes=wkeys + [("trcf", tp)])

        qjobs = []
        for hh in range(2):
            for pair in range(16):
                for gu, base in ((0, 0), (1, H)):
                    row0 = base + hh * HHALF + pair * 256
                    for jj, (sub, colh) in enumerate(
                            [(s, c) for s in range(2) for c in range(2)]):
                        def mk_gu(rb, colh=colh, sub=sub):
                            o = wsl[rb].rearrange("p (c x) -> p c x", x=256)
                            return (o[:, colh * 8:(colh + 1) * 8,
                                      sub * 128:(sub + 1) * 128], True)
                        qjobs.append((wgu_d[row0 + sub * 128: row0 + (sub + 1) * 128,
                                            colh * 1024:(colh + 1) * 1024], jj, mk_gu))
            for dc in range(DC):
                for pcl in range(4):
                    def mk_wp(rb, pcl=pcl):
                        return (wsl[rb][:, pcl * 1024:(pcl + 1) * 1024], False)
                    qjobs.append((wp_d[dc * 128:(dc + 1) * 128,
                                       hh * HHALF + pcl * 1024: hh * HHALF + (pcl + 1) * 1024],
                                  pcl, mk_wp))
        NSLICE = len(qjobs) // 4
        qnext = [0]
        qdma_done = set()
        SETS4 = [0, 1, 2, 3]

        def emit_qdma_g(jobs, sets):
            for i, (src, jj, mk) in enumerate(jobs):
                k_ = sets[i]
                p.op("gpsimd", lambda eng, a=src, w=qsc[k_]["wnat"]: eng.dma_start(out=w, in_=a),
                     writes=[("wnat", k_)], dma=True)

        def emit_qdma(sid):
            emit_qdma_g(qjobs[sid * 4:(sid + 1) * 4], SETS4)
            qdma_done.add(sid)

        def emit_qRED_g(jobs, sets):
            for i in range(len(jobs)):
                k_ = sets[i]
                p.op("vector", lambda eng, w=qsc[k_]["wnat"], q=qsc[k_]["qr"]: eng.reduce_sum(
                    out=q, in_=w.rearrange("p (g s) -> p g s", s=64),
                    axis=AX.X, apply_absolute_value=True),
                    reads=[("wnat", k_)], writes=[("qr", k_)])

        def emit_qTS_g(jobs, sets):
            for i in range(len(jobs)):
                k_ = sets[i]
                p.op("vector", lambda eng, a=qsc[k_]["qr"], b=qsc[k_]["qs"]: eng.tensor_scalar(
                    out=b, in0=a, scalar1=1.0 / 64, scalar2=1e-8, op0=ALU.mult, op1=ALU.max),
                    reads=[("qr", k_)], writes=[("qs", k_)])
                p.op("vector", lambda eng, a=qsc[k_]["qs"], b=qsc[k_]["qt"]: eng.tensor_scalar_mul(
                    out=b, in0=a, scalar1=THR),
                    reads=[("qs", k_)], writes=[("qt", k_)])

        def emit_qA1_g(jobs, sets):
            emit_qRED_g(jobs, sets)
            emit_qTS_g(jobs, sets)

        def emit_qA2_g(jobs, sets):
            n = len(jobs)
            for i in range(n):
                k_ = sets[i]
                p.op("scalar", lambda eng, w=qsc[k_]["wnat"], q=qsc[k_]["qa"]: eng.activation(
                    out=q, in_=w, func=AF.Abs),
                    reads=[("wnat", k_)], writes=[("qa", k_)])
            for i in range(n):
                k_ = sets[i]
                p.op("scalar", lambda eng, w=qsc[k_]["wnat"], q=qsc[k_]["qsg"]: eng.activation(
                    out=q, in_=w, func=AF.Sign),
                    reads=[("wnat", k_)], writes=[("qsg", k_)])

        def emit_qA(sid):
            jobs = qjobs[sid * 4:(sid + 1) * 4]
            emit_qA1_g(jobs, SETS4)
            emit_qA2_g(jobs, SETS4)

        def emit_qB1_g(jobs, sets):
            n = len(jobs)
            for i in range(n):
                k_ = sets[i]
                sc = qsc[k_]
                a3 = sc["qa"].rearrange("p (g s) -> p g s", s=64)
                m3 = sc["qms"].rearrange("p (g s) -> p g s", s=64)
                tb = sc["qt"].unsqueeze(2).broadcast_to((128, 16, 64))
                p.op("vector", lambda eng, a3=a3, m3=m3, tb=tb: eng.tensor_tensor(
                    out=m3, in0=a3, in1=tb, op=ALU.is_gt),
                    reads=[("qa", k_), ("qt", k_)], writes=[("qms", k_)])
            for i in range(n):
                k_ = sets[i]
                sc = qsc[k_]
                m3 = sc["qms"].rearrange("p (g s) -> p g s", s=64)
                sb = sc["qs"].unsqueeze(2).broadcast_to((128, 16, 64))
                p.op("vector", lambda eng, m3=m3, sb=sb: eng.tensor_tensor(
                    out=m3, in0=m3, in1=sb, op=ALU.mult),
                    reads=[("qms", k_), ("qs", k_)], writes=[("qms", k_)])
            for i in range(n):
                k_ = sets[i]
                sc = qsc[k_]
                p.op("gpsimd", lambda eng, a=sc["qa"], m=sc["qms"], g=sc["qsg"]: eng.tensor_tensor(
                    out=a, in0=m, in1=g, op=ALU.mult),
                    reads=[("qms", k_), ("qsg", k_)], writes=[("qa", k_)])

        def emit_qB1(sid):
            emit_qB1_g(qjobs[sid * 4:(sid + 1) * 4], SETS4)

        def emit_qB2_g(jobs, sets, sid):
            rb = sid % RING
            for i in range(len(jobs)):
                k_, (src, jj, mk) = sets[i], jobs[i]
                tp = qpar[0] % 2
                qpar[0] += 1
                ps = trcs[tp]
                for j in range(8):
                    p.op("tensor", lambda eng, j=j, ps=ps, q=qsc[k_]["qa"]: eng.transpose(
                        ps[:, j * 128:(j + 1) * 128], in_=q[:, j * 128:(j + 1) * 128],
                        identity=idn[:, :]),
                        reads=[("qa", k_), "idn", ("trcf", tp)], writes=[("trc", tp)])
                out_ap, three_d = mk(rb)
                in_ap = ps[:, :].rearrange("p (j x) -> p j x", x=128) if three_d else ps[:, :]
                wkeys = [("wb", sid, jj)]
                if sid >= RING:
                    wkeys += [("wb", sid - RING, j2) for j2 in range(4)]
                p.op("scalar", lambda eng, out_ap=out_ap, in_ap=in_ap: eng.activation(
                    out=out_ap, in_=in_ap, func=AF.Copy),
                    reads=[("trc", tp), ("qa", k_)], writes=wkeys + [("trcf", tp)])

        def emit_qB2(sid):
            emit_qB2_g(qjobs[sid * 4:(sid + 1) * 4], SETS4, sid)

        RLIM = NSLICE - 12
        qa_pend = [False]

        def pump_half_a():
            if qa_pend[0] or qnext[0] >= min(NSLICE, RLIM):
                return
            sid = qnext[0]
            if sid not in qdma_done:
                emit_qdma(sid)
            emit_qA(sid)
            if sid + 1 < min(NSLICE, RLIM):
                emit_qdma(sid + 1)
            qa_pend[0] = True

        qb1_done = [False]

        def pump_half_b1():
            if qa_pend[0] and not qb1_done[0]:
                emit_qB1(qnext[0])
                qb1_done[0] = True

        def pump_half_b():
            if qa_pend[0]:
                pump_half_b1()
                emit_qB2(qnext[0])
                qnext[0] += 1
                qa_pend[0] = False
                qb1_done[0] = False

        def pump_to(upto):
            while qnext[0] < min(upto, RLIM):
                pump_half_a()
                pump_half_b()

        RSETS = [2, 3]
        NU = (NSLICE - RLIM) * 2
        rseq = [("a1", 0), ("a2", 0), ("b1", 0), ("a1", 1)]
        for u_ in range(1, NU):
            rseq += [("b2", u_ - 1), ("a2", u_), ("b1", u_)]
            if u_ + 1 < NU:
                rseq += [("a1", u_ + 1)]
        rseq += [("b2", NU - 1)]
        rsi = [0]
        rdone = {"b2": 0}

        def runit_jobs(u):
            sid = RLIM + u // 2
            o = sid * 4 + (u % 2) * 2
            return qjobs[o:o + 2], sid

        def rstep():
            if rsi[0] >= len(rseq):
                return
            stg, u = rseq[rsi[0]]
            rsi[0] += 1
            jobs, sid = runit_jobs(u)
            if stg == "a1":
                emit_qdma_g(jobs, RSETS)
            elif stg == "a2":
                emit_qA2_g(jobs, RSETS)
            elif stg == "b1":
                emit_qRED_g(jobs, RSETS)
                emit_qTS_g(jobs, RSETS)
                emit_qB1_g(jobs, RSETS)
            else:
                emit_qB2_g(jobs, RSETS, sid)
                rdone["b2"] += 1

        for hhalf in range(2):
            sc_c1 = ExitStack()
            pps = [sc_c1.enter_context(nc.psum_tensor(f"pps{hhalf}_{i}", [128, 1024], F32)) for i in range(2)]
            trcs = [sc_c1.enter_context(nc.psum_tensor(f"trc{hhalf}_{i}", [128, 1024], BF16)) for i in range(2)]
            sqp = sc_c1.enter_context(nc.psum_tensor(f"sqp{hhalf}", [128, 1024], F32))

            pump_to(hhalf * (NSLICE // 2) + RING)
            gi = 0
            for pair in range(16):
                pump_half_a()
                for gu in range(2):
                    sid = hhalf * (NSLICE // 2) + 2 * pair + gu
                    assert qnext[0] > sid, ("quant not emitted", sid)
                    rb = sid % RING
                    wv = wsl[rb].rearrange("p (c x) -> p c x", x=256)
                    mes = []
                    for hcl in range(2):
                        me = gi % 2
                        mes.append(me)
                        ps = pps[me]
                        for c in range(DC):
                            for m in range(2):
                                p.op("tensor", lambda eng, ps=ps, c=c, m=m, hcl=hcl, wv=wv: eng.matmul(
                                    ps[:, m * 512:(m + 1) * 512],
                                    lhsT=wv[:, c, hcl * 128:(hcl + 1) * 128],
                                    rhs=x2T3[:, c, m * 512:(m + 1) * 512],
                                    start=(c == 0), stop=(c == DC - 1)),
                                    reads=[("wb", sid, hcl * 2 + c // 8)]
                                    + ([("ppsf", me)] if c == 0 and m == 0 else []),
                                    writes=[("pps", me)])
                        gi += 1
                    pump_half_b1()
                    for hcl in range(2):
                        me = mes[hcl]
                        ps = pps[me]
                        if gu == 0:
                            p.op("scalar", lambda eng, ps=ps, hcl=hcl: eng.activation(
                                out=sgf[hcl], in_=ps[:, :], func=AF.Silu),
                                reads=[("pps", me)], writes=[("sg", hcl), ("ppsf", me)])
                        else:
                            ch = pair * 2 + hcl
                            p.op("vector", lambda eng, ps=ps, ch=ch, hcl=hcl: eng.tensor_tensor(
                                out=h_fm3[:, ch, :], in0=sgf[hcl], in1=ps[:, :], op=ALU.mult),
                                reads=[("sg", hcl), ("pps", me)], writes=[("hf", ch), ("ppsf", me)])
                            p.op("scalar", lambda eng, ch=ch: eng.activation(
                                out=hsq, in_=h_fm3[:, ch, :], func=AF.Square),
                                reads=[("hf", ch), ("hsqf",)], writes=[("hsq",)])
                            first = (pair == 0 and hcl == 0)
                            last = (pair == 15 and hcl == 1)
                            for m in range(2):
                                p.op("tensor", lambda eng, m=m, first=first, last=last: eng.matmul(
                                    sqp[0:1, m * 512:(m + 1) * 512], lhsT=ones[:, 0:1],
                                    rhs=hsq[:, m * 512:(m + 1) * 512],
                                    start=first, stop=last),
                                    reads=[("hsq",), "ones"], writes=[("sqp",), ("hsqf",)])
                    pump_half_b()
                    if gu == 0:
                        pump_half_a()
                pump_to(hhalf * (NSLICE // 2) + 2 * pair + 2 + RING)

            p.op("vector", lambda eng: eng.tensor_copy(out=ssqrow, in_=sqp[0:1, 0:TOK]),
                 reads=[("sqp",), ("hsq",)], writes=[("ssqr",)])
            for t in range(TT):
                p.op("tensor", lambda eng, t=t: eng.transpose(
                    pps[0][:, 0:128], in_=RE[:, 2048 + t * 128: 2048 + (t + 1) * 128],
                    identity=idnf[:, :]),
                    reads=[("ssqr",), "idnf", ("ppsf", 0)], writes=[("pps", 0)])
                p.op("vector", lambda eng, t=t, hhalf=hhalf: eng.tensor_copy(
                    out=ssq_h[:, t * 2 + hhalf: t * 2 + hhalf + 1], in_=pps[0][:, 0:1]),
                    reads=[("pps", 0)], writes=[("ssqc", hhalf, t), ("ppsf", 0)])

            if "h0" in tap_d and hhalf == 0:
                p.op("sync", lambda eng: eng.dma_start(out=tap_d["h0"], in_=h_fm),
                     reads=[("hf", ch) for ch in range(32)], dma=True)

            if hhalf == 1:
                p.barrier()
            sc_c1.close()

            if hhalf == 1:
                ssq2 = ssq_h[:, 0:2 * TT].rearrange("p (t u) -> p t u", u=2)
                p.op("vector", lambda eng: eng.tensor_tensor(
                    out=rmsr[:, 0:TT], in0=ssq2[:, :, 0], in1=ssq2[:, :, 1], op=ALU.add),
                    reads=[], writes=["rmsA"])
                p.op("vector", lambda eng: eng.tensor_scalar(
                    out=rmsr[:, 0:TT], in0=rmsr[:, 0:TT], scalar1=1.0 / H, scalar2=EPS_RMS,
                    op0=ALU.mult, op1=ALU.add),
                    reads=["rmsA"], writes=["rmsA"])
                p.op("scalar", lambda eng: eng.activation(out=rmsr[:, 0:TT], in_=rmsr[:, 0:TT], func=AF.Ln),
                     reads=["rmsA"], writes=["rmsA"])
                p.op("scalar", lambda eng: eng.activation(out=rmsr[:, 0:TT], in_=rmsr[:, 0:TT],
                                                          func=AF.Exp, scale=-0.5),
                     reads=["rmsA"], writes=["rmsr"])

            sc_c2 = ExitStack()
            pjp = [sc_c2.enter_context(nc.psum_tensor(f"pjp{hhalf}_{i}", [128, 1024], F32)) for i in range(2)]
            trcs = [sc_c2.enter_context(nc.psum_tensor(f"trd{hhalf}_{i}", [128, 1024], BF16)) for i in range(2)]
            tro = sc_c2.enter_context(nc.psum_tensor(f"tro{hhalf}", [128, 1024], BF16))
            pst = RG[:, 0:512]
            x2r = RFf[:, :]
            ffcv = RG[:, :].bitcast(BF16)

            for dc in range(DC):
                sid = hhalf * (NSLICE // 2) + 32 + dc
                if sid < RLIM:
                    assert qnext[0] > sid, ("wp quant not emitted", sid)
                else:
                    assert rdone["b2"] >= (sid - RLIM + 1) * 2, ("tail unit", sid)
                pump_half_a()
                if hhalf == 1:
                    rstep()
                rb = sid % RING
                me = dc % 2
                ps = pjp[me]
                for m in range(2):
                    for ci in range(32):
                        p.op("tensor", lambda eng, ps=ps, ci=ci, m=m, rb=rb: eng.matmul(
                            ps[:, m * 512:(m + 1) * 512],
                            lhsT=wsl[rb][:, ci * 128:(ci + 1) * 128],
                            rhs=h_fm3[:, ci, m * 512:(m + 1) * 512],
                            start=(ci == 0), stop=(ci == 31)),
                            reads=[("wb", sid, ci // 8), ("hf", ci)]
                            + ([("pjpf", me), ("ppsf", 0), ("ppsf", 1)]
                               if ci == 0 and m == 0 else []),
                            writes=[("pjp", me, m)])
                pump_half_b1()
                if hhalf == 1:
                    rstep()
                    rstep()
                if hhalf == 0:
                    for m in range(2):
                        p.op("vector", lambda eng, ps=ps, m=m: eng.tensor_copy(
                            out=pst, in_=ps[:, m * 512:(m + 1) * 512]),
                            reads=[("pjp", me, m), ("pstf",)], writes=[("pst",)])
                        p.op("scalar", lambda eng, dc=dc, m=m: eng.dma_start(
                            out=pp_d[dc * 128:(dc + 1) * 128, m * 512:(m + 1) * 512], in_=pst),
                            reads=[("pst",)], writes=[("ppd", dc, m), ("pstf",)], dma=True)
                    p.op("sync", lambda eng: eng.nop(),
                         reads=[("pjp", me, 0), ("pjp", me, 1)], writes=[("pjpf", me)])
                else:
                    p.op("sync", lambda eng, dc=dc: eng.dma_start(
                        out=x2r[:, 0:1024], in_=pp_d[dc * 128:(dc + 1) * 128, 0:1024]),
                        reads=[("x2rf",)], writes=[("x2r",)], dma=True)
                    p.op("vector", lambda eng, ps=ps: eng.tensor_tensor(
                        out=ffcv, in0=ps[:, :], in1=x2r[:, 0:1024], op=ALU.add),
                        reads=[("pjp", me, 0), ("pjp", me, 1), ("x2r",), ("ffcf",)],
                        writes=[("ffc",), ("pjpf", me), ("x2rf",)])
                    rstep()
                    rstep()
                    for t in range(TT):
                        p.op("tensor", lambda eng, t=t: eng.transpose(
                            tro[:, t * 128:(t + 1) * 128],
                            in_=ffcv[:, t * 128:(t + 1) * 128], identity=idn[:, :]),
                            reads=[("ffc",), "idn"] + ([("trof",)] if t == 0 else []),
                            writes=[("tro",)])
                    for t in range(TT):
                        p.op("vector", lambda eng, t=t, dc=dc: eng.tensor_scalar_mul(
                            out=ano2(t)[:, dc * 128:(dc + 1) * 128], in0=tro[:, t * 128:(t + 1) * 128],
                            scalar1=rmsr[:, t:t + 1]),
                            reads=[("tro",), "rmsr"],
                            writes=[("ffo", t)] + ([("trof",), ("ffcf",)] if t == 7 else []))
                if hhalf == 1:
                    rstep()
                    rstep()
                pump_half_b()
                pump_to(hhalf * (NSLICE // 2) + 32 + dc + 1 + RING)

            if hhalf == 1:
                if "ffo" in tap_d:
                    for t in range(TT):
                        p.op("sync", lambda eng, t=t: eng.dma_start(
                            out=tap_d["ffo"][t * 128:(t + 1) * 128, :], in_=ano2(t)),
                            reads=[("ffo", t)], dma=True)
                ln_stats(TT, lambda t: (ano2(t), [("ffo", t)]), EPS_LN)
                rdx = [RD[:, 0:D], RD[:, D:2 * D]]
                for t in range(TT):
                    xb = rdx[t % 2]
                    wk = [("x2b", t % 2)] + ([("qa", 2), ("qa", 3), ("qms", 2),
                                              ("qms", 3)] if t < 2 else [])
                    p.op("sync", lambda eng, t=t, xb=xb: eng.dma_start(
                        out=xb, in_=x2_d[t * 128:(t + 1) * 128, :]),
                        writes=wk, dma=True)
                    p.op("vector", lambda eng, t=t: eng.tensor_scalar(
                        out=ano2(t), in0=ano2(t), scalar1=st[:, 32 + t:33 + t], scalar2=st2[:, 32 + t:33 + t],
                        op0=ALU.subtract, op1=ALU.mult),
                        reads=[("ffo", t), "stC", "stF"], writes=[("ffo", t)])
                    p.op("vector", lambda eng, t=t, xb=xb: eng.tensor_tensor(
                        out=ano2(t), in0=ano2(t), in1=xb, op=ALU.add),
                        reads=[("ffo", t), ("x2b", t % 2)], writes=[("ffo", t)])
                    p.op("scalar", lambda eng, t=t: eng.dma_start(
                        out=out_d[t * 128:(t + 1) * 128, :], in_=ano2(t)),
                        reads=[("ffo", t)], writes=[("outd", t)], dma=True)

            p.barrier()
            sc_c2.close()


        p.op("gpsimd", lambda eng: eng.nop(), reads=[("outd", t) for t in range(TT)])

        with nc.Block() as block:
            p.emit(block, sems_factory)

    return nc


_NC = {}


def _masks(boundary):
    m = np.zeros((5, 128, WIN), dtype=np.float32)
    i = np.arange(128)[:, None]
    j = np.arange(WIN)[None, :]
    for t in range(5):
        valid = (j >= i + 1) & (j <= i + 512)
        if boundary and t < 4:
            valid = valid & ((128 * t + j - 512) >= 0)
        m[t] = np.where(valid, 0.0, NEG)
    return m.reshape(5 * 128, WIN).astype(bfdt)


def kernel(L0_attn_hidden_states, L0_attn_Wq, L0_attn_Wk, L0_attn_Wv, L0_attn_Wo,
           L0_ffn_w_gate_up, L0_ffn_w_proj):
    hs = np.ascontiguousarray(L0_attn_hidden_states, dtype=np.float32)
    if "prog" not in _NC:
        _NC["prog"] = build()
    nc = _NC["prog"]

    shared = {
        "wq": np.ascontiguousarray(L0_attn_Wq, dtype=np.float32),
        "wk": np.ascontiguousarray(L0_attn_Wk, dtype=np.float32),
        "wv": np.ascontiguousarray(L0_attn_Wv, dtype=np.float32),
        "wo": np.ascontiguousarray(L0_attn_Wo, dtype=np.float32),
        "wgu": np.ascontiguousarray(L0_ffn_w_gate_up, dtype=np.float32),
        "wp": np.ascontiguousarray(L0_ffn_w_proj, dtype=np.float32),
        "idn": np.eye(128, dtype=bfdt),
    }
    mb, mi = _masks(True), _masks(False)
    zh = np.zeros((HALO, D), dtype=np.float32)
    in_maps = []
    for c in range(NCORES):
        b, q = divmod(c, 4)
        im = dict(shared)
        im["x"] = np.ascontiguousarray(hs[b, q * TOK:(q + 1) * TOK, :])
        im["xh"] = zh if q == 0 else np.ascontiguousarray(hs[b, q * TOK - HALO:q * TOK, :])
        im["msk"] = mb if q == 0 else mi
        in_maps.append(im)

    res = run_bass_kernel_spmd(nc, in_maps, core_ids=list(range(NCORES)))
    out = np.empty((B, S, D), dtype=np.float32)
    for c in range(NCORES):
        b, q = divmod(c, 4)
        out[b, q * TOK:(q + 1) * TOK, :] = res.results[c]["out"]
    return out
